# version 29
# baseline (speedup 1.0000x reference)
import contextlib
import numpy as np
import concourse.bass as bass
import concourse.mybir as mybir
from concourse.bass_utils import run_bass_kernel_spmd

F32 = mybir.dt.float32
BF16 = mybir.dt.bfloat16
AF = mybir.ActivationFunctionType
ALU = mybir.AluOpType
AX = mybir.AxisListType

B, S, D, H, DH = 4, 4096, 2048, 16, 128
RQ, RK, RV = 6, 2, 2
HID = 1024
NE = 8
NT = S // 128
NO = 16
KD = D // 128
KH = HID // 128
NCHUNK = 4
Q_SCALE = 1.0 / (RQ * float(np.sqrt(DH)))

C_BQ, C_BK, C_BV, C_AQ, C_AK, C_AV = 0, 768, 1024, 1280, 1376, 1408
C_END = 1440

ENGINES = ("sync", "gpsimd", "vector", "scalar", "tensor")


class Pipe:

    DMA_SLOTS = {"sync": 4, "gpsimd": 4, "scalar": 2, "vector": 1, "tensor": 1}

    def __init__(self, nc, sems, cnt, dma_rot, tag, xsem=None, xbase=0):
        self.nc = nc
        self.tag = tag
        self.sem = sems
        self.cnt = cnt
        self.dma_rot = dma_rot
        self.xsem = xsem
        self.xbase = xbase
        self.start_cnt = dict(cnt)
        self.ops = {e: [] for e in ENGINES}
        self.hwm = {e: {k: cnt[k] for k in sems} for e in ENGINES}
        self.slot_cons = {}

    def op(self, eng, fn, deps=(), dma_n=0, self_edge=False):
        if dma_n:
            slot = self.dma_rot[eng] % self.DMA_SLOTS[eng]
            self.dma_rot[eng] += 1
            key = f"{eng}#d{slot}"
            cons = self.slot_cons.pop(key, {})
            deps = list(deps) + list(cons.values())
        else:
            key = eng + "#c"
        waits = []
        if dma_n and not cons and self.cnt[key] > self.start_cnt.get(key, 0):
            waits.append((key, self.cnt[key]))
            self.hwm[eng][key] = max(self.hwm[eng][key], self.cnt[key])
        if self_edge:
            for k in self.sem:
                if (k.split("#")[0] == eng and k != key
                        and self.cnt[k] > self.hwm[eng][k]):
                    waits.append((k, self.cnt[k]))
                    self.hwm[eng][k] = self.cnt[k]
            if self.cnt[key] > self.hwm[eng][key]:
                waits.append((key, self.cnt[key]))
                self.hwm[eng][key] = self.cnt[key]
        for d in deps:
            if d is None:
                continue
            pk, pc = d
            if pk == key or (pk.split("#")[0] == eng and pk.endswith("#c")
                             and not dma_n):
                continue
            if pc > self.hwm[eng][pk]:
                waits.append((pk, pc))
                self.hwm[eng][pk] = pc
        amt = 16 * dma_n if dma_n else 1
        self.ops[eng].append((fn, waits, amt, key))
        self.cnt[key] += amt
        tok = (key, self.cnt[key])
        for d in deps:
            if d is not None and "#d" in d[0]:
                self.slot_cons.setdefault(d[0], {})[eng] = tok
        return tok

    def wait_only(self, eng, deps):
        waits = []
        for d in deps:
            if d is None:
                continue
            pk, pc = d
            if pc > self.hwm[eng][pk]:
                waits.append((pk, pc))
                self.hwm[eng][pk] = pc
        self.ops[eng].append((None, waits, 0, eng + "#c"))

    def check(self):
        pos = {e: 0 for e in ENGINES}
        val = dict(self.start_cnt)
        progress = True
        while progress:
            progress = False
            for e in ENGINES:
                while pos[e] < len(self.ops[e]):
                    fn, waits, amt, key = self.ops[e][pos[e]]
                    if any(val[pk] < pc for pk, pc in waits):
                        break
                    val[key] += amt
                    pos[e] += 1
                    progress = True
        stuck = {e: (pos[e], len(self.ops[e])) for e in ENGINES
                 if pos[e] < len(self.ops[e])}
        if stuck:
            msg = []
            for e, (p, n) in stuck.items():
                msg.append(f"{e}@{p}/{n} waits={self.ops[e][p][1]}")
            raise RuntimeError(f"pipe {self.tag} deadlock: {msg}")

    def emit(self):
        self.check()
        if self.xsem is not None:
            finals = [(k, self.cnt[k]) for k in self.cnt
                      if self.cnt[k] > self.start_cnt[k] or k == "gpsimd#c"]
            self.ops["gpsimd"].append((None, finals, 0, "gpsimd#c"))
            self.ops["gpsimd"].append(("XINC", [], 0, "gpsimd#c"))
        with self.nc.Block(self.tag) as blk:
            for e in ENGINES:
                if not self.ops[e]:
                    continue

                def body(eng, _e=e):
                    if self.xsem is not None and self.xbase > 0:
                        eng.wait_ge(self.xsem, self.xbase)
                    for fn, waits, amt, key in self.ops[_e]:
                        if fn == "XINC":
                            eng.nop().then_inc(self.xsem, 1)
                            continue
                        for pk, pc in waits:
                            eng.wait_ge(self.sem[pk], pc)
                        if fn is None:
                            continue
                        instr = fn(eng)
                        if amt:
                            if isinstance(instr, (list, tuple)):
                                assert 16 * len(instr) == amt
                                for it in instr:
                                    it.then_inc(self.sem[key], 16)
                            elif instr is None:
                                eng.nop().then_inc(self.sem[key], amt)
                            else:
                                instr.then_inc(self.sem[key], amt)
                getattr(blk, e)(body)


def make_identity(g, identity):
    sq = identity.shape[1]
    g.memset(identity[:], 0.0)
    return g.affine_select(
        out=identity[:], in_=identity[:],
        compare_op=ALU.not_equal, fill=1.0, base=0,
        pattern=[[-1, sq]], channel_multiplier=1)


def prev_tok(lst, g, n=1):
    return lst[g - n] if g - n >= 0 else None


def build_program(debug=False):
    nc = bass.Bass("TRN2", target_bir_lowering=False)

    xT = nc.dram_tensor("xT", [D * S], F32, kind="ExternalInput")
    wproj = nc.dram_tensor("wproj", [D * C_END], F32, kind="ExternalInput")
    wo_in = nc.dram_tensor("wo_in", [D * D], F32, kind="ExternalInput")
    freq = nc.dram_tensor("freq", [2049 * 128], F32, kind="ExternalInput")
    gateT = nc.dram_tensor("gateT", [D * NE], F32, kind="ExternalInput")
    ebias = nc.dram_tensor("ebias", [NE], F32, kind="ExternalInput")
    sw1T = nc.dram_tensor("sw1T", [D * HID], F32, kind="ExternalInput")
    sw3T = nc.dram_tensor("sw3T", [D * HID], F32, kind="ExternalInput")
    sw2T = nc.dram_tensor("sw2T", [HID * D], F32, kind="ExternalInput")
    w1T = nc.dram_tensor("w1T", [NE * D * HID], F32, kind="ExternalInput")
    w3T = nc.dram_tensor("w3T", [NE * D * HID], F32, kind="ExternalInput")
    w2T = nc.dram_tensor("w2T", [NE * HID * D], F32, kind="ExternalInput")
    rsel8_in = nc.dram_tensor("rsel8_in", [8 * 8 * 128], F32, kind="ExternalInput")

    y2_out = nc.dram_tensor("y2", [D, 2048], F32, kind="ExternalOutput")
    if debug:
        dbg = {n: nc.dram_tensor(f"dbg_{n}", sz, BF16, kind="ExternalOutput")
               for n, sz in (("q", [H * 128 * 2048]), ("k", [H * 128 * S]),
                             ("v", [H * S * 128]), ("y1", [D * 2048]),
                             ("h", [9 * HID * 2048]), ("att", [128 * H * 2048]),
                             ("mf", [8 * 2048]), ("ffn", [128 * KD * 2048]))}

    q_scr = nc.dram_tensor("q_scr", [H * 128 * 2048], BF16, kind="Internal")
    k_scr = nc.dram_tensor("k_scr", [H * 128 * S], BF16, kind="Internal")
    v_scr = nc.dram_tensor("v_scr", [H * S * 128], BF16, kind="Internal")
    y1_scr = nc.dram_tensor("y1_scr", [D * 2048], BF16, kind="Internal")
    h_scr = nc.dram_tensor("h_scr", [9 * HID * 2048], BF16, kind="Internal")

    def dap(t, off, dims):
        return bass.AP(t, off, dims)

    with contextlib.ExitStack() as st0:
        Sb = lambda *a: st0.enter_context(nc.sbuf_tensor(*a))
        xsem = st0.enter_context(nc.semaphore("xphase"))
        xctr = [0]
        gsems = {}
        gcnt = {}
        grot = {e: 0 for e in ENGINES}
        for e in ENGINES:
            gsems[e + "#c"] = st0.enter_context(nc.semaphore(f"{e}_c"))
            gcnt[e + "#c"] = 0
            for sl in range(Pipe.DMA_SLOTS[e]):
                gsems[f"{e}#d{sl}"] = st0.enter_context(
                    nc.semaphore(f"{e}_d{sl}"))
                gcnt[f"{e}#d{sl}"] = 0

        def mkpipe(tag):
            p = Pipe(nc, gsems, gcnt, grot, tag, xsem=xsem, xbase=xctr[0])
            xctr[0] += 1
            return p

        onesK = Sb("onesK", [128, 16], BF16)
        onesrow = Sb("onesrow", [32, 128], BF16)
        identm = Sb("identm", [128, 128], BF16)
        rsel8 = Sb("rsel8", [32, 8 * 128], BF16)
        bias_bc = Sb("bias_bc", [128, NE], F32)
        gbf = Sb("gbf", [128, KD * NE], BF16)
        m_feat = Sb("m_feat", [32, 2048], BF16)
        negM16 = Sb("negM16", [32, 4], F32)
        cval = Sb("cval", [32, 4], F32)

        with contextlib.ExitStack() as stA:
            SA = lambda *a: stA.enter_context(nc.sbuf_tensor(*a))
            wbf = SA("wbf", [128, KD * C_END], BF16)

            with contextlib.ExitStack() as stp:
                SP = lambda *a: stp.enter_context(nc.sbuf_tensor(*a))
                wf32 = SP("wf32", [128, KD * C_END], F32)
                frB = SP("frB", [128, 64], F32)
                fiB = SP("fiB", [128, 64], F32)
                frW = SP("frW", [128, 512], F32)
                fiW = SP("fiW", [128, 512], F32)
                tA = SP("tA", [128, 512], F32)
                tB = SP("tB", [128, 512], F32)
                tC = SP("tC", [128, 512], F32)

                pp = mkpipe("p0")
                d_wk = [None] * KD
                for k in range(KD):
                    d_wk[k] = pp.op("sync", lambda e, k=k: e.dma_start(
                        out=wf32[:, k * C_END:(k + 1) * C_END],
                        in_=dap(wproj, k * 128 * C_END,
                                [[C_END, 128], [1, C_END]])), dma_n=1)
                d_fr = pp.op("gpsimd", lambda e: e.dma_start(
                    out=frB[:],
                    in_=dap(freq, 2048 * 128, [[0, 128], [2, 64]]),
                    allow_slow_non_contiguous=True), dma_n=1)
                d_fi = pp.op("gpsimd", lambda e: e.dma_start(
                    out=fiB[:],
                    in_=dap(freq, 2048 * 128 + 1, [[0, 128], [2, 64]]),
                    allow_slow_non_contiguous=True), dma_n=1)
                d_b = pp.op("gpsimd", lambda e: e.dma_start(
                    out=bias_bc[:], in_=dap(ebias, 0, [[0, 128], [1, NE]])), dma_n=1)
                d_g = pp.op("gpsimd", lambda e: e.dma_start(
                    out=gbf[:].rearrange("p (k c) -> p k c", k=KD),
                    in_=dap(gateT, 0, [[NE, 128], [128 * NE, KD], [1, NE]])), dma_n=1)
                t_ms = pp.op("gpsimd", lambda e: e.memset(rsel8[:], 0.0))
                d_r = pp.op("gpsimd", lambda e: e.dma_start(
                    out=rsel8[0:8, :],
                    in_=dap(rsel8_in, 0, [[1024, 8], [1, 1024]])),
                    deps=[t_ms], dma_n=1)
                pp.op("gpsimd", lambda e: e.memset(cval[:, 0:1], 1e-30))
                pp.op("gpsimd", lambda e: e.memset(cval[:, 1:2], 1e-5))
                pp.op("gpsimd", lambda e: e.memset(cval[:, 2:3], -1.0))
                pp.op("gpsimd", lambda e: e.memset(cval[:, 3:4], -0.5))
                pp.op("gpsimd", lambda e: e.memset(onesK[:], 1.0))
                pp.op("gpsimd", lambda e: e.memset(onesrow[:], 1.0))
                t_mi = pp.op("gpsimd", lambda e: e.memset(identm[:], 0.0))
                pp.op("gpsimd", lambda e: e.affine_select(
                    out=identm[:], in_=identm[:],
                    compare_op=ALU.not_equal, fill=1.0, base=0,
                    pattern=[[-1, 128]], channel_multiplier=1), self_edge=True)

                def rep(v):
                    last = None
                    for rr in range(8):
                        v.tensor_copy(frW[:, rr * 64:(rr + 1) * 64], frB[:])
                    for rr in range(8):
                        last = v.tensor_copy(fiW[:, rr * 64:(rr + 1) * 64],
                                             fiB[:])
                    return last

                t_rep = pp.op("vector", rep, deps=[d_fr, d_fi])

                def fold_k(v, k):
                    base = k * C_END
                    ev = wf32[:, base: base + 1024: 2]
                    od = wf32[:, base + 1: base + 1024: 2]
                    v.tensor_mul(tA[:], ev, frW[:])
                    v.tensor_mul(tB[:], od, fiW[:])
                    v.tensor_sub(tA[:], tA[:], tB[:])
                    v.tensor_mul(tB[:], ev, fiW[:])
                    v.tensor_mul(tC[:], od, frW[:])
                    v.tensor_add(tB[:], tB[:], tC[:])
                    v.tensor_copy(ev, tA[:])
                    return v.tensor_copy(od, tB[:])

                def cast_k(v, k):
                    b = k * C_END
                    v.tensor_copy(wbf[:, b: b + C_BV], wf32[:, b: b + C_BV])
                    v.tensor_scalar_mul(wbf[:, b + C_BV: b + C_AQ],
                                        wf32[:, b + C_BV: b + C_AQ], 1.0 / RV)
                    v.tensor_scalar_mul(wbf[:, b + C_AQ: b + C_AK],
                                        wf32[:, b + C_AQ: b + C_AK], Q_SCALE)
                    v.tensor_scalar_mul(wbf[:, b + C_AK: b + C_AV],
                                        wf32[:, b + C_AK: b + C_AV], 1.0 / RK)
                    return v.tensor_copy(wbf[:, b + C_AV: b + C_END],
                                         wf32[:, b + C_AV: b + C_END])

                t_fold = None
                for k in range(KD):
                    t_fk = pp.op("vector", lambda v, k=k: fold_k(v, k),
                                 deps=[d_wk[k], t_rep],
                                 self_edge=(k == 0))
                    t_fold = pp.op("vector", lambda v, k=k: cast_k(v, k),
                                   deps=[t_fk])
                pp.wait_only("gpsimd", [d_fr, d_fi, d_b, d_g, d_r])
                pp.wait_only("sync", [d_wk[KD - 1]])
                pp.emit()

            with contextlib.ExitStack() as stp:
                SP = lambda *a: stp.enter_context(nc.sbuf_tensor(*a))
                PPs = lambda *a: stp.enter_context(nc.psum_tensor(*a))
                xbf = [SP(f"xbf_{p}", [128, D], BF16) for p in range(3)]
                bfq = [SP(f"bfq_{p}", [128, 768], BF16) for p in range(3)]
                faq = [SP(f"faq_{p}", [128, 96], F32) for p in range(3)]
                bkv = [SP(f"bkv_{p}", [128, 512], BF16) for p in range(3)]
                fakv = [SP(f"fakv_{p}", [128, 64], F32) for p in range(3)]
                qtok = [SP(f"qtok_{p}", [128, 2048], BF16) for p in range(3)]
                ktok = [SP(f"ktok_{p}", [128, 2048], BF16) for p in range(3)]
                vtok = [SP(f"vtok_{p}", [128, 2048], BF16) for p in range(3)]
                prod = [SP(f"prod_{r}", [128, 2048], BF16) for r in range(6)]
                prodK = [SP(f"prodK_{r}", [128, 2048], BF16) for r in range(2)]
                prodV = [SP(f"prodV_{r}", [128, 2048], BF16) for r in range(2)]
                sqn = [SP(f"sqn_{p}", [128, 2048], BF16) for p in range(3)]
                nrmt = [SP(f"nrmt_{p}", [128, 16], BF16) for p in range(3)]
                qnrm = SP("qnrm", [32, 2048], BF16)
                knrm = SP("knrm", [32, 4096], BF16)
                nrmall = SP("nrmall", [128, 48 * 16], BF16)
                nmax = SP("nmax", [32, 4], F32)
                psNT = [PPs(f"psNT_{p}", [32, 128], BF16) for p in range(2)]
                psA = [PPs(f"psA_{p}", [128, 512], F32) for p in range(2)]
                psB = [PPs(f"psB_{p}", [128, 416], F32) for p in range(2)]
                psC = [PPs(f"psC_{p}", [128, 512], F32) for p in range(2)]

                pp = mkpipe("p1")
                t_x = [None] * NT
                t_cast = [None] * NT
                t_proj = [None] * NT
                t_pev = [None] * NT
                t_form = [None] * NT
                t_wq = [None] * NT
                t_wk = [None] * NT
                t_wv = [None] * NT
                t_nsq = []
                t_nred = []
                t_ntp = []
                t_nev = []
                t_ntile = [None] * NT

                t_prodv = [None] * NT
                t_prods = [None] * NT

                def emit_front(i):
                    p = i % 3
                    q = i % 2
                    own = i < NO
                    t_x[i] = pp.op("gpsimd", lambda e, i=i, p=p: e.dma_start(
                        out=xbf[p][:].rearrange("p (k c) -> p k c", k=KD),
                        in_=dap(xT, i * 128, [[S, 128], [128 * S, KD], [1, 128]])),
                        deps=[t_proj[i - 3] if i >= 3 else None], dma_n=1)

                    def proj(t, p=p, q=q, own=own):
                        last = None
                        for k in range(KD):
                            lhsT = xbf[p][:, k * 128:(k + 1) * 128]
                            wb = k * C_END
                            st_, sp_ = (k == 0), (k == KD - 1)
                            if own:
                                t.matmul(psA[q][:], lhsT,
                                         wbf[:, wb: wb + 512], start=st_, stop=sp_)
                                t.matmul(psB[q][:, 0:256], lhsT,
                                         wbf[:, wb + 512: wb + 768],
                                         start=st_, stop=sp_)
                                t.matmul(psB[q][:, 256:352], lhsT,
                                         wbf[:, wb + C_AQ: wb + C_AK],
                                         start=False, stop=sp_)
                            t.matmul(psC[q][:], lhsT,
                                     wbf[:, wb + C_BK: wb + C_AQ],
                                     start=st_, stop=sp_)
                            last = t.matmul(psB[q][:, 352:416], lhsT,
                                            wbf[:, wb + C_AK: wb + C_END],
                                            start=(st_ and not own), stop=sp_)
                        return last

                    t_proj[i] = pp.op("tensor", proj,
                                      deps=[t_x[i],
                                            t_pev[i - 2] if i >= 2 else None])

                    def pev(sc, p=p, q=q, own=own):
                        if own:
                            sc.copy(bfq[p][:, 0:512], psA[q][:])
                            sc.copy(bfq[p][:, 512:768], psB[q][:, 0:256])
                            sc.copy(faq[p][:], psB[q][:, 256:352])
                        sc.copy(bkv[p][:], psC[q][:])
                        return sc.copy(fakv[p][:], psB[q][:, 352:416])

                    t_pev[i] = pp.op("scalar", pev,
                                     deps=[t_proj[i],
                                           t_prodv[i - 3] if i >= 3 else None,
                                           t_prods[i - 3] if i >= 3 else None])

                def emit_compute(i):
                    p = i % 3
                    own = i < NO

                    def q_prods(v, p=p):
                        last = None
                        for r in range(RQ):
                            for hh in range(H):
                                last = v.tensor_scalar(
                                    prod[r][:, hh * 128:(hh + 1) * 128],
                                    bfq[p][:, r * 128:(r + 1) * 128],
                                    faq[p][:, r * H + hh: r * H + hh + 1],
                                    None, op0=ALU.mult)
                        return last

                    def q_adds(v, p=p):
                        v.tensor_add(prod[0][:], prod[0][:], prod[1][:])
                        v.tensor_add(prod[2][:], prod[2][:], prod[3][:])
                        v.tensor_add(prod[4][:], prod[4][:], prod[5][:])
                        v.tensor_add(prod[0][:], prod[0][:], prod[2][:])
                        return v.tensor_add(qtok[p][:], prod[0][:], prod[4][:])

                    def k_prods_dve(v, p=p):
                        last = None
                        for r in range(RK):
                            for hh in range(H):
                                last = v.tensor_scalar(
                                    prodK[r][:, hh * 128:(hh + 1) * 128],
                                    bkv[p][:, r * 128:(r + 1) * 128],
                                    fakv[p][:, r * H + hh: r * H + hh + 1],
                                    None, op0=ALU.mult)
                        return last

                    def kv_prods_sc(sc, p=p, own=own):
                        last = None
                        if own:
                            for r in range(RK):
                                for hh in range(H):
                                    last = sc.activation(
                                        prodK[r][:, hh * 128:(hh + 1) * 128],
                                        bkv[p][:, r * 128:(r + 1) * 128],
                                        AF.Copy,
                                        scale=fakv[p][:, r * H + hh:
                                                      r * H + hh + 1])
                        for r in range(RV):
                            for hh in range(H):
                                last = sc.activation(
                                    prodV[r][:, hh * 128:(hh + 1) * 128],
                                    bkv[p][:, 256 + r * 128: 256 + (r + 1) * 128],
                                    AF.Copy,
                                    scale=fakv[p][:, 32 + r * H + hh:
                                                  33 + r * H + hh])
                        return last

                    def kv_adds(v, p=p):
                        v.tensor_add(ktok[p][:], prodK[0][:], prodK[1][:])
                        return v.tensor_add(vtok[p][:], prodV[0][:], prodV[1][:])

                    war = [t_wv[i - 3] if i >= 3 else None,
                           t_wk[i - 3] if i >= 3 else None,
                           t_wq[i - 3] if (i >= 3 and i - 3 < NO) else None,
                           t_ntile[i - 3] if i >= 3 else None]
                    if own:
                        tq1 = pp.op("vector", q_prods, deps=[t_pev[i]] + war)
                        tq2 = pp.op("vector", q_adds, deps=[tq1])
                        tsc = pp.op("scalar", kv_prods_sc,
                                    deps=[t_pev[i], t_form[i - 1] if i >= 1
                                          else None] + war)
                        t_form[i] = pp.op("vector", kv_adds, deps=[tq2, tsc])
                        t_prodv[i] = tq1
                    else:
                        tk1 = pp.op("vector", k_prods_dve,
                                    deps=[t_pev[i]] + war)
                        tsc = pp.op("scalar", kv_prods_sc,
                                    deps=[t_pev[i], t_form[i - 1] if i >= 1
                                          else None] + war)
                        t_form[i] = pp.op("vector", kv_adds, deps=[tk1, tsc])
                        t_prodv[i] = tk1
                    t_prods[i] = tsc

                    if own:
                        t_wq[i] = pp.op("gpsimd", lambda e, p=p, i=i: e.dma_start(
                            out=dap(q_scr, i * 128 * 128,
                                    [[128, 128], [2048 * 128, H], [1, 128]]),
                            in_=qtok[p][:].rearrange("p (h c) -> p h c", h=H)),
                            deps=[t_form[i]], dma_n=1)
                    t_wk[i] = pp.op("gpsimd", lambda e, p=p, i=i: e.dma_start(
                        out=dap(k_scr, i * 128 * 128,
                                [[128, 128], [S * 128, H], [1, 128]]),
                        in_=ktok[p][:].rearrange("p (h c) -> p h c", h=H)),
                        deps=[t_form[i]], dma_n=1)
                    t_wv[i] = pp.op("gpsimd", lambda e, p=p, i=i: e.dma_start(
                        out=dap(v_scr, i * 128 * 128,
                                [[128, 128], [S * 128, H], [1, 128]]),
                        in_=vtok[p][:].rearrange("p (h c) -> p h c", h=H)),
                        deps=[t_form[i]], dma_n=1)

                norm_dsts = []

                def emit_norm(ti, src_t, dst, col):
                    gn = len(t_nsq)
                    pn = gn % 3
                    norm_dsts.append((dst, col))
                    t_nsq.append(pp.op(
                        "scalar", lambda e, pn=pn, src_t=src_t: e.activation(
                            sqn[pn][:], src_t[:], AF.Square),
                        deps=[t_form[ti],
                              t_nred[gn - 3] if gn >= 3 else None]))

                    def nred(e, pn=pn, gn=gn):
                        with nc.allow_low_precision(reason="softmax bound only"):
                            return e.reduce_sum(
                                nrmall[:, gn * 16:(gn + 1) * 16].rearrange(
                                    "p (h o) -> p h o", h=H),
                                sqn[pn][:].rearrange("p (h c) -> p h c", h=H),
                                axis=AX.X)

                    t_nred.append(pp.op(
                        "vector", nred, deps=[t_nsq[gn]]))

                def emit_norms_for(j):
                    pj = j % 3
                    emit_norm(j, ktok[pj], knrm, j * 128)
                    if j < NO:
                        emit_norm(j, qtok[pj], qnrm, j * 128)
                    t_ntile[j] = t_nsq[-1]

                for i in range(NT + 2):
                    if i < NT:
                        emit_front(i)
                    if 1 <= i <= NT:
                        emit_compute(i - 1)
                    if 2 <= i:
                        emit_norms_for(i - 2)

                for gn, (dst, col) in enumerate(norm_dsts):
                    bn = gn % 2
                    t_ntp.append(pp.op(
                        "tensor", lambda e, gn=gn, bn=bn: e.transpose(
                            psNT[bn][0:16, :],
                            nrmall[:, gn * 16:(gn + 1) * 16], identm[:]),
                        deps=[t_nred[gn],
                              t_nev[gn - 2] if gn >= 2 else None]))
                    t_nev.append(pp.op(
                        "scalar", lambda e, dst=dst, col=col, bn=bn: e.copy(
                            dst[0:16, col:col + 128], psNT[bn][0:16, :]),
                        deps=[t_ntp[gn]]))
                b1 = pp.op("vector", lambda e: e.reduce_max(
                    nmax[0:16, 0:1], qnrm[0:16, :], axis=AX.X),
                    deps=[t_nev[-1]], self_edge=True)
                b2 = pp.op("vector", lambda e: e.reduce_max(
                    nmax[0:16, 1:2], knrm[0:16, :], axis=AX.X),
                    deps=[b1], self_edge=True)
                b3 = pp.op("vector", lambda e: e.tensor_mul(
                    nmax[0:16, 2:3], nmax[0:16, 0:1], nmax[0:16, 1:2]),
                    deps=[b2], self_edge=True)
                b4 = pp.op("scalar", lambda e: e.activation(
                    nmax[0:16, 3:4], nmax[0:16, 2:3], AF.Sqrt), deps=[b3])
                pp.op("vector", lambda e: e.tensor_scalar(
                    negM16[0:16, 0:1], nmax[0:16, 3:4], -1.0, -2.0,
                    op0=ALU.mult, op1=ALU.add), deps=[b4], self_edge=True)
                pp.wait_only("gpsimd", [t_wv[NT - 1], t_wk[NT - 1], t_wq[NO - 1]])
                pp.emit()

        with contextlib.ExitStack() as stB:
            SB_ = lambda *a: stB.enter_context(nc.sbuf_tensor(*a))
            wobf = SB_("wobf", [128, KD * 2048], BF16)
            att = SB_("att", [128, H * 2048], BF16)

            with contextlib.ExitStack() as stp:
                SP = lambda *a: stp.enter_context(nc.sbuf_tensor(*a))
                PPs = lambda *a: stp.enter_context(nc.psum_tensor(*a))
                qsb = [SP(f"qsb_{p}", [128, 2048], BF16) for p in range(2)]
                ksb = [SP(f"ksb_{p}", [128, 4096], BF16) for p in range(2)]
                vsb = [SP(f"vsb_{p}", [128, 4096], BF16) for p in range(2)]
                ebuf = [SP(f"ebuf_{j}", [128, 1024], BF16) for j in range(2)]
                diag16 = SP("diag16", [32, 16], BF16)
                negMbA = SP("negMbA", [128, 16], F32)
                rrow = SP("rrow", [32, 512], BF16)
                rbc = SP("rbc", [128, 512], F32)
                psS = [PPs(f"psS_{j}", [128, 1024], F32) for j in range(2)]
                psAV = [PPs(f"psAV_{p}", [128, 512], F32) for p in range(2)]
                psD = [PPs(f"psD_{p}", [128, 512], F32) for p in range(2)]

                pp = mkpipe("p2")
                NCI = H * NCHUNK
                NG = NCI * NT
                NP = NG // 2
                t_ld = [None] * H
                t_sl = [None] * NG
                t_ep = [None] * NP
                t_avl = [None] * NG
                t_ln1 = [None] * NCI
                t_ln2 = [None] * NCI
                t_bcm = [None] * NCI
                t_rbce = [None] * NCI
                t_ev = [None] * NCI

                t_wo = pp.op("gpsimd", lambda e: e.dma_start(
                    out=wobf[:].rearrange("p (k c) -> p k c", k=KD),
                    in_=dap(wo_in, 0, [[D, 128], [128 * D, KD], [1, D]])), dma_n=1)

                def emit_load(hh):
                    p = hh % 2
                    deps = []
                    if hh >= 2:
                        deps = [t_sl[(hh - 1) * NCHUNK * NT - 1],
                                t_avl[(hh - 1) * NCHUNK * NT - 1]]

                    def f(e, hh=hh, p=p):
                        return [
                            e.dma_start(out=qsb[p][:],
                                        in_=dap(q_scr, hh * 2048 * 128,
                                                [[128, 2048], [1, 128]]),
                                        transpose=True),
                            e.dma_start(out=ksb[p][:, 0:2048],
                                        in_=dap(k_scr, hh * S * 128,
                                                [[128, 2048], [1, 128]]),
                                        transpose=True),
                            e.dma_start(out=ksb[p][:, 2048:4096],
                                        in_=dap(k_scr, hh * S * 128 + 2048 * 128,
                                                [[128, 2048], [1, 128]]),
                                        transpose=True),
                            e.dma_start(
                                out=vsb[p][:].rearrange("p (j c) -> p j c", j=NT),
                                in_=dap(v_scr, hh * S * 128,
                                        [[128, 128], [128 * 128, NT], [1, 128]]))]

                    t_ld[hh] = pp.op("sync", f, deps=deps, dma_n=4)

                def emit_bound():
                    d1 = pp.op("vector", lambda e: e.tensor_scalar(
                        diag16[0:16, :], identm[0:16, 0:16], negM16[0:16, 0:1],
                        None, op0=ALU.mult))
                    d2 = pp.op("tensor", lambda e: e.matmul(
                        psD[0][0:128, 0:16], onesrow[0:16, :], diag16[0:16, :],
                        start=True, stop=True), deps=[d1])
                    return pp.op("scalar", lambda e: e.copy(
                        negMbA[:], psD[0][0:128, 0:16]), deps=[d2])

                t_bnd = emit_bound()

                def emit_S(hh, cc, j):
                    p = hh % 2
                    g = (hh * NCHUNK + cc) * NT + j
                    pr = g // 2
                    t_sl[g] = pp.op(
                        "tensor", lambda e, p=p, cc=cc, j=j, g=g, pr=pr: e.matmul(
                            psS[pr % 2][:, (g % 2) * 512:(g % 2 + 1) * 512],
                            ksb[p][:, j * 128:(j + 1) * 128],
                            qsb[p][:, cc * 512:(cc + 1) * 512],
                            start=True, stop=True),
                        deps=[t_ld[hh], t_ep[pr - 2] if pr >= 2 else None])
                    if g % 2 == 1:
                        t_ep[pr] = pp.op(
                            "scalar", lambda e, hh=hh, pr=pr: e.activation(
                                ebuf[pr % 2][:], psS[pr % 2][:], AF.Exp,
                                bias=negMbA[:, hh:hh + 1], scale=1.0),
                            deps=[t_sl[g], t_bnd,
                                  t_avl[g - 3] if g >= 3 else None])

                def emit_AV(hh, cc, j):
                    p = hh % 2
                    ci = hh * NCHUNK + cc
                    g = ci * NT + j
                    pr = g // 2
                    deps_av = [t_ep[pr]]
                    if j == 0 and ci >= 2:
                        deps_av += [t_ev[ci - 2], t_rbce[ci - 2]]
                    eb = lambda: ebuf[pr % 2][:, (g % 2) * 512:(g % 2 + 1) * 512]
                    t_avl[g] = pp.op(
                        "tensor", lambda e, p=p, cc=cc, j=j, pr=pr, g=g: (
                            e.matmul(psAV[cc % 2][:],
                                     vsb[p][:, j * 128:(j + 1) * 128],
                                     ebuf[pr % 2][:, (g % 2) * 512:(g % 2 + 1) * 512],
                                     start=(j == 0), stop=(j == NT - 1)),
                            e.matmul(psD[cc % 2][0:1, :], onesK[:, 0:1],
                                     ebuf[pr % 2][:, (g % 2) * 512:(g % 2 + 1) * 512],
                                     start=(j == 0), stop=(j == NT - 1)))[-1],
                        deps=deps_av)

                def emit_tail(ci):
                    hh, cc = divmod(ci, NCHUNK)
                    t_ln1[ci] = pp.op(
                        "scalar", lambda e, cc=cc: e.activation(
                            rrow[0:1, :], psD[cc % 2][0:1, :], AF.Ln,
                            bias=cval[0:1, 0:1]),
                        deps=[t_avl[ci * NT + NT - 1]])
                    t_ln2[ci] = pp.op(
                        "scalar", lambda e: e.activation(
                            rrow[0:1, :], rrow[0:1, :], AF.Exp,
                            scale=cval[0:1, 2:3]),
                        deps=[t_ln1[ci]], self_edge=True)
                    t_bcm[ci] = pp.op(
                        "tensor", lambda e, cc=cc: e.matmul(
                            psD[cc % 2][:], onesrow[0:1, :], rrow[0:1, :],
                            start=True, stop=True), deps=[t_ln2[ci]])
                    t_rbce[ci] = pp.op(
                        "scalar", lambda e, cc=cc: e.copy(rbc[:], psD[cc % 2][:]),
                        deps=[t_bcm[ci], t_ev[ci - 1] if ci >= 1 else None])
                    t_ev[ci] = pp.op(
                        "vector", lambda e, hh=hh, cc=cc: e.tensor_mul(
                            att[:, hh * 2048 + cc * 512: hh * 2048 + (cc + 1) * 512],
                            psAV[cc % 2][:], rbc[:]),
                        deps=[t_rbce[ci], t_avl[ci * NT + NT - 1]])

                def emit_chunk(hh, cc):
                    ci = hh * NCHUNK + cc
                    emit_S(hh, cc, 0)
                    emit_S(hh, cc, 1)
                    for jp in range(1, NT // 2):
                        emit_S(hh, cc, 2 * jp)
                        emit_S(hh, cc, 2 * jp + 1)
                        emit_AV(hh, cc, 2 * jp - 2)
                        emit_AV(hh, cc, 2 * jp - 1)
                        if jp == 2 and ci >= 1 and t_ln1[ci - 1] is None:
                            emit_tail(ci - 1)
                    emit_AV(hh, cc, NT - 2)
                    emit_AV(hh, cc, NT - 1)

                emit_load(0)
                for hh in range(H):
                    if hh + 1 < H:
                        emit_load(hh + 1)
                    for cc in range(NCHUNK):
                        emit_chunk(hh, cc)
                emit_tail(H * NCHUNK - 1)
                pp.wait_only("sync", [t_ld[H - 1]])
                pp.wait_only("gpsimd", [t_wo])
                pp.emit()

            with contextlib.ExitStack() as stp:
                SP = lambda *a: stp.enter_context(nc.sbuf_tensor(*a))
                PPs = lambda *a: stp.enter_context(nc.psum_tensor(*a))
                xres = [SP(f"xres_{p}", [128, 512], F32) for p in range(4)]
                ufull = SP("ufull", [128, KD * 512], F32)
                ubx = [SP(f"ubx_{p}", [128, 512], BF16) for p in range(2)]
                usx = [SP(f"usx_{p}", [128, 512], BF16) for p in range(2)]
                srow = SP("srow", [32, 2048], F32)
                mrow = SP("mrow", [32, 512], BF16)
                rrw = SP("rrw", [32, 512], BF16)
                mbc = SP("mbc", [128, 512], BF16)
                rbc3 = SP("rbc3", [128, 512], BF16)
                y1c = SP("y1c", [128, KD * 512], BF16)
                scrs = SP("scrs", [128, 32], F32)
                ttb = [SP(f"ttb_{p}", [128, 8], F32) for p in range(2)]
                cge = [SP("cge_0", [128, 64], F32)] * 2
                cgt = [SP("cgt_0", [128, 64], F32)] * 2
                rka = [SP(f"rka_{p}", [128, 16], F32) for p in range(2)]
                mskb = [SP(f"mskb_{p}", [128, 8], BF16) for p in range(2)]
                psW = [PPs(f"psW_{p}", [128, 512], F32) for p in range(2)]
                psMV = PPs("psMV", [32, 1024], F32)
                psBC = PPs("psBC", [128, 512], F32)
                psG = [PPs(f"psG_{p}", [128, 8], F32) for p in range(2)]
                psMT = PPs("psMT", [32, 512], BF16)

                pp = mkpipe("p3")
                if debug:
                    pp.op("gpsimd", lambda e: e.dma_start(
                        out=dap(dbg["att"], 0, [[H * 2048, 128], [1, H * 2048]]),
                        in_=att[:]), dma_n=1)
                t_x3 = [None] * (NCHUNK * KD)
                t_wom = [None] * (NCHUNK * KD)
                t_u = [None] * (NCHUNK * KD)
                t_ub = [None] * (NCHUNK * KD)
                t_stk = [None] * (NCHUNK * KD)
                stk_fns = [None] * (NCHUNK * KD)
                t_st = [None] * NCHUNK
                t_lnv = [None] * NCHUNK
                t_rstd = [None] * NCHUNK
                t_bme = [None] * NCHUNK
                t_brme = [None] * NCHUNK
                t_y1 = [None] * (NCHUNK * KD)
                t_wr1 = [None] * NCHUNK
                t_g = [None] * 16
                t_sg = [None] * 16
                t_rk = [None] * 16
                t_ms1 = [None] * 16
                t_ms2 = [None] * 16
                t_ms3 = [None] * 16
                t_mt = [None] * 16
                t_mte = [None] * 16

                for cc in range(NCHUNK):
                    for dt in range(KD):
                        idx = cc * KD + dt
                        t_x3[idx] = pp.op(
                            "sync", lambda e, cc=cc, dt=dt, idx=idx: e.dma_start(
                                out=xres[idx % 4][:],
                                in_=dap(xT, dt * 128 * S + cc * 512,
                                        [[S, 128], [1, 512]])),
                            deps=[prev_tok(t_u, idx, 4)], dma_n=1)

                        def wom(t, cc=cc, dt=dt):
                            last = None
                            for k in range(KD):
                                last = t.matmul(
                                    psW[(cc * KD + dt) % 2][:],
                                    wobf[:, k * 2048 + dt * 128:
                                         k * 2048 + (dt + 1) * 128],
                                    att[:, k * 2048 + cc * 512:
                                        k * 2048 + (cc + 1) * 512],
                                    start=(k == 0), stop=(k == KD - 1))
                            return last

                        t_wom[idx] = pp.op("tensor", wom,
                                           deps=[prev_tok(t_u, idx, 2)])
                        t_u[idx] = pp.op(
                            "vector", lambda e, dt=dt, idx=idx: e.tensor_add(
                                ufull[:, dt * 512:(dt + 1) * 512],
                                psW[idx % 2][:], xres[idx % 4][:]),
                            deps=[t_wom[idx], t_x3[idx],
                                  prev_tok(t_y1, idx, KD)])

                        def stk_mm(t, dt=dt, cc=cc):
                            t.matmul(psMV[0:1, 0:512], onesK[:, 0:1],
                                     ubx[dt % 2][:],
                                     start=(dt == 0), stop=(dt == KD - 1))
                            return t.matmul(psMV[0:1, 512:1024], onesK[:, 0:1],
                                            usx[dt % 2][:],
                                            start=(dt == 0), stop=(dt == KD - 1))

                        stk_fns[idx] = stk_mm
                        if dt >= 2:
                            t_stk[idx - 2] = pp.op(
                                "tensor", stk_fns[idx - 2],
                                deps=[t_ub[idx - 2],
                                      t_lnv[cc - 1] if (dt == 2 and cc >= 1)
                                      else None])
                        t_ub[idx] = pp.op(
                            "scalar", lambda e, dt=dt: (
                                e.copy(ubx[dt % 2][:],
                                       ufull[:, dt * 512:(dt + 1) * 512]),
                                e.activation(usx[dt % 2][:],
                                             ufull[:, dt * 512:(dt + 1) * 512],
                                             AF.Square))[-1],
                            deps=[t_u[idx],
                                  t_stk[cc * KD + dt - 2]
                                  if cc * KD + dt >= 2 else None])
                    for dt in (KD - 2, KD - 1):
                        t_stk[cc * KD + dt] = pp.op(
                            "tensor", stk_fns[cc * KD + dt],
                            deps=[t_ub[cc * KD + dt]])
                    t_st[cc] = t_stk[cc * KD + KD - 1]

                    a1 = pp.op("vector", lambda e: e.tensor_scalar_mul(
                        srow[0:1, 0:512], psMV[0:1, 0:512], 1.0 / D),
                        deps=[t_st[cc]])
                    a2 = pp.op("vector", lambda e: e.tensor_scalar_mul(
                        srow[0:1, 512:1024], psMV[0:1, 512:1024], 1.0 / D),
                        deps=[t_st[cc]])
                    a3 = pp.op("vector", lambda e: e.tensor_mul(
                        srow[0:1, 1024:1536], srow[0:1, 0:512], srow[0:1, 0:512]),
                        deps=[a1], self_edge=True)
                    a4 = pp.op("vector", lambda e: e.tensor_sub(
                        srow[0:1, 1536:2048], srow[0:1, 512:1024],
                        srow[0:1, 1024:1536]), deps=[a2, a3], self_edge=True)
                    a5 = pp.op("vector", lambda e: e.tensor_copy(
                        mrow[0:1, :], srow[0:1, 0:512]), deps=[a1], self_edge=True)
                    t_lnv[cc] = a5
                    r1 = pp.op("scalar", lambda e: e.activation(
                        srow[0:1, 1024:1536], srow[0:1, 1536:2048], AF.Ln,
                        bias=cval[0:1, 1:2]), deps=[a4])
                    t_rstd[cc] = pp.op("scalar", lambda e: e.activation(
                        rrw[0:1, :], srow[0:1, 1024:1536], AF.Exp, scale=cval[0:1, 3:4]),
                        deps=[r1], self_edge=True)

                    bm = pp.op("tensor", lambda e: e.matmul(
                        psBC[:], onesrow[0:1, :], mrow[0:1, :],
                        start=True, stop=True),
                        deps=[a5, t_brme[cc - 1] if cc >= 1 else None])
                    t_bme[cc] = pp.op("scalar", lambda e: e.copy(mbc[:], psBC[:]),
                                      deps=[bm])
                    brm = pp.op("tensor", lambda e: e.matmul(
                        psBC[:], onesrow[0:1, :], rrw[0:1, :],
                        start=True, stop=True),
                        deps=[t_bme[cc], t_rstd[cc]])
                    t_brme[cc] = pp.op("scalar", lambda e: e.copy(rbc3[:], psBC[:]),
                                       deps=[brm])

                    for dt in range(KD):
                        idx = cc * KD + dt
                        t_y1[idx] = pp.op(
                            "vector", lambda e, dt=dt: (
                                e.tensor_sub(ufull[:, dt * 512:(dt + 1) * 512],
                                             ufull[:, dt * 512:(dt + 1) * 512],
                                             mbc[:]),
                                e.tensor_mul(y1c[:, dt * 512:(dt + 1) * 512],
                                             ufull[:, dt * 512:(dt + 1) * 512],
                                             rbc3[:]))[-1],
                            deps=[t_bme[cc], t_brme[cc], t_ub[idx],
                                  t_wr1[cc - 1] if cc >= 1 else None,
                                  t_g[(cc - 1) * 4 + 3] if cc >= 1 else None])

                    t_wr1[cc] = pp.op("gpsimd", lambda e, cc=cc: e.dma_start(
                        out=dap(y1_scr, cc * 512,
                                [[2048, 128], [128 * 2048, KD], [1, 512]]),
                        in_=y1c[:].rearrange("p (k c) -> p k c", k=KD)),
                        deps=[t_y1[cc * KD + k] for k in range(KD)], dma_n=1)

                    for st4 in range(4):
                        gi = cc * 4 + st4
                        p = gi % 2

                        def gmm(t, cc=cc, st4=st4, p=p):
                            last = None
                            for k in range(KD):
                                last = t.matmul(
                                    psG[p][:],
                                    y1c[:, k * 512 + st4 * 128:
                                        k * 512 + (st4 + 1) * 128],
                                    gbf[:, k * 8:(k + 1) * 8],
                                    start=(k == 0), stop=(k == KD - 1))
                            return last

                        t_g[gi] = pp.op(
                            "tensor", gmm,
                            deps=[t_y1[cc * KD + k] for k in range(KD)] +
                                 [prev_tok(t_sg, gi, 2)])
                        t_sg[gi] = pp.op(
                            "scalar", lambda e, st4=st4, p=p: e.activation(
                                scrs[:, st4 * 8:(st4 + 1) * 8], psG[p][:],
                                AF.Sigmoid),
                            deps=[t_g[gi], prev_tok(t_ms3, gi, 4)])

                        def rank(v, st4=st4, p=p):
                            sl = scrs[:, st4 * 8:(st4 + 1) * 8]
                            v.tensor_add(ttb[p][:], sl, bias_bc[:])
                            v.memset(rka[p][:], 0.0)
                            for e2 in range(NE):
                                v.tensor_scalar(cge[p][:, e2 * 8:(e2 + 1) * 8],
                                                ttb[p][:], ttb[p][:, e2:e2 + 1],
                                                None, op0=ALU.is_ge)
                                v.tensor_scalar(cgt[p][:, e2 * 8:(e2 + 1) * 8],
                                                ttb[p][:], ttb[p][:, e2:e2 + 1],
                                                None, op0=ALU.is_gt)
                            last = None
                            for e2 in range(NE):
                                if e2 > 0:
                                    last = v.reduce_sum(
                                        rka[p][:, e2:e2 + 1],
                                        cge[p][:, e2 * 8:e2 * 8 + e2], axis=AX.X)
                                if e2 < NE - 1:
                                    last = v.reduce_sum(
                                        rka[p][:, 8 + e2:9 + e2],
                                        cgt[p][:, e2 * 8 + e2 + 1:(e2 + 1) * 8],
                                        axis=AX.X)
                            return last

                        t_rk[gi] = pp.op("vector", rank,
                                         deps=[t_sg[gi], prev_tok(t_mt, gi, 2)],
                                         self_edge=True)
                        t_ms1[gi] = pp.op("vector", lambda v, p=p: v.tensor_add(
                            rka[p][:, 0:8], rka[p][:, 0:8], rka[p][:, 8:16]),
                            deps=[t_rk[gi]], self_edge=True)
                        t_ms2[gi] = pp.op("vector", lambda v, p=p: v.tensor_scalar(
                            ttb[p][:], rka[p][:, 0:8], 2.0, None, op0=ALU.is_lt),
                            deps=[t_ms1[gi]], self_edge=True)
                        t_ms3[gi] = pp.op("vector", lambda v, p=p, st4=st4:
                                          v.tensor_mul(
                            mskb[p][:], ttb[p][:], scrs[:, st4 * 8:(st4 + 1) * 8]),
                            deps=[t_ms2[gi]], self_edge=True)
                        t_mt[gi] = pp.op(
                            "tensor", lambda e, p=p, st4=st4: e.transpose(
                                psMT[0:8, st4 * 128:(st4 + 1) * 128],
                                mskb[p][:], identm[:]),
                            deps=[t_ms3[gi], prev_tok(t_mte, gi, 1)])
                        t_mte[gi] = pp.op(
                            "scalar", lambda e, cc=cc, st4=st4: e.copy(
                                m_feat[0:8, cc * 512 + st4 * 128:
                                       cc * 512 + (st4 + 1) * 128],
                                psMT[0:8, st4 * 128:(st4 + 1) * 128]),
                            deps=[t_mt[gi]])

                pp.wait_only("gpsimd", [t_wr1[NCHUNK - 1]])
                pp.wait_only("sync", [t_x3[NCHUNK * KD - 1]])
                pp.emit()

        with contextlib.ExitStack() as stp:
            SP = lambda *a: stp.enter_context(nc.sbuf_tensor(*a))
            PPs = lambda *a: stp.enter_context(nc.psum_tensor(*a))
            w1b = [SP(f"w1b_{p}", [128, KD * HID], BF16) for p in range(2)]
            w3b = [SP(f"w3b_{p}", [128, KD * HID], BF16) for p in range(2)]
            y1res = SP("y1res", [128, KD * 2048], BF16)
            abuf = [SP(f"abuf_{p}", [128, 512], BF16) for p in range(2)]
            bbuf = [SP(f"bbuf_{p}", [128, 512], BF16) for p in range(2)]
            sabuf = [SP("sabuf_0", [128, 512], BF16)] * 2
            hbuf = [SP(f"hbuf_{p}", [128, 512], BF16) for p in range(2)]
            mbc4 = SP("mbc4", [128, 512], BF16)
            psU1 = [PPs(f"psU1_{p}", [128, 512], F32) for p in range(2)]
            psU3 = [PPs(f"psU3_{p}", [128, 512], F32) for p in range(2)]
            psMB = PPs("psMB", [128, 512], F32)

            pp = mkpipe("p4")
            if debug:
                pp.op("gpsimd", lambda e: e.dma_start(
                    out=dap(dbg["mf"], 0, [[2048, 8], [1, 2048]]),
                    in_=m_feat[0:8, :]), dma_n=1)
            NHS = 9 * NCHUNK * KH
            t_w = [None] * 9
            t_yl = pp.op("sync", lambda e: e.dma_start(
                out=y1res[:].rearrange("p (k c) -> p k c", k=KD),
                in_=dap(y1_scr, 0,
                        [[2048, 128], [128 * 2048, KD], [1, 2048]])), dma_n=1)
            t_mbe = [None] * (9 * NCHUNK)
            t_u1 = [None] * NHS
            t_u3 = [None] * NHS
            t_a = [None] * NHS
            t_b = [None] * NHS
            t_sa = [None] * NHS
            t_h = [None] * NHS
            t_hw = [None] * NHS

            def emit_wload(s):
                p = s % 2
                deps = [t_u3[(s - 1) * NCHUNK * KH - 1]] if s >= 2 else []
                if s >= 2:
                    assert deps[0] is not None
                off = 0 if s == 0 else (s - 1) * D * HID

                def f(e, s=s, off=off, p=p):
                    return [
                        e.dma_start(
                            out=w1b[p][:].rearrange("p (k c) -> p k c", k=KD),
                            in_=dap(sw1T if s == 0 else w1T, off,
                                    [[HID, 128], [128 * HID, KD], [1, HID]])),
                        e.dma_start(
                            out=w3b[p][:].rearrange("p (k c) -> p k c", k=KD),
                            in_=dap(sw3T if s == 0 else w3T, off,
                                    [[HID, 128], [128 * HID, KD], [1, HID]]))]

                t_w[s] = pp.op("gpsimd", f, deps=deps, dma_n=2)

            emit_wload(0)
            emit_wload(1)
            for s in range(9):
                p = s % 2
                for cc in range(NCHUNK):
                    sci = s * NCHUNK + cc
                    if s > 0:
                        mb = pp.op("tensor", lambda e, s=s, cc=cc: e.matmul(
                            psMB[:], rsel8[0:8, (s - 1) * 128: s * 128],
                            m_feat[0:8, cc * 512:(cc + 1) * 512],
                            start=True, stop=True),
                            deps=[prev_tok(t_mbe, sci, 1)])
                        t_mbe[sci] = pp.op("scalar", lambda e: e.copy(
                            mbc4[:], psMB[:]),
                            deps=[mb, t_b[sci * KH - 1] if sci >= 1 else None])
                    for ht in range(KH):
                        hs = sci * KH + ht
                        hp = hs % 2

                        def u1(t, p=p, cc=cc, ht=ht, hp=hp):
                            last = None
                            for k in range(KD):
                                last = t.matmul(
                                    psU1[hp][:],
                                    w1b[p][:, k * HID + ht * 128:
                                           k * HID + (ht + 1) * 128],
                                    y1res[:, k * 2048 + cc * 512:
                                          k * 2048 + (cc + 1) * 512],
                                    start=(k == 0), stop=(k == KD - 1))
                            return last

                        def u3(t, p=p, cc=cc, ht=ht, hp=hp):
                            last = None
                            for k in range(KD):
                                last = t.matmul(
                                    psU3[hp][:],
                                    w3b[p][:, k * HID + ht * 128:
                                           k * HID + (ht + 1) * 128],
                                    y1res[:, k * 2048 + cc * 512:
                                          k * 2048 + (cc + 1) * 512],
                                    start=(k == 0), stop=(k == KD - 1))
                            return last

                        t_u1[hs] = pp.op("tensor", u1,
                                         deps=[t_w[s], t_yl,
                                               prev_tok(t_a, hs, 2)])
                        t_u3[hs] = pp.op("tensor", u3,
                                         deps=[prev_tok(t_b, hs, 2)])

                        if s == 0:
                            t_sa[hs] = pp.op(
                                "scalar", lambda e, hp=hp: e.activation(
                                    sabuf[hp][:], psU1[hp][:], AF.Silu),
                                deps=[t_u1[hs], prev_tok(t_h, hs, 1)])
                            t_a[hs] = t_sa[hs]
                            t_b[hs] = pp.op(
                                "vector", lambda e, hp=hp: e.tensor_copy(
                                    bbuf[hp][:], psU3[hp][:]),
                                deps=[t_u3[hs], prev_tok(t_h, hs, 2)])
                        else:
                            t_a[hs] = pp.op(
                                "vector", lambda e, hp=hp: e.tensor_mul(
                                    abuf[hp][:], psU1[hp][:], mbc4[:]),
                                deps=[t_u1[hs], t_mbe[sci],
                                      prev_tok(t_sa, hs, 2)])
                            t_b[hs] = pp.op(
                                "vector", lambda e, hp=hp: e.tensor_mul(
                                    bbuf[hp][:], psU3[hp][:], mbc4[:]),
                                deps=[t_u3[hs], t_mbe[sci],
                                      prev_tok(t_h, hs, 2)])
                            t_sa[hs] = pp.op(
                                "scalar", lambda e, hp=hp: e.activation(
                                    sabuf[hp][:], abuf[hp][:], AF.Silu),
                                deps=[t_a[hs], prev_tok(t_h, hs, 1)])
                        t_h[hs] = pp.op(
                            "vector", lambda e, hp=hp: e.tensor_mul(
                                hbuf[hp][:], sabuf[hp][:], bbuf[hp][:]),
                            deps=[t_sa[hs], t_b[hs], prev_tok(t_hw, hs, 2)])
                        t_hw[hs] = pp.op(
                            "scalar", lambda e, s=s, cc=cc, ht=ht, hp=hp:
                            e.dma_start(
                                out=dap(h_scr,
                                        s * HID * 2048 + ht * 128 * 2048 + cc * 512,
                                        [[2048, 128], [1, 512]]),
                                in_=hbuf[hp][:]),
                            deps=[t_h[hs]], dma_n=1)
                if s + 2 <= 8:
                    emit_wload(s + 2)

            pp.wait_only("scalar", [t_hw[NHS - 1]])
            pp.wait_only("gpsimd", [t_w[8]])
            pp.wait_only("sync", [t_yl])
            pp.emit()

        with contextlib.ExitStack() as stB:
            SB5 = lambda *a: stB.enter_context(nc.sbuf_tensor(*a))
            ffnacc = SB5("ffnacc", [128, KD * 2048], BF16)

            with contextlib.ExitStack() as stp:
                SP = lambda *a: stp.enter_context(nc.sbuf_tensor(*a))
                PPs = lambda *a: stp.enter_context(nc.psum_tensor(*a))
                w2b = [SP(f"w2b_{p}", [128, KH * D], BF16) for p in range(2)]
                hcb = [SP(f"hcb_{p}", [128, KH * 512], BF16) for p in range(2)]
                psW2 = [PPs(f"psW2_{p}", [128, 512], F32) for p in range(2)]

                pp = mkpipe("p5a")
                NDS = 9 * NCHUNK * KD
                t_w2 = [None] * 9
                t_hc = [None] * (9 * NCHUNK)
                t_m2 = [None] * NDS
                t_ac = [None] * NDS

                def emit_w2(s):
                    p = s % 2
                    deps = [t_m2[(s - 1) * NCHUNK * KD - 1]] if s >= 2 else []
                    if s >= 2:
                        assert deps[0] is not None
                    off = 0 if s == 0 else (s - 1) * HID * D
                    t_w2[s] = pp.op("gpsimd", lambda e, s=s, off=off, p=p:
                                    e.dma_start(
                        out=w2b[p][:].rearrange("p (k c) -> p k c", k=KH),
                        in_=dap(sw2T if s == 0 else w2T, off,
                                [[D, 128], [128 * D, KH], [1, D]])),
                        deps=deps, dma_n=1)

                emit_w2(0)
                emit_w2(1)
                for s in range(9):
                    p = s % 2
                    for cc in range(NCHUNK):
                        sci = s * NCHUNK + cc
                        hp = sci % 2
                        t_hc[sci] = pp.op(
                            "sync", lambda e, s=s, cc=cc, hp=hp: e.dma_start(
                                out=hcb[hp][:].rearrange("p (k c) -> p k c", k=KH),
                                in_=dap(h_scr, s * HID * 2048 + cc * 512,
                                        [[2048, 128], [128 * 2048, KH], [1, 512]])),
                            deps=[t_m2[(sci - 1) * KD - 1] if sci >= 2 else None],
                            dma_n=1)
                        for dt in range(KD):
                            ds_ = sci * KD + dt
                            wp = ds_ % 2

                            def w2mm(t, cc=cc, dt=dt, hp=hp, wp=wp, p=p):
                                last = None
                                for k in range(KH):
                                    last = t.matmul(
                                        psW2[wp][:],
                                        w2b[p][:, k * D + dt * 128:
                                               k * D + (dt + 1) * 128],
                                        hcb[hp][:, k * 512:(k + 1) * 512],
                                        start=(k == 0), stop=(k == KH - 1))
                                return last

                            t_m2[ds_] = pp.op(
                                "tensor", w2mm,
                                deps=[t_w2[s], t_hc[sci], prev_tok(t_ac, ds_, 2)])

                            def acc(v, s=s, cc=cc, dt=dt, wp=wp):
                                sl = ffnacc[:, dt * 2048 + cc * 512:
                                            dt * 2048 + (cc + 1) * 512]
                                if s == 0:
                                    return v.tensor_copy(sl, psW2[wp][:])
                                return v.tensor_add(sl, psW2[wp][:], sl)

                            t_ac[ds_] = pp.op("vector", acc, deps=[t_m2[ds_]])
                    if s + 2 <= 8:
                        emit_w2(s + 2)

                pp.wait_only("sync", [t_hc[9 * NCHUNK - 1]])
                pp.wait_only("gpsimd", [t_w2[8]])
                pp.emit()

            with contextlib.ExitStack() as stp:
                SP = lambda *a: stp.enter_context(nc.sbuf_tensor(*a))
                PPs = lambda *a: stp.enter_context(nc.psum_tensor(*a))
                y1f = [SP("y1f_0", [128, KD * 512], BF16)]
                ufull = SP("ufull5", [128, KD * 512], F32)
                ubx5 = [SP(f"ubx5_{p}", [128, 512], BF16) for p in range(2)]
                usx5 = [SP(f"usx5_{p}", [128, 512], BF16) for p in range(2)]
                srow = SP("srow5", [32, 2048], F32)
                mrow = SP("mrow5", [32, 512], BF16)
                rrw = SP("rrw5", [32, 512], BF16)
                mbc = SP("mbc5", [128, 512], BF16)
                rbc5 = SP("rbc5", [128, 512], BF16)
                y2c = SP("y2c", [128, KD * 512], F32)
                psMV = PPs("psMV5", [32, 1024], F32)
                psBC = PPs("psBC5", [128, 512], F32)

                def dap_y2c():
                    return y2c[:].rearrange("p (k c) -> p k c", k=KD)

                pp = mkpipe("p5b")
                if debug:
                    pp.op("gpsimd", lambda e: e.dma_start(
                        out=dap(dbg["ffn"], 0, [[KD * 2048, 128], [1, KD * 2048]]),
                        in_=ffnacc[:]), dma_n=1)
                    for nm, scr, sz in (("q", q_scr, H * 128 * 2048),
                                        ("k", k_scr, H * 128 * S),
                                        ("v", v_scr, H * S * 128),
                                        ("y1", y1_scr, D * 2048),
                                        ("h", h_scr, 9 * HID * 2048)):
                        pp.op("gpsimd", lambda e, nm=nm, scr=scr, sz=sz:
                              e.dma_start(out=dap(dbg[nm], 0, [[1, sz]]),
                                          in_=dap(scr, 0, [[1, sz]])),
                              dma_n=1)
                t_y = [None] * NCHUNK
                t_u = [None] * (NCHUNK * KD)
                t_ub = [None] * (NCHUNK * KD)
                t_stk = [None] * (NCHUNK * KD)
                stk_fns = [None] * (NCHUNK * KD)
                t_st = [None] * NCHUNK
                t_lnv = [None] * NCHUNK
                t_rstd = [None] * NCHUNK
                t_bme = [None] * NCHUNK
                t_brme = [None] * NCHUNK
                t_y2 = [None] * (NCHUNK * KD)
                t_wr = [None] * NCHUNK

                for cc in range(NCHUNK):
                    t_y[cc] = pp.op("sync", lambda e, cc=cc: e.dma_start(
                        out=y1f[0][:].rearrange("p (k c) -> p k c", k=KD),
                        in_=dap(y1_scr, cc * 512,
                                [[2048, 128], [128 * 2048, KD], [1, 512]])),
                        deps=[t_u[cc * KD - 1] if cc >= 1 else None], dma_n=1)
                    for dt in range(KD):
                        idx = cc * KD + dt
                        t_u[idx] = pp.op(
                            "vector", lambda e, cc=cc, dt=dt: e.tensor_add(
                                ufull[:, dt * 512:(dt + 1) * 512],
                                ffnacc[:, dt * 2048 + cc * 512:
                                       dt * 2048 + (cc + 1) * 512],
                                y1f[0][:, dt * 512:(dt + 1) * 512]),
                            deps=[t_y[cc], prev_tok(t_y2, idx, KD)])

                        def stk_mm(t, dt=dt):
                            t.matmul(psMV[0:1, 0:512], onesK[:, 0:1],
                                     ubx5[dt % 2][:],
                                     start=(dt == 0), stop=(dt == KD - 1))
                            return t.matmul(psMV[0:1, 512:1024], onesK[:, 0:1],
                                            usx5[dt % 2][:],
                                            start=(dt == 0), stop=(dt == KD - 1))

                        stk_fns[idx] = stk_mm
                        if dt >= 2:
                            t_stk[idx - 2] = pp.op(
                                "tensor", stk_fns[idx - 2],
                                deps=[t_ub[idx - 2],
                                      t_lnv[cc - 1] if (dt == 2 and cc >= 1)
                                      else None])
                        t_ub[idx] = pp.op(
                            "scalar", lambda e, dt=dt: (
                                e.copy(ubx5[dt % 2][:],
                                       ufull[:, dt * 512:(dt + 1) * 512]),
                                e.activation(usx5[dt % 2][:],
                                             ufull[:, dt * 512:(dt + 1) * 512],
                                             AF.Square))[-1],
                            deps=[t_u[idx],
                                  t_stk[idx - 2] if idx >= 2 else None])
                    for dt in (KD - 2, KD - 1):
                        t_stk[cc * KD + dt] = pp.op(
                            "tensor", stk_fns[cc * KD + dt],
                            deps=[t_ub[cc * KD + dt]])
                    t_st[cc] = t_stk[cc * KD + KD - 1]

                    a1 = pp.op("vector", lambda e: e.tensor_scalar_mul(
                        srow[0:1, 0:512], psMV[0:1, 0:512], 1.0 / D),
                        deps=[t_st[cc]])
                    a2 = pp.op("vector", lambda e: e.tensor_scalar_mul(
                        srow[0:1, 512:1024], psMV[0:1, 512:1024], 1.0 / D),
                        deps=[t_st[cc]])
                    a3 = pp.op("vector", lambda e: e.tensor_mul(
                        srow[0:1, 1024:1536], srow[0:1, 0:512], srow[0:1, 0:512]),
                        deps=[a1], self_edge=True)
                    a4 = pp.op("vector", lambda e: e.tensor_sub(
                        srow[0:1, 1536:2048], srow[0:1, 512:1024],
                        srow[0:1, 1024:1536]), deps=[a2, a3], self_edge=True)
                    a5 = pp.op("vector", lambda e: e.tensor_copy(
                        mrow[0:1, :], srow[0:1, 0:512]), deps=[a1], self_edge=True)
                    t_lnv[cc] = a5
                    r1 = pp.op("scalar", lambda e: e.activation(
                        srow[0:1, 1024:1536], srow[0:1, 1536:2048], AF.Ln,
                        bias=cval[0:1, 1:2]), deps=[a4])
                    t_rstd[cc] = pp.op("scalar", lambda e: e.activation(
                        rrw[0:1, :], srow[0:1, 1024:1536],
                        AF.Exp, scale=cval[0:1, 3:4]),
                        deps=[r1], self_edge=True)
                    bm = pp.op("tensor", lambda e: e.matmul(
                        psBC[:], onesrow[0:1, :], mrow[0:1, :],
                        start=True, stop=True),
                        deps=[a5, t_brme[cc - 1] if cc >= 1 else None])
                    t_bme[cc] = pp.op("scalar", lambda e: e.copy(mbc[:], psBC[:]),
                                      deps=[bm])
                    brm = pp.op("tensor", lambda e: e.matmul(
                        psBC[:], onesrow[0:1, :], rrw[0:1, :],
                        start=True, stop=True),
                        deps=[t_bme[cc], t_rstd[cc]])
                    t_brme[cc] = pp.op("scalar", lambda e: e.copy(rbc5[:], psBC[:]),
                                       deps=[brm])

                    for dt in range(KD):
                        idx = cc * KD + dt
                        t_y2[idx] = pp.op(
                            "vector", lambda e, dt=dt: (
                                e.tensor_sub(ufull[:, dt * 512:(dt + 1) * 512],
                                             ufull[:, dt * 512:(dt + 1) * 512],
                                             mbc[:]),
                                e.tensor_mul(y2c[:, dt * 512:(dt + 1) * 512],
                                             ufull[:, dt * 512:(dt + 1) * 512],
                                             rbc5[:]))[-1],
                            deps=[t_bme[cc], t_brme[cc], t_ub[idx],
                                  t_wr[cc - 1] if cc >= 1 else None])
                    t_wr[cc] = pp.op("gpsimd", lambda e, cc=cc: e.dma_start(
                        out=dap(y2_out, cc * 512,
                                [[2048, 128], [128 * 2048, KD], [1, 512]]),
                        in_=dap_y2c(),
                    ), deps=[t_y2[cc * KD + k] for k in range(KD)], dma_n=1)

                pp.wait_only("gpsimd", [t_wr[NCHUNK - 1]])
                pp.wait_only("sync", [t_y[NCHUNK - 1]])
                pp.emit()

    return nc


_NC_CACHE = None


def kernel(L0_attn_x, L0_attn_complex_freq, L0_attn_W_Aq, L0_attn_W_Ak,
           L0_attn_W_Av, L0_attn_W_Bq, L0_attn_W_Bk, L0_attn_W_Bv, L0_attn_Wo,
           L0_ffn_gate, L0_ffn_expert_bias, L0_ffn_w1, L0_ffn_w2, L0_ffn_w3,
           L0_ffn_shared_w1, L0_ffn_shared_w2, L0_ffn_shared_w3):
    global _NC_CACHE
    x = np.asarray(L0_attn_x, dtype=np.float32)
    wproj = np.ascontiguousarray(np.concatenate(
        [np.asarray(L0_attn_W_Bq), np.asarray(L0_attn_W_Bk),
         np.asarray(L0_attn_W_Bv), np.asarray(L0_attn_W_Aq),
         np.asarray(L0_attn_W_Ak), np.asarray(L0_attn_W_Av)],
        axis=1).astype(np.float32)).reshape(-1)
    wo = np.ascontiguousarray(np.asarray(L0_attn_Wo, dtype=np.float32)).reshape(-1)
    fr = np.ascontiguousarray(np.asarray(
        L0_attn_complex_freq, dtype=np.float32)).reshape(-1)
    gT = np.ascontiguousarray(
        np.asarray(L0_ffn_gate, dtype=np.float32).T).reshape(-1)
    eb = np.ascontiguousarray(np.asarray(L0_ffn_expert_bias, dtype=np.float32))
    s1 = np.ascontiguousarray(
        np.asarray(L0_ffn_shared_w1, dtype=np.float32).T).reshape(-1)
    s3 = np.ascontiguousarray(
        np.asarray(L0_ffn_shared_w3, dtype=np.float32).T).reshape(-1)
    s2 = np.ascontiguousarray(
        np.asarray(L0_ffn_shared_w2, dtype=np.float32).T).reshape(-1)
    w1 = np.ascontiguousarray(
        np.asarray(L0_ffn_w1, dtype=np.float32).transpose(0, 2, 1)).reshape(-1)
    w3 = np.ascontiguousarray(
        np.asarray(L0_ffn_w3, dtype=np.float32).transpose(0, 2, 1)).reshape(-1)
    w2 = np.ascontiguousarray(
        np.asarray(L0_ffn_w2, dtype=np.float32).transpose(0, 2, 1)).reshape(-1)
    rsel = np.zeros((8, 8 * 128), dtype=np.float32)
    for e in range(8):
        rsel[e, e * 128:(e + 1) * 128] = 1.0
    rsel = rsel.reshape(-1)

    if _NC_CACHE is None:
        _NC_CACHE = build_program()
    nc = _NC_CACHE

    in_maps = []
    for c in range(8):
        b, h2 = c // 2, c % 2
        own = x[b, h2 * 2048:(h2 + 1) * 2048]
        other = x[b, (1 - h2) * 2048:(2 - h2) * 2048]
        xTc = np.ascontiguousarray(
            np.concatenate([own, other], axis=0).T).reshape(-1)
        in_maps.append({
            "xT": xTc, "wproj": wproj, "wo_in": wo, "freq": fr, "gateT": gT,
            "ebias": eb, "sw1T": s1, "sw3T": s3, "sw2T": s2,
            "w1T": w1, "w3T": w3, "w2T": w2, "rsel8_in": rsel,
        })

    res = run_bass_kernel_spmd(nc, in_maps, core_ids=list(range(8)))
    out = np.empty((B, S, D), dtype=np.float32)
    for c in range(8):
        b, h2 = c // 2, c % 2
        out[b, h2 * 2048:(h2 + 1) * 2048, :] = res.results[c]["y2"].T
    return out


# version 30
# speedup vs baseline: 1.0183x; 1.0183x over previous
import contextlib
import numpy as np
import concourse.bass as bass
import concourse.mybir as mybir
from concourse.bass_utils import run_bass_kernel_spmd

F32 = mybir.dt.float32
BF16 = mybir.dt.bfloat16
AF = mybir.ActivationFunctionType
ALU = mybir.AluOpType
AX = mybir.AxisListType

B, S, D, H, DH = 4, 4096, 2048, 16, 128
RQ, RK, RV = 6, 2, 2
HID = 1024
NE = 8
NT = S // 128
NO = 16
KD = D // 128
KH = HID // 128
NCHUNK = 4
Q_SCALE = 1.0 / (RQ * float(np.sqrt(DH)))

C_BQ, C_BK, C_BV, C_AQ, C_AK, C_AV = 0, 768, 1024, 1280, 1376, 1408
C_END = 1440

ENGINES = ("sync", "gpsimd", "vector", "scalar", "tensor")


class Pipe:

    DMA_SLOTS = {"sync": 4, "gpsimd": 4, "scalar": 2, "vector": 1, "tensor": 1}

    def __init__(self, nc, sems, cnt, dma_rot, tag, xsem=None, xbase=0):
        self.nc = nc
        self.tag = tag
        self.sem = sems
        self.cnt = cnt
        self.dma_rot = dma_rot
        self.xsem = xsem
        self.xbase = xbase
        self.start_cnt = dict(cnt)
        self.ops = {e: [] for e in ENGINES}
        self.hwm = {e: {k: cnt[k] for k in sems} for e in ENGINES}
        self.slot_cons = {}

    def op(self, eng, fn, deps=(), dma_n=0, self_edge=False):
        if dma_n:
            slot = self.dma_rot[eng] % self.DMA_SLOTS[eng]
            self.dma_rot[eng] += 1
            key = f"{eng}#d{slot}"
            cons = self.slot_cons.pop(key, {})
            deps = list(deps) + list(cons.values())
        else:
            key = eng + "#c"
        waits = []
        if dma_n and not cons and self.cnt[key] > self.start_cnt.get(key, 0):
            waits.append((key, self.cnt[key]))
            self.hwm[eng][key] = max(self.hwm[eng][key], self.cnt[key])
        if self_edge:
            for k in self.sem:
                if (k.split("#")[0] == eng and k != key
                        and self.cnt[k] > self.hwm[eng][k]):
                    waits.append((k, self.cnt[k]))
                    self.hwm[eng][k] = self.cnt[k]
            if self.cnt[key] > self.hwm[eng][key]:
                waits.append((key, self.cnt[key]))
                self.hwm[eng][key] = self.cnt[key]
        for d in deps:
            if d is None:
                continue
            pk, pc = d
            if pk == key or (pk.split("#")[0] == eng and pk.endswith("#c")
                             and not dma_n):
                continue
            if pc > self.hwm[eng][pk]:
                waits.append((pk, pc))
                self.hwm[eng][pk] = pc
        amt = 16 * dma_n if dma_n else 1
        self.ops[eng].append((fn, waits, amt, key))
        self.cnt[key] += amt
        tok = (key, self.cnt[key])
        for d in deps:
            if d is not None and "#d" in d[0]:
                self.slot_cons.setdefault(d[0], {})[eng] = tok
        return tok

    def wait_only(self, eng, deps):
        waits = []
        for d in deps:
            if d is None:
                continue
            pk, pc = d
            if pc > self.hwm[eng][pk]:
                waits.append((pk, pc))
                self.hwm[eng][pk] = pc
        self.ops[eng].append((None, waits, 0, eng + "#c"))

    def check(self):
        pos = {e: 0 for e in ENGINES}
        val = dict(self.start_cnt)
        progress = True
        while progress:
            progress = False
            for e in ENGINES:
                while pos[e] < len(self.ops[e]):
                    fn, waits, amt, key = self.ops[e][pos[e]]
                    if any(val[pk] < pc for pk, pc in waits):
                        break
                    val[key] += amt
                    pos[e] += 1
                    progress = True
        stuck = {e: (pos[e], len(self.ops[e])) for e in ENGINES
                 if pos[e] < len(self.ops[e])}
        if stuck:
            msg = []
            for e, (p, n) in stuck.items():
                msg.append(f"{e}@{p}/{n} waits={self.ops[e][p][1]}")
            raise RuntimeError(f"pipe {self.tag} deadlock: {msg}")

    def emit(self):
        self.check()
        if self.xsem is not None:
            finals = [(k, self.cnt[k]) for k in self.cnt
                      if self.cnt[k] > self.start_cnt[k] or k == "gpsimd#c"]
            self.ops["gpsimd"].append((None, finals, 0, "gpsimd#c"))
            self.ops["gpsimd"].append(("XINC", [], 0, "gpsimd#c"))
        with self.nc.Block(self.tag) as blk:
            for e in ENGINES:
                if not self.ops[e]:
                    continue

                def body(eng, _e=e):
                    if self.xsem is not None and self.xbase > 0:
                        eng.wait_ge(self.xsem, self.xbase)
                    for fn, waits, amt, key in self.ops[_e]:
                        if fn == "XINC":
                            eng.nop().then_inc(self.xsem, 1)
                            continue
                        for pk, pc in waits:
                            eng.wait_ge(self.sem[pk], pc)
                        if fn is None:
                            continue
                        instr = fn(eng)
                        if amt:
                            if isinstance(instr, (list, tuple)):
                                assert 16 * len(instr) == amt
                                for it in instr:
                                    it.then_inc(self.sem[key], 16)
                            elif instr is None:
                                eng.nop().then_inc(self.sem[key], amt)
                            else:
                                instr.then_inc(self.sem[key], amt)
                getattr(blk, e)(body)


def make_identity(g, identity):
    sq = identity.shape[1]
    g.memset(identity[:], 0.0)
    return g.affine_select(
        out=identity[:], in_=identity[:],
        compare_op=ALU.not_equal, fill=1.0, base=0,
        pattern=[[-1, sq]], channel_multiplier=1)


def prev_tok(lst, g, n=1):
    return lst[g - n] if g - n >= 0 else None


def build_program(debug=False):
    nc = bass.Bass("TRN2", target_bir_lowering=False)

    xT = nc.dram_tensor("xT", [D * S], F32, kind="ExternalInput")
    wproj = nc.dram_tensor("wproj", [D * C_END], F32, kind="ExternalInput")
    wo_in = nc.dram_tensor("wo_in", [D * D], F32, kind="ExternalInput")
    freq = nc.dram_tensor("freq", [2049 * 128], F32, kind="ExternalInput")
    gateT = nc.dram_tensor("gateT", [D * NE], F32, kind="ExternalInput")
    ebias = nc.dram_tensor("ebias", [NE], F32, kind="ExternalInput")
    sw1T = nc.dram_tensor("sw1T", [D * HID], F32, kind="ExternalInput")
    sw3T = nc.dram_tensor("sw3T", [D * HID], F32, kind="ExternalInput")
    sw2T = nc.dram_tensor("sw2T", [HID * D], F32, kind="ExternalInput")
    w1T = nc.dram_tensor("w1T", [NE * D * HID], F32, kind="ExternalInput")
    w3T = nc.dram_tensor("w3T", [NE * D * HID], F32, kind="ExternalInput")
    w2T = nc.dram_tensor("w2T", [NE * HID * D], F32, kind="ExternalInput")
    rsel8_in = nc.dram_tensor("rsel8_in", [8 * 8 * 128], F32, kind="ExternalInput")

    y2_out = nc.dram_tensor("y2", [D, 2048], F32, kind="ExternalOutput")
    if debug:
        dbg = {n: nc.dram_tensor(f"dbg_{n}", sz, BF16, kind="ExternalOutput")
               for n, sz in (("q", [H * 128 * 2048]), ("k", [H * 128 * S]),
                             ("v", [H * S * 128]), ("y1", [D * 2048]),
                             ("h", [9 * HID * 2048]), ("att", [128 * H * 2048]),
                             ("mf", [8 * 2048]), ("ffn", [128 * KD * 2048]))}

    q_scr = nc.dram_tensor("q_scr", [H * 128 * 2048], BF16, kind="Internal")
    k_scr = nc.dram_tensor("k_scr", [H * 128 * S], BF16, kind="Internal")
    v_scr = nc.dram_tensor("v_scr", [H * S * 128], BF16, kind="Internal")
    y1_scr = nc.dram_tensor("y1_scr", [D * 2048], BF16, kind="Internal")
    h_scr = nc.dram_tensor("h_scr", [9 * HID * 2048], BF16, kind="Internal")

    def dap(t, off, dims):
        return bass.AP(t, off, dims)

    with contextlib.ExitStack() as st0:
        Sb = lambda *a: st0.enter_context(nc.sbuf_tensor(*a))
        xsem = st0.enter_context(nc.semaphore("xphase"))
        xctr = [0]
        gsems = {}
        gcnt = {}
        grot = {e: 0 for e in ENGINES}
        for e in ENGINES:
            gsems[e + "#c"] = st0.enter_context(nc.semaphore(f"{e}_c"))
            gcnt[e + "#c"] = 0
            for sl in range(Pipe.DMA_SLOTS[e]):
                gsems[f"{e}#d{sl}"] = st0.enter_context(
                    nc.semaphore(f"{e}_d{sl}"))
                gcnt[f"{e}#d{sl}"] = 0

        def mkpipe(tag):
            p = Pipe(nc, gsems, gcnt, grot, tag, xsem=xsem, xbase=xctr[0])
            xctr[0] += 1
            return p

        onesK = Sb("onesK", [128, 16], BF16)
        onesrow = Sb("onesrow", [32, 128], BF16)
        identm = Sb("identm", [128, 128], BF16)
        rsel8 = Sb("rsel8", [32, 8 * 128], BF16)
        bias_bc = Sb("bias_bc", [128, NE], F32)
        gbf = Sb("gbf", [128, KD * NE], BF16)
        m_feat = Sb("m_feat", [32, 2048], BF16)
        negM16 = Sb("negM16", [32, 4], F32)
        cval = Sb("cval", [32, 4], F32)

        with contextlib.ExitStack() as stA:
            SA = lambda *a: stA.enter_context(nc.sbuf_tensor(*a))
            wbf = SA("wbf", [128, KD * C_END], BF16)

            with contextlib.ExitStack() as stp:
                SP = lambda *a: stp.enter_context(nc.sbuf_tensor(*a))
                wf32 = SP("wf32", [128, KD * C_END], F32)
                frB = SP("frB", [128, 64], F32)
                fiB = SP("fiB", [128, 64], F32)
                frW = SP("frW", [128, 512], F32)
                fiW = SP("fiW", [128, 512], F32)
                tA = SP("tA", [128, 512], F32)
                tB = SP("tB", [128, 512], F32)
                tC = SP("tC", [128, 512], F32)

                pp = mkpipe("p0")
                d_wk = [None] * KD
                for k in range(KD):
                    d_wk[k] = pp.op("sync", lambda e, k=k: e.dma_start(
                        out=wf32[:, k * C_END:(k + 1) * C_END],
                        in_=dap(wproj, k * 128 * C_END,
                                [[C_END, 128], [1, C_END]])), dma_n=1)
                d_fr = pp.op("gpsimd", lambda e: e.dma_start(
                    out=frB[:],
                    in_=dap(freq, 2048 * 128, [[0, 128], [2, 64]]),
                    allow_slow_non_contiguous=True), dma_n=1)
                d_fi = pp.op("gpsimd", lambda e: e.dma_start(
                    out=fiB[:],
                    in_=dap(freq, 2048 * 128 + 1, [[0, 128], [2, 64]]),
                    allow_slow_non_contiguous=True), dma_n=1)
                d_b = pp.op("gpsimd", lambda e: e.dma_start(
                    out=bias_bc[:], in_=dap(ebias, 0, [[0, 128], [1, NE]])), dma_n=1)
                d_g = pp.op("gpsimd", lambda e: e.dma_start(
                    out=gbf[:].rearrange("p (k c) -> p k c", k=KD),
                    in_=dap(gateT, 0, [[NE, 128], [128 * NE, KD], [1, NE]])), dma_n=1)
                t_ms = pp.op("gpsimd", lambda e: e.memset(rsel8[:], 0.0))
                d_r = pp.op("gpsimd", lambda e: e.dma_start(
                    out=rsel8[0:8, :],
                    in_=dap(rsel8_in, 0, [[1024, 8], [1, 1024]])),
                    deps=[t_ms], dma_n=1)
                pp.op("gpsimd", lambda e: e.memset(cval[:, 0:1], 1e-30))
                pp.op("gpsimd", lambda e: e.memset(cval[:, 1:2], 1e-5))
                pp.op("gpsimd", lambda e: e.memset(cval[:, 2:3], -1.0))
                pp.op("gpsimd", lambda e: e.memset(cval[:, 3:4], -0.5))
                pp.op("gpsimd", lambda e: e.memset(onesK[:], 1.0))
                pp.op("gpsimd", lambda e: e.memset(onesrow[:], 1.0))
                t_mi = pp.op("gpsimd", lambda e: e.memset(identm[:], 0.0))
                pp.op("gpsimd", lambda e: e.affine_select(
                    out=identm[:], in_=identm[:],
                    compare_op=ALU.not_equal, fill=1.0, base=0,
                    pattern=[[-1, 128]], channel_multiplier=1), self_edge=True)

                def rep(v):
                    last = None
                    for rr in range(8):
                        v.tensor_copy(frW[:, rr * 64:(rr + 1) * 64], frB[:])
                    for rr in range(8):
                        last = v.tensor_copy(fiW[:, rr * 64:(rr + 1) * 64],
                                             fiB[:])
                    return last

                t_rep = pp.op("vector", rep, deps=[d_fr, d_fi])

                def fold_k(v, k):
                    base = k * C_END
                    ev = wf32[:, base: base + 1024: 2]
                    od = wf32[:, base + 1: base + 1024: 2]
                    v.tensor_mul(tA[:], ev, frW[:])
                    v.tensor_mul(tB[:], od, fiW[:])
                    v.tensor_sub(tA[:], tA[:], tB[:])
                    v.tensor_mul(tB[:], ev, fiW[:])
                    v.tensor_mul(tC[:], od, frW[:])
                    v.tensor_add(tB[:], tB[:], tC[:])
                    v.tensor_copy(ev, tA[:])
                    return v.tensor_copy(od, tB[:])

                def cast_k(v, k):
                    b = k * C_END
                    v.tensor_copy(wbf[:, b: b + C_BV], wf32[:, b: b + C_BV])
                    v.tensor_scalar_mul(wbf[:, b + C_BV: b + C_AQ],
                                        wf32[:, b + C_BV: b + C_AQ], 1.0 / RV)
                    v.tensor_scalar_mul(wbf[:, b + C_AQ: b + C_AK],
                                        wf32[:, b + C_AQ: b + C_AK], Q_SCALE)
                    v.tensor_scalar_mul(wbf[:, b + C_AK: b + C_AV],
                                        wf32[:, b + C_AK: b + C_AV], 1.0 / RK)
                    return v.tensor_copy(wbf[:, b + C_AV: b + C_END],
                                         wf32[:, b + C_AV: b + C_END])

                t_fold = None
                for k in range(KD):
                    t_fk = pp.op("vector", lambda v, k=k: fold_k(v, k),
                                 deps=[d_wk[k], t_rep],
                                 self_edge=(k == 0))
                    t_fold = pp.op("vector", lambda v, k=k: cast_k(v, k),
                                   deps=[t_fk])
                pp.wait_only("gpsimd", [d_fr, d_fi, d_b, d_g, d_r])
                pp.wait_only("sync", [d_wk[KD - 1]])
                pp.emit()

            with contextlib.ExitStack() as stp:
                SP = lambda *a: stp.enter_context(nc.sbuf_tensor(*a))
                PPs = lambda *a: stp.enter_context(nc.psum_tensor(*a))
                xf32 = [SP(f"xf32_{p}", [128, D], F32) for p in range(3)]
                xbf = [SP(f"xbf_{p}", [128, D], BF16) for p in range(3)]
                bfq = [SP(f"bfq_{p}", [128, 768], BF16) for p in range(3)]
                faq = [SP(f"faq_{p}", [128, 96], F32) for p in range(3)]
                bkv = [SP(f"bkv_{p}", [128, 512], BF16) for p in range(3)]
                fakv = [SP(f"fakv_{p}", [128, 64], F32) for p in range(3)]
                qtok = [SP(f"qtok_{p}", [128, 2048], BF16) for p in range(3)]
                ktok = [SP(f"ktok_{p}", [128, 2048], BF16) for p in range(3)]
                vtok = [SP(f"vtok_{p}", [128, 2048], BF16) for p in range(3)]
                prod = [SP(f"prod_{r}", [128, 2048], BF16) for r in range(6)]
                prodK = [SP(f"prodK_{r}", [128, 2048], BF16) for r in range(2)]
                prodV = [SP(f"prodV_{r}", [128, 2048], BF16) for r in range(2)]
                sqn = [SP(f"sqn_{p}", [128, 2048], BF16) for p in range(3)]
                nrmt = [SP(f"nrmt_{p}", [128, 16], BF16) for p in range(3)]
                qnrm = SP("qnrm", [32, 2048], BF16)
                knrm = SP("knrm", [32, 4096], BF16)
                nrmall = SP("nrmall", [128, 48 * 16], BF16)
                nmax = SP("nmax", [32, 4], F32)
                psNT = [PPs(f"psNT_{p}", [32, 128], BF16) for p in range(2)]
                psA = [PPs(f"psA_{p}", [128, 512], F32) for p in range(2)]
                psB = [PPs(f"psB_{p}", [128, 416], F32) for p in range(2)]
                psC = [PPs(f"psC_{p}", [128, 512], F32) for p in range(2)]

                pp = mkpipe("p1")
                t_x = [None] * NT
                t_cast = [None] * NT
                t_proj = [None] * NT
                t_pev = [None] * NT
                t_form = [None] * NT
                t_wq = [None] * NT
                t_wk = [None] * NT
                t_wv = [None] * NT
                t_nsq = []
                t_nred = []
                t_ntp = []
                t_nev = []
                t_ntile = [None] * NT

                t_prodv = [None] * NT
                t_prods = [None] * NT

                def emit_front(i):
                    p = i % 3
                    q = i % 2
                    own = i < NO
                    t_x[i] = pp.op("sync", lambda e, i=i, p=p: e.dma_start(
                        out=xf32[p][:].rearrange("p (k c) -> p k c", k=KD),
                        in_=dap(xT, i * 128, [[S, 128], [128 * S, KD], [1, 128]])),
                        deps=[t_cast[i - 3] if i >= 3 else None], dma_n=1)
                    t_cast[i] = pp.op("vector", lambda e, p=p: e.tensor_copy(
                        xbf[p][:], xf32[p][:]),
                        deps=[t_x[i], t_proj[i - 3] if i >= 3 else None])

                    def proj(t, p=p, q=q, own=own):
                        last = None
                        for k in range(KD):
                            lhsT = xbf[p][:, k * 128:(k + 1) * 128]
                            wb = k * C_END
                            st_, sp_ = (k == 0), (k == KD - 1)
                            if own:
                                t.matmul(psA[q][:], lhsT,
                                         wbf[:, wb: wb + 512], start=st_, stop=sp_)
                                t.matmul(psB[q][:, 0:256], lhsT,
                                         wbf[:, wb + 512: wb + 768],
                                         start=st_, stop=sp_)
                                t.matmul(psB[q][:, 256:352], lhsT,
                                         wbf[:, wb + C_AQ: wb + C_AK],
                                         start=False, stop=sp_)
                            t.matmul(psC[q][:], lhsT,
                                     wbf[:, wb + C_BK: wb + C_AQ],
                                     start=st_, stop=sp_)
                            last = t.matmul(psB[q][:, 352:416], lhsT,
                                            wbf[:, wb + C_AK: wb + C_END],
                                            start=(st_ and not own), stop=sp_)
                        return last

                    t_proj[i] = pp.op("tensor", proj,
                                      deps=[t_cast[i],
                                            t_pev[i - 2] if i >= 2 else None])

                    def pev(sc, p=p, q=q, own=own):
                        if own:
                            sc.copy(bfq[p][:, 0:512], psA[q][:])
                            sc.copy(bfq[p][:, 512:768], psB[q][:, 0:256])
                            sc.copy(faq[p][:], psB[q][:, 256:352])
                        sc.copy(bkv[p][:], psC[q][:])
                        return sc.copy(fakv[p][:], psB[q][:, 352:416])

                    t_pev[i] = pp.op("scalar", pev,
                                     deps=[t_proj[i],
                                           t_prodv[i - 3] if i >= 3 else None,
                                           t_prods[i - 3] if i >= 3 else None])

                def emit_compute(i):
                    p = i % 3
                    own = i < NO

                    def q_prods(v, p=p):
                        last = None
                        for r in range(RQ):
                            for hh in range(H):
                                last = v.tensor_scalar(
                                    prod[r][:, hh * 128:(hh + 1) * 128],
                                    bfq[p][:, r * 128:(r + 1) * 128],
                                    faq[p][:, r * H + hh: r * H + hh + 1],
                                    None, op0=ALU.mult)
                        return last

                    def q_adds(v, p=p):
                        v.tensor_add(prod[0][:], prod[0][:], prod[1][:])
                        v.tensor_add(prod[2][:], prod[2][:], prod[3][:])
                        v.tensor_add(prod[4][:], prod[4][:], prod[5][:])
                        v.tensor_add(prod[0][:], prod[0][:], prod[2][:])
                        return v.tensor_add(qtok[p][:], prod[0][:], prod[4][:])

                    def k_prods_dve(v, p=p):
                        last = None
                        for r in range(RK):
                            for hh in range(H):
                                last = v.tensor_scalar(
                                    prodK[r][:, hh * 128:(hh + 1) * 128],
                                    bkv[p][:, r * 128:(r + 1) * 128],
                                    fakv[p][:, r * H + hh: r * H + hh + 1],
                                    None, op0=ALU.mult)
                        return last

                    def kv_prods_sc(sc, p=p, own=own):
                        last = None
                        if own:
                            for r in range(RK):
                                for hh in range(H):
                                    last = sc.activation(
                                        prodK[r][:, hh * 128:(hh + 1) * 128],
                                        bkv[p][:, r * 128:(r + 1) * 128],
                                        AF.Copy,
                                        scale=fakv[p][:, r * H + hh:
                                                      r * H + hh + 1])
                        for r in range(RV):
                            for hh in range(H):
                                last = sc.activation(
                                    prodV[r][:, hh * 128:(hh + 1) * 128],
                                    bkv[p][:, 256 + r * 128: 256 + (r + 1) * 128],
                                    AF.Copy,
                                    scale=fakv[p][:, 32 + r * H + hh:
                                                  33 + r * H + hh])
                        return last

                    def kv_adds(v, p=p):
                        v.tensor_add(ktok[p][:], prodK[0][:], prodK[1][:])
                        return v.tensor_add(vtok[p][:], prodV[0][:], prodV[1][:])

                    war = [t_wv[i - 3] if i >= 3 else None,
                           t_wk[i - 3] if i >= 3 else None,
                           t_wq[i - 3] if (i >= 3 and i - 3 < NO) else None,
                           t_ntile[i - 3] if i >= 3 else None]
                    if own:
                        tq1 = pp.op("vector", q_prods, deps=[t_pev[i]] + war)
                        tq2 = pp.op("vector", q_adds, deps=[tq1])
                        tsc = pp.op("scalar", kv_prods_sc,
                                    deps=[t_pev[i], t_form[i - 1] if i >= 1
                                          else None] + war)
                        t_form[i] = pp.op("vector", kv_adds, deps=[tq2, tsc])
                        t_prodv[i] = tq1
                    else:
                        tk1 = pp.op("vector", k_prods_dve,
                                    deps=[t_pev[i]] + war)
                        tsc = pp.op("scalar", kv_prods_sc,
                                    deps=[t_pev[i], t_form[i - 1] if i >= 1
                                          else None] + war)
                        t_form[i] = pp.op("vector", kv_adds, deps=[tk1, tsc])
                        t_prodv[i] = tk1
                    t_prods[i] = tsc

                    if own:
                        t_wq[i] = pp.op("gpsimd", lambda e, p=p, i=i: e.dma_start(
                            out=dap(q_scr, i * 128 * 128,
                                    [[128, 128], [2048 * 128, H], [1, 128]]),
                            in_=qtok[p][:].rearrange("p (h c) -> p h c", h=H)),
                            deps=[t_form[i]], dma_n=1)
                    t_wk[i] = pp.op("gpsimd", lambda e, p=p, i=i: e.dma_start(
                        out=dap(k_scr, i * 128 * 128,
                                [[128, 128], [S * 128, H], [1, 128]]),
                        in_=ktok[p][:].rearrange("p (h c) -> p h c", h=H)),
                        deps=[t_form[i]], dma_n=1)
                    t_wv[i] = pp.op("gpsimd", lambda e, p=p, i=i: e.dma_start(
                        out=dap(v_scr, i * 128 * 128,
                                [[128, 128], [S * 128, H], [1, 128]]),
                        in_=vtok[p][:].rearrange("p (h c) -> p h c", h=H)),
                        deps=[t_form[i]], dma_n=1)

                norm_dsts = []

                def emit_norm(ti, src_t, dst, col):
                    gn = len(t_nsq)
                    pn = gn % 3
                    norm_dsts.append((dst, col))
                    t_nsq.append(pp.op(
                        "scalar", lambda e, pn=pn, src_t=src_t: e.activation(
                            sqn[pn][:], src_t[:], AF.Square),
                        deps=[t_form[ti],
                              t_nred[gn - 3] if gn >= 3 else None]))

                    def nred(e, pn=pn, gn=gn):
                        with nc.allow_low_precision(reason="softmax bound only"):
                            return e.reduce_sum(
                                nrmall[:, gn * 16:(gn + 1) * 16].rearrange(
                                    "p (h o) -> p h o", h=H),
                                sqn[pn][:].rearrange("p (h c) -> p h c", h=H),
                                axis=AX.X)

                    t_nred.append(pp.op(
                        "vector", nred, deps=[t_nsq[gn]]))

                def emit_norms_for(j):
                    pj = j % 3
                    emit_norm(j, ktok[pj], knrm, j * 128)
                    if j < NO:
                        emit_norm(j, qtok[pj], qnrm, j * 128)
                    t_ntile[j] = t_nsq[-1]

                for i in range(NT + 2):
                    if i < NT:
                        emit_front(i)
                    if 1 <= i <= NT:
                        emit_compute(i - 1)
                    if 2 <= i:
                        emit_norms_for(i - 2)

                for gn, (dst, col) in enumerate(norm_dsts):
                    bn = gn % 2
                    t_ntp.append(pp.op(
                        "tensor", lambda e, gn=gn, bn=bn: e.transpose(
                            psNT[bn][0:16, :],
                            nrmall[:, gn * 16:(gn + 1) * 16], identm[:]),
                        deps=[t_nred[gn],
                              t_nev[gn - 2] if gn >= 2 else None]))
                    t_nev.append(pp.op(
                        "scalar", lambda e, dst=dst, col=col, bn=bn: e.copy(
                            dst[0:16, col:col + 128], psNT[bn][0:16, :]),
                        deps=[t_ntp[gn]]))
                b1 = pp.op("vector", lambda e: e.reduce_max(
                    nmax[0:16, 0:1], qnrm[0:16, :], axis=AX.X),
                    deps=[t_nev[-1]], self_edge=True)
                b2 = pp.op("vector", lambda e: e.reduce_max(
                    nmax[0:16, 1:2], knrm[0:16, :], axis=AX.X),
                    deps=[b1], self_edge=True)
                b3 = pp.op("vector", lambda e: e.tensor_mul(
                    nmax[0:16, 2:3], nmax[0:16, 0:1], nmax[0:16, 1:2]),
                    deps=[b2], self_edge=True)
                b4 = pp.op("scalar", lambda e: e.activation(
                    nmax[0:16, 3:4], nmax[0:16, 2:3], AF.Sqrt), deps=[b3])
                pp.op("vector", lambda e: e.tensor_scalar(
                    negM16[0:16, 0:1], nmax[0:16, 3:4], -1.0, -2.0,
                    op0=ALU.mult, op1=ALU.add), deps=[b4], self_edge=True)
                pp.wait_only("gpsimd", [t_wv[NT - 1], t_wk[NT - 1], t_wq[NO - 1]])
                pp.emit()

        with contextlib.ExitStack() as stB:
            SB_ = lambda *a: stB.enter_context(nc.sbuf_tensor(*a))
            wobf = SB_("wobf", [128, KD * 2048], BF16)
            att = SB_("att", [128, H * 2048], BF16)

            with contextlib.ExitStack() as stp:
                SP = lambda *a: stp.enter_context(nc.sbuf_tensor(*a))
                PPs = lambda *a: stp.enter_context(nc.psum_tensor(*a))
                qsb = [SP(f"qsb_{p}", [128, 2048], BF16) for p in range(2)]
                ksb = [SP(f"ksb_{p}", [128, 4096], BF16) for p in range(2)]
                vsb = [SP(f"vsb_{p}", [128, 4096], BF16) for p in range(2)]
                ebuf = [SP(f"ebuf_{j}", [128, 1024], BF16) for j in range(2)]
                diag16 = SP("diag16", [32, 16], BF16)
                negMbA = SP("negMbA", [128, 16], F32)
                rrow = SP("rrow", [32, 512], BF16)
                rbc = SP("rbc", [128, 512], F32)
                psS = [PPs(f"psS_{j}", [128, 1024], F32) for j in range(2)]
                psAV = [PPs(f"psAV_{p}", [128, 512], F32) for p in range(2)]
                psD = [PPs(f"psD_{p}", [128, 512], F32) for p in range(2)]

                pp = mkpipe("p2")
                NCI = H * NCHUNK
                NG = NCI * NT
                NP = NG // 2
                t_ld = [None] * H
                t_sl = [None] * NG
                t_ep = [None] * NP
                t_avl = [None] * NG
                t_ln1 = [None] * NCI
                t_ln2 = [None] * NCI
                t_bcm = [None] * NCI
                t_rbce = [None] * NCI
                t_ev = [None] * NCI

                t_wo = pp.op("gpsimd", lambda e: e.dma_start(
                    out=wobf[:].rearrange("p (k c) -> p k c", k=KD),
                    in_=dap(wo_in, 0, [[D, 128], [128 * D, KD], [1, D]])), dma_n=1)

                def emit_load(hh):
                    p = hh % 2
                    deps = []
                    if hh >= 2:
                        deps = [t_sl[(hh - 1) * NCHUNK * NT - 1],
                                t_avl[(hh - 1) * NCHUNK * NT - 1]]

                    def f(e, hh=hh, p=p):
                        return [
                            e.dma_start(out=qsb[p][:],
                                        in_=dap(q_scr, hh * 2048 * 128,
                                                [[128, 2048], [1, 128]]),
                                        transpose=True),
                            e.dma_start(out=ksb[p][:, 0:2048],
                                        in_=dap(k_scr, hh * S * 128,
                                                [[128, 2048], [1, 128]]),
                                        transpose=True),
                            e.dma_start(out=ksb[p][:, 2048:4096],
                                        in_=dap(k_scr, hh * S * 128 + 2048 * 128,
                                                [[128, 2048], [1, 128]]),
                                        transpose=True),
                            e.dma_start(
                                out=vsb[p][:].rearrange("p (j c) -> p j c", j=NT),
                                in_=dap(v_scr, hh * S * 128,
                                        [[128, 128], [128 * 128, NT], [1, 128]]))]

                    t_ld[hh] = pp.op("sync", f, deps=deps, dma_n=4)

                def emit_bound():
                    d1 = pp.op("vector", lambda e: e.tensor_scalar(
                        diag16[0:16, :], identm[0:16, 0:16], negM16[0:16, 0:1],
                        None, op0=ALU.mult))
                    d2 = pp.op("tensor", lambda e: e.matmul(
                        psD[0][0:128, 0:16], onesrow[0:16, :], diag16[0:16, :],
                        start=True, stop=True), deps=[d1])
                    return pp.op("scalar", lambda e: e.copy(
                        negMbA[:], psD[0][0:128, 0:16]), deps=[d2])

                t_bnd = emit_bound()

                def emit_S(hh, cc, j):
                    p = hh % 2
                    g = (hh * NCHUNK + cc) * NT + j
                    pr = g // 2
                    t_sl[g] = pp.op(
                        "tensor", lambda e, p=p, cc=cc, j=j, g=g, pr=pr: e.matmul(
                            psS[pr % 2][:, (g % 2) * 512:(g % 2 + 1) * 512],
                            ksb[p][:, j * 128:(j + 1) * 128],
                            qsb[p][:, cc * 512:(cc + 1) * 512],
                            start=True, stop=True),
                        deps=[t_ld[hh], t_ep[pr - 2] if pr >= 2 else None])
                    if g % 2 == 1:
                        t_ep[pr] = pp.op(
                            "scalar", lambda e, hh=hh, pr=pr: e.activation(
                                ebuf[pr % 2][:], psS[pr % 2][:], AF.Exp,
                                bias=negMbA[:, hh:hh + 1], scale=1.0),
                            deps=[t_sl[g], t_bnd,
                                  t_avl[g - 3] if g >= 3 else None])

                def emit_AV(hh, cc, j):
                    p = hh % 2
                    ci = hh * NCHUNK + cc
                    g = ci * NT + j
                    pr = g // 2
                    deps_av = [t_ep[pr]]
                    if j == 0 and ci >= 2:
                        deps_av += [t_ev[ci - 2], t_rbce[ci - 2]]
                    eb = lambda: ebuf[pr % 2][:, (g % 2) * 512:(g % 2 + 1) * 512]
                    t_avl[g] = pp.op(
                        "tensor", lambda e, p=p, cc=cc, j=j, pr=pr, g=g: (
                            e.matmul(psAV[cc % 2][:],
                                     vsb[p][:, j * 128:(j + 1) * 128],
                                     ebuf[pr % 2][:, (g % 2) * 512:(g % 2 + 1) * 512],
                                     start=(j == 0), stop=(j == NT - 1)),
                            e.matmul(psD[cc % 2][0:1, :], onesK[:, 0:1],
                                     ebuf[pr % 2][:, (g % 2) * 512:(g % 2 + 1) * 512],
                                     start=(j == 0), stop=(j == NT - 1)))[-1],
                        deps=deps_av)

                def emit_tail(ci):
                    hh, cc = divmod(ci, NCHUNK)
                    t_ln1[ci] = pp.op(
                        "scalar", lambda e, cc=cc: e.activation(
                            rrow[0:1, :], psD[cc % 2][0:1, :], AF.Ln,
                            bias=cval[0:1, 0:1]),
                        deps=[t_avl[ci * NT + NT - 1]])
                    t_ln2[ci] = pp.op(
                        "scalar", lambda e: e.activation(
                            rrow[0:1, :], rrow[0:1, :], AF.Exp,
                            scale=cval[0:1, 2:3]),
                        deps=[t_ln1[ci]], self_edge=True)
                    t_bcm[ci] = pp.op(
                        "tensor", lambda e, cc=cc: e.matmul(
                            psD[cc % 2][:], onesrow[0:1, :], rrow[0:1, :],
                            start=True, stop=True), deps=[t_ln2[ci]])
                    t_rbce[ci] = pp.op(
                        "scalar", lambda e, cc=cc: e.copy(rbc[:], psD[cc % 2][:]),
                        deps=[t_bcm[ci], t_ev[ci - 1] if ci >= 1 else None])
                    t_ev[ci] = pp.op(
                        "vector", lambda e, hh=hh, cc=cc: e.tensor_mul(
                            att[:, hh * 2048 + cc * 512: hh * 2048 + (cc + 1) * 512],
                            psAV[cc % 2][:], rbc[:]),
                        deps=[t_rbce[ci], t_avl[ci * NT + NT - 1]])

                def emit_chunk(hh, cc):
                    ci = hh * NCHUNK + cc
                    emit_S(hh, cc, 0)
                    emit_S(hh, cc, 1)
                    for jp in range(1, NT // 2):
                        emit_S(hh, cc, 2 * jp)
                        emit_S(hh, cc, 2 * jp + 1)
                        emit_AV(hh, cc, 2 * jp - 2)
                        emit_AV(hh, cc, 2 * jp - 1)
                        if jp == 2 and ci >= 1 and t_ln1[ci - 1] is None:
                            emit_tail(ci - 1)
                    emit_AV(hh, cc, NT - 2)
                    emit_AV(hh, cc, NT - 1)

                emit_load(0)
                for hh in range(H):
                    if hh + 1 < H:
                        emit_load(hh + 1)
                    for cc in range(NCHUNK):
                        emit_chunk(hh, cc)
                emit_tail(H * NCHUNK - 1)
                pp.wait_only("sync", [t_ld[H - 1]])
                pp.wait_only("gpsimd", [t_wo])
                pp.emit()

            with contextlib.ExitStack() as stp:
                SP = lambda *a: stp.enter_context(nc.sbuf_tensor(*a))
                PPs = lambda *a: stp.enter_context(nc.psum_tensor(*a))
                xres = [SP(f"xres_{p}", [128, 512], F32) for p in range(4)]
                ufull = SP("ufull", [128, KD * 512], F32)
                ubx = [SP(f"ubx_{p}", [128, 512], BF16) for p in range(2)]
                usx = [SP(f"usx_{p}", [128, 512], BF16) for p in range(2)]
                srow = SP("srow", [32, 2048], F32)
                mrow = SP("mrow", [32, 512], BF16)
                rrw = SP("rrw", [32, 512], BF16)
                mbc = SP("mbc", [128, 512], BF16)
                rbc3 = SP("rbc3", [128, 512], BF16)
                y1c = SP("y1c", [128, KD * 512], BF16)
                scrs = SP("scrs", [128, 32], F32)
                ttb = [SP(f"ttb_{p}", [128, 8], F32) for p in range(2)]
                cge = [SP("cge_0", [128, 64], F32)] * 2
                cgt = [SP("cgt_0", [128, 64], F32)] * 2
                rka = [SP(f"rka_{p}", [128, 16], F32) for p in range(2)]
                mskb = [SP(f"mskb_{p}", [128, 8], BF16) for p in range(2)]
                psW = [PPs(f"psW_{p}", [128, 512], F32) for p in range(2)]
                psMV = PPs("psMV", [32, 1024], F32)
                psBC = PPs("psBC", [128, 512], F32)
                psG = [PPs(f"psG_{p}", [128, 8], F32) for p in range(2)]
                psMT = PPs("psMT", [32, 512], BF16)

                pp = mkpipe("p3")
                if debug:
                    pp.op("gpsimd", lambda e: e.dma_start(
                        out=dap(dbg["att"], 0, [[H * 2048, 128], [1, H * 2048]]),
                        in_=att[:]), dma_n=1)
                t_x3 = [None] * (NCHUNK * KD)
                t_wom = [None] * (NCHUNK * KD)
                t_u = [None] * (NCHUNK * KD)
                t_ub = [None] * (NCHUNK * KD)
                t_stk = [None] * (NCHUNK * KD)
                stk_fns = [None] * (NCHUNK * KD)
                t_st = [None] * NCHUNK
                t_lnv = [None] * NCHUNK
                t_rstd = [None] * NCHUNK
                t_bme = [None] * NCHUNK
                t_brme = [None] * NCHUNK
                t_y1 = [None] * (NCHUNK * KD)
                t_wr1 = [None] * NCHUNK
                t_g = [None] * 16
                t_sg = [None] * 16
                t_rk = [None] * 16
                t_ms1 = [None] * 16
                t_ms2 = [None] * 16
                t_ms3 = [None] * 16
                t_mt = [None] * 16
                t_mte = [None] * 16

                for cc in range(NCHUNK):
                    for dt in range(KD):
                        idx = cc * KD + dt
                        t_x3[idx] = pp.op(
                            "sync", lambda e, cc=cc, dt=dt, idx=idx: e.dma_start(
                                out=xres[idx % 4][:],
                                in_=dap(xT, dt * 128 * S + cc * 512,
                                        [[S, 128], [1, 512]])),
                            deps=[prev_tok(t_u, idx, 4)], dma_n=1)

                        def wom(t, cc=cc, dt=dt):
                            last = None
                            for k in range(KD):
                                last = t.matmul(
                                    psW[(cc * KD + dt) % 2][:],
                                    wobf[:, k * 2048 + dt * 128:
                                         k * 2048 + (dt + 1) * 128],
                                    att[:, k * 2048 + cc * 512:
                                        k * 2048 + (cc + 1) * 512],
                                    start=(k == 0), stop=(k == KD - 1))
                            return last

                        t_wom[idx] = pp.op("tensor", wom,
                                           deps=[prev_tok(t_u, idx, 2)])
                        t_u[idx] = pp.op(
                            "vector", lambda e, dt=dt, idx=idx: e.tensor_add(
                                ufull[:, dt * 512:(dt + 1) * 512],
                                psW[idx % 2][:], xres[idx % 4][:]),
                            deps=[t_wom[idx], t_x3[idx],
                                  prev_tok(t_y1, idx, KD)])

                        def stk_mm(t, dt=dt, cc=cc):
                            t.matmul(psMV[0:1, 0:512], onesK[:, 0:1],
                                     ubx[dt % 2][:],
                                     start=(dt == 0), stop=(dt == KD - 1))
                            return t.matmul(psMV[0:1, 512:1024], onesK[:, 0:1],
                                            usx[dt % 2][:],
                                            start=(dt == 0), stop=(dt == KD - 1))

                        stk_fns[idx] = stk_mm
                        if dt >= 2:
                            t_stk[idx - 2] = pp.op(
                                "tensor", stk_fns[idx - 2],
                                deps=[t_ub[idx - 2],
                                      t_lnv[cc - 1] if (dt == 2 and cc >= 1)
                                      else None])
                        t_ub[idx] = pp.op(
                            "scalar", lambda e, dt=dt: (
                                e.copy(ubx[dt % 2][:],
                                       ufull[:, dt * 512:(dt + 1) * 512]),
                                e.activation(usx[dt % 2][:],
                                             ufull[:, dt * 512:(dt + 1) * 512],
                                             AF.Square))[-1],
                            deps=[t_u[idx],
                                  t_stk[cc * KD + dt - 2]
                                  if cc * KD + dt >= 2 else None])
                    for dt in (KD - 2, KD - 1):
                        t_stk[cc * KD + dt] = pp.op(
                            "tensor", stk_fns[cc * KD + dt],
                            deps=[t_ub[cc * KD + dt]])
                    t_st[cc] = t_stk[cc * KD + KD - 1]

                    a1 = pp.op("vector", lambda e: e.tensor_scalar_mul(
                        srow[0:1, 0:512], psMV[0:1, 0:512], 1.0 / D),
                        deps=[t_st[cc]])
                    a2 = pp.op("vector", lambda e: e.tensor_scalar_mul(
                        srow[0:1, 512:1024], psMV[0:1, 512:1024], 1.0 / D),
                        deps=[t_st[cc]])
                    a3 = pp.op("vector", lambda e: e.tensor_mul(
                        srow[0:1, 1024:1536], srow[0:1, 0:512], srow[0:1, 0:512]),
                        deps=[a1], self_edge=True)
                    a4 = pp.op("vector", lambda e: e.tensor_sub(
                        srow[0:1, 1536:2048], srow[0:1, 512:1024],
                        srow[0:1, 1024:1536]), deps=[a2, a3], self_edge=True)
                    a5 = pp.op("vector", lambda e: e.tensor_copy(
                        mrow[0:1, :], srow[0:1, 0:512]), deps=[a1], self_edge=True)
                    t_lnv[cc] = a5
                    r1 = pp.op("scalar", lambda e: e.activation(
                        srow[0:1, 1024:1536], srow[0:1, 1536:2048], AF.Ln,
                        bias=cval[0:1, 1:2]), deps=[a4])
                    t_rstd[cc] = pp.op("scalar", lambda e: e.activation(
                        rrw[0:1, :], srow[0:1, 1024:1536], AF.Exp, scale=cval[0:1, 3:4]),
                        deps=[r1], self_edge=True)

                    bm = pp.op("tensor", lambda e: e.matmul(
                        psBC[:], onesrow[0:1, :], mrow[0:1, :],
                        start=True, stop=True),
                        deps=[a5, t_brme[cc - 1] if cc >= 1 else None])
                    t_bme[cc] = pp.op("scalar", lambda e: e.copy(mbc[:], psBC[:]),
                                      deps=[bm])
                    brm = pp.op("tensor", lambda e: e.matmul(
                        psBC[:], onesrow[0:1, :], rrw[0:1, :],
                        start=True, stop=True),
                        deps=[t_bme[cc], t_rstd[cc]])
                    t_brme[cc] = pp.op("scalar", lambda e: e.copy(rbc3[:], psBC[:]),
                                       deps=[brm])

                    for dt in range(KD):
                        idx = cc * KD + dt
                        t_y1[idx] = pp.op(
                            "vector", lambda e, dt=dt: (
                                e.tensor_sub(ufull[:, dt * 512:(dt + 1) * 512],
                                             ufull[:, dt * 512:(dt + 1) * 512],
                                             mbc[:]),
                                e.tensor_mul(y1c[:, dt * 512:(dt + 1) * 512],
                                             ufull[:, dt * 512:(dt + 1) * 512],
                                             rbc3[:]))[-1],
                            deps=[t_bme[cc], t_brme[cc], t_ub[idx],
                                  t_wr1[cc - 1] if cc >= 1 else None,
                                  t_g[(cc - 1) * 4 + 3] if cc >= 1 else None])

                    t_wr1[cc] = pp.op("gpsimd", lambda e, cc=cc: e.dma_start(
                        out=dap(y1_scr, cc * 512,
                                [[2048, 128], [128 * 2048, KD], [1, 512]]),
                        in_=y1c[:].rearrange("p (k c) -> p k c", k=KD)),
                        deps=[t_y1[cc * KD + k] for k in range(KD)], dma_n=1)

                    for st4 in range(4):
                        gi = cc * 4 + st4
                        p = gi % 2

                        def gmm(t, cc=cc, st4=st4, p=p):
                            last = None
                            for k in range(KD):
                                last = t.matmul(
                                    psG[p][:],
                                    y1c[:, k * 512 + st4 * 128:
                                        k * 512 + (st4 + 1) * 128],
                                    gbf[:, k * 8:(k + 1) * 8],
                                    start=(k == 0), stop=(k == KD - 1))
                            return last

                        t_g[gi] = pp.op(
                            "tensor", gmm,
                            deps=[t_y1[cc * KD + k] for k in range(KD)] +
                                 [prev_tok(t_sg, gi, 2)])
                        t_sg[gi] = pp.op(
                            "scalar", lambda e, st4=st4, p=p: e.activation(
                                scrs[:, st4 * 8:(st4 + 1) * 8], psG[p][:],
                                AF.Sigmoid),
                            deps=[t_g[gi], prev_tok(t_ms3, gi, 4)])

                        def rank(v, st4=st4, p=p):
                            sl = scrs[:, st4 * 8:(st4 + 1) * 8]
                            v.tensor_add(ttb[p][:], sl, bias_bc[:])
                            v.memset(rka[p][:], 0.0)
                            for e2 in range(NE):
                                v.tensor_scalar(cge[p][:, e2 * 8:(e2 + 1) * 8],
                                                ttb[p][:], ttb[p][:, e2:e2 + 1],
                                                None, op0=ALU.is_ge)
                                v.tensor_scalar(cgt[p][:, e2 * 8:(e2 + 1) * 8],
                                                ttb[p][:], ttb[p][:, e2:e2 + 1],
                                                None, op0=ALU.is_gt)
                            last = None
                            for e2 in range(NE):
                                if e2 > 0:
                                    last = v.reduce_sum(
                                        rka[p][:, e2:e2 + 1],
                                        cge[p][:, e2 * 8:e2 * 8 + e2], axis=AX.X)
                                if e2 < NE - 1:
                                    last = v.reduce_sum(
                                        rka[p][:, 8 + e2:9 + e2],
                                        cgt[p][:, e2 * 8 + e2 + 1:(e2 + 1) * 8],
                                        axis=AX.X)
                            return last

                        t_rk[gi] = pp.op("vector", rank,
                                         deps=[t_sg[gi], prev_tok(t_mt, gi, 2)],
                                         self_edge=True)
                        t_ms1[gi] = pp.op("vector", lambda v, p=p: v.tensor_add(
                            rka[p][:, 0:8], rka[p][:, 0:8], rka[p][:, 8:16]),
                            deps=[t_rk[gi]], self_edge=True)
                        t_ms2[gi] = pp.op("vector", lambda v, p=p: v.tensor_scalar(
                            ttb[p][:], rka[p][:, 0:8], 2.0, None, op0=ALU.is_lt),
                            deps=[t_ms1[gi]], self_edge=True)
                        t_ms3[gi] = pp.op("vector", lambda v, p=p, st4=st4:
                                          v.tensor_mul(
                            mskb[p][:], ttb[p][:], scrs[:, st4 * 8:(st4 + 1) * 8]),
                            deps=[t_ms2[gi]], self_edge=True)
                        t_mt[gi] = pp.op(
                            "tensor", lambda e, p=p, st4=st4: e.transpose(
                                psMT[0:8, st4 * 128:(st4 + 1) * 128],
                                mskb[p][:], identm[:]),
                            deps=[t_ms3[gi], prev_tok(t_mte, gi, 1)])
                        t_mte[gi] = pp.op(
                            "scalar", lambda e, cc=cc, st4=st4: e.copy(
                                m_feat[0:8, cc * 512 + st4 * 128:
                                       cc * 512 + (st4 + 1) * 128],
                                psMT[0:8, st4 * 128:(st4 + 1) * 128]),
                            deps=[t_mt[gi]])

                pp.wait_only("gpsimd", [t_wr1[NCHUNK - 1]])
                pp.wait_only("sync", [t_x3[NCHUNK * KD - 1]])
                pp.emit()

        with contextlib.ExitStack() as stp:
            SP = lambda *a: stp.enter_context(nc.sbuf_tensor(*a))
            PPs = lambda *a: stp.enter_context(nc.psum_tensor(*a))
            w1b = [SP(f"w1b_{p}", [128, KD * HID], BF16) for p in range(2)]
            w3b = [SP(f"w3b_{p}", [128, KD * HID], BF16) for p in range(2)]
            y1res = SP("y1res", [128, KD * 2048], BF16)
            abuf = [SP(f"abuf_{p}", [128, 512], BF16) for p in range(2)]
            bbuf = [SP(f"bbuf_{p}", [128, 512], BF16) for p in range(2)]
            sabuf = [SP("sabuf_0", [128, 512], BF16)] * 2
            hbuf = [SP(f"hbuf_{p}", [128, 512], BF16) for p in range(2)]
            mbc4 = SP("mbc4", [128, 512], BF16)
            psU1 = [PPs(f"psU1_{p}", [128, 512], F32) for p in range(2)]
            psU3 = [PPs(f"psU3_{p}", [128, 512], F32) for p in range(2)]
            psMB = PPs("psMB", [128, 512], F32)

            pp = mkpipe("p4")
            if debug:
                pp.op("gpsimd", lambda e: e.dma_start(
                    out=dap(dbg["mf"], 0, [[2048, 8], [1, 2048]]),
                    in_=m_feat[0:8, :]), dma_n=1)
            NHS = 9 * NCHUNK * KH
            t_w = [None] * 9
            t_yl = pp.op("sync", lambda e: e.dma_start(
                out=y1res[:].rearrange("p (k c) -> p k c", k=KD),
                in_=dap(y1_scr, 0,
                        [[2048, 128], [128 * 2048, KD], [1, 2048]])), dma_n=1)
            t_mbe = [None] * (9 * NCHUNK)
            t_u1 = [None] * NHS
            t_u3 = [None] * NHS
            t_a = [None] * NHS
            t_b = [None] * NHS
            t_sa = [None] * NHS
            t_h = [None] * NHS
            t_hw = [None] * NHS

            def emit_wload(s):
                p = s % 2
                deps = [t_u3[(s - 1) * NCHUNK * KH - 1]] if s >= 2 else []
                if s >= 2:
                    assert deps[0] is not None
                off = 0 if s == 0 else (s - 1) * D * HID

                def f(e, s=s, off=off, p=p):
                    return [
                        e.dma_start(
                            out=w1b[p][:].rearrange("p (k c) -> p k c", k=KD),
                            in_=dap(sw1T if s == 0 else w1T, off,
                                    [[HID, 128], [128 * HID, KD], [1, HID]])),
                        e.dma_start(
                            out=w3b[p][:].rearrange("p (k c) -> p k c", k=KD),
                            in_=dap(sw3T if s == 0 else w3T, off,
                                    [[HID, 128], [128 * HID, KD], [1, HID]]))]

                t_w[s] = pp.op("gpsimd", f, deps=deps, dma_n=2)

            emit_wload(0)
            emit_wload(1)
            for s in range(9):
                p = s % 2
                for cc in range(NCHUNK):
                    sci = s * NCHUNK + cc
                    if s > 0:
                        mb = pp.op("tensor", lambda e, s=s, cc=cc: e.matmul(
                            psMB[:], rsel8[0:8, (s - 1) * 128: s * 128],
                            m_feat[0:8, cc * 512:(cc + 1) * 512],
                            start=True, stop=True),
                            deps=[prev_tok(t_mbe, sci, 1)])
                        t_mbe[sci] = pp.op("scalar", lambda e: e.copy(
                            mbc4[:], psMB[:]),
                            deps=[mb, t_b[sci * KH - 1] if sci >= 1 else None])
                    for ht in range(KH):
                        hs = sci * KH + ht
                        hp = hs % 2

                        def u1(t, p=p, cc=cc, ht=ht, hp=hp):
                            last = None
                            for k in range(KD):
                                last = t.matmul(
                                    psU1[hp][:],
                                    w1b[p][:, k * HID + ht * 128:
                                           k * HID + (ht + 1) * 128],
                                    y1res[:, k * 2048 + cc * 512:
                                          k * 2048 + (cc + 1) * 512],
                                    start=(k == 0), stop=(k == KD - 1))
                            return last

                        def u3(t, p=p, cc=cc, ht=ht, hp=hp):
                            last = None
                            for k in range(KD):
                                last = t.matmul(
                                    psU3[hp][:],
                                    w3b[p][:, k * HID + ht * 128:
                                           k * HID + (ht + 1) * 128],
                                    y1res[:, k * 2048 + cc * 512:
                                          k * 2048 + (cc + 1) * 512],
                                    start=(k == 0), stop=(k == KD - 1))
                            return last

                        t_u1[hs] = pp.op("tensor", u1,
                                         deps=[t_w[s], t_yl,
                                               prev_tok(t_a, hs, 2)])
                        t_u3[hs] = pp.op("tensor", u3,
                                         deps=[prev_tok(t_b, hs, 2)])

                        if s == 0:
                            t_sa[hs] = pp.op(
                                "scalar", lambda e, hp=hp: e.activation(
                                    sabuf[hp][:], psU1[hp][:], AF.Silu),
                                deps=[t_u1[hs], prev_tok(t_h, hs, 1)])
                            t_a[hs] = t_sa[hs]
                            t_b[hs] = pp.op(
                                "vector", lambda e, hp=hp: e.tensor_copy(
                                    bbuf[hp][:], psU3[hp][:]),
                                deps=[t_u3[hs], prev_tok(t_h, hs, 2)])
                        else:
                            t_a[hs] = pp.op(
                                "vector", lambda e, hp=hp: e.tensor_mul(
                                    abuf[hp][:], psU1[hp][:], mbc4[:]),
                                deps=[t_u1[hs], t_mbe[sci],
                                      prev_tok(t_sa, hs, 2)])
                            t_b[hs] = pp.op(
                                "vector", lambda e, hp=hp: e.tensor_mul(
                                    bbuf[hp][:], psU3[hp][:], mbc4[:]),
                                deps=[t_u3[hs], t_mbe[sci],
                                      prev_tok(t_h, hs, 2)])
                            t_sa[hs] = pp.op(
                                "scalar", lambda e, hp=hp: e.activation(
                                    sabuf[hp][:], abuf[hp][:], AF.Silu),
                                deps=[t_a[hs], prev_tok(t_h, hs, 1)])
                        t_h[hs] = pp.op(
                            "vector", lambda e, hp=hp: e.tensor_mul(
                                hbuf[hp][:], sabuf[hp][:], bbuf[hp][:]),
                            deps=[t_sa[hs], t_b[hs], prev_tok(t_hw, hs, 2)])
                        t_hw[hs] = pp.op(
                            "scalar", lambda e, s=s, cc=cc, ht=ht, hp=hp:
                            e.dma_start(
                                out=dap(h_scr,
                                        s * HID * 2048 + ht * 128 * 2048 + cc * 512,
                                        [[2048, 128], [1, 512]]),
                                in_=hbuf[hp][:]),
                            deps=[t_h[hs]], dma_n=1)
                if s + 2 <= 8:
                    emit_wload(s + 2)

            pp.wait_only("scalar", [t_hw[NHS - 1]])
            pp.wait_only("gpsimd", [t_w[8]])
            pp.wait_only("sync", [t_yl])
            pp.emit()

        with contextlib.ExitStack() as stB:
            SB5 = lambda *a: stB.enter_context(nc.sbuf_tensor(*a))
            ffnacc = SB5("ffnacc", [128, KD * 2048], BF16)

            with contextlib.ExitStack() as stp:
                SP = lambda *a: stp.enter_context(nc.sbuf_tensor(*a))
                PPs = lambda *a: stp.enter_context(nc.psum_tensor(*a))
                w2b = [SP(f"w2b_{p}", [128, KH * D], BF16) for p in range(2)]
                hcb = [SP(f"hcb_{p}", [128, KH * 512], BF16) for p in range(2)]
                psW2 = [PPs(f"psW2_{p}", [128, 512], F32) for p in range(2)]

                pp = mkpipe("p5a")
                NDS = 9 * NCHUNK * KD
                t_w2 = [None] * 9
                t_hc = [None] * (9 * NCHUNK)
                t_m2 = [None] * NDS
                t_ac = [None] * NDS

                def emit_w2(s):
                    p = s % 2
                    deps = [t_m2[(s - 1) * NCHUNK * KD - 1]] if s >= 2 else []
                    if s >= 2:
                        assert deps[0] is not None
                    off = 0 if s == 0 else (s - 1) * HID * D
                    t_w2[s] = pp.op("gpsimd", lambda e, s=s, off=off, p=p:
                                    e.dma_start(
                        out=w2b[p][:].rearrange("p (k c) -> p k c", k=KH),
                        in_=dap(sw2T if s == 0 else w2T, off,
                                [[D, 128], [128 * D, KH], [1, D]])),
                        deps=deps, dma_n=1)

                emit_w2(0)
                emit_w2(1)
                for s in range(9):
                    p = s % 2
                    for cc in range(NCHUNK):
                        sci = s * NCHUNK + cc
                        hp = sci % 2
                        t_hc[sci] = pp.op(
                            "sync", lambda e, s=s, cc=cc, hp=hp: e.dma_start(
                                out=hcb[hp][:].rearrange("p (k c) -> p k c", k=KH),
                                in_=dap(h_scr, s * HID * 2048 + cc * 512,
                                        [[2048, 128], [128 * 2048, KH], [1, 512]])),
                            deps=[t_m2[(sci - 1) * KD - 1] if sci >= 2 else None],
                            dma_n=1)
                        for dt in range(KD):
                            ds_ = sci * KD + dt
                            wp = ds_ % 2

                            def w2mm(t, cc=cc, dt=dt, hp=hp, wp=wp, p=p):
                                last = None
                                for k in range(KH):
                                    last = t.matmul(
                                        psW2[wp][:],
                                        w2b[p][:, k * D + dt * 128:
                                               k * D + (dt + 1) * 128],
                                        hcb[hp][:, k * 512:(k + 1) * 512],
                                        start=(k == 0), stop=(k == KH - 1))
                                return last

                            t_m2[ds_] = pp.op(
                                "tensor", w2mm,
                                deps=[t_w2[s], t_hc[sci], prev_tok(t_ac, ds_, 2)])

                            def acc(v, s=s, cc=cc, dt=dt, wp=wp):
                                sl = ffnacc[:, dt * 2048 + cc * 512:
                                            dt * 2048 + (cc + 1) * 512]
                                if s == 0:
                                    return v.tensor_copy(sl, psW2[wp][:])
                                return v.tensor_add(sl, psW2[wp][:], sl)

                            t_ac[ds_] = pp.op("vector", acc, deps=[t_m2[ds_]])
                    if s + 2 <= 8:
                        emit_w2(s + 2)

                pp.wait_only("sync", [t_hc[9 * NCHUNK - 1]])
                pp.wait_only("gpsimd", [t_w2[8]])
                pp.emit()

            with contextlib.ExitStack() as stp:
                SP = lambda *a: stp.enter_context(nc.sbuf_tensor(*a))
                PPs = lambda *a: stp.enter_context(nc.psum_tensor(*a))
                y1f = [SP("y1f_0", [128, KD * 512], BF16)]
                ufull = SP("ufull5", [128, KD * 512], F32)
                ubx5 = [SP(f"ubx5_{p}", [128, 512], BF16) for p in range(2)]
                usx5 = [SP(f"usx5_{p}", [128, 512], BF16) for p in range(2)]
                srow = SP("srow5", [32, 2048], F32)
                mrow = SP("mrow5", [32, 512], BF16)
                rrw = SP("rrw5", [32, 512], BF16)
                mbc = SP("mbc5", [128, 512], BF16)
                rbc5 = SP("rbc5", [128, 512], BF16)
                y2c = SP("y2c", [128, KD * 512], F32)
                psMV = PPs("psMV5", [32, 1024], F32)
                psBC = PPs("psBC5", [128, 512], F32)

                def dap_y2c():
                    return y2c[:].rearrange("p (k c) -> p k c", k=KD)

                pp = mkpipe("p5b")
                if debug:
                    pp.op("gpsimd", lambda e: e.dma_start(
                        out=dap(dbg["ffn"], 0, [[KD * 2048, 128], [1, KD * 2048]]),
                        in_=ffnacc[:]), dma_n=1)
                    for nm, scr, sz in (("q", q_scr, H * 128 * 2048),
                                        ("k", k_scr, H * 128 * S),
                                        ("v", v_scr, H * S * 128),
                                        ("y1", y1_scr, D * 2048),
                                        ("h", h_scr, 9 * HID * 2048)):
                        pp.op("gpsimd", lambda e, nm=nm, scr=scr, sz=sz:
                              e.dma_start(out=dap(dbg[nm], 0, [[1, sz]]),
                                          in_=dap(scr, 0, [[1, sz]])),
                              dma_n=1)
                t_y = [None] * NCHUNK
                t_u = [None] * (NCHUNK * KD)
                t_ub = [None] * (NCHUNK * KD)
                t_stk = [None] * (NCHUNK * KD)
                stk_fns = [None] * (NCHUNK * KD)
                t_st = [None] * NCHUNK
                t_lnv = [None] * NCHUNK
                t_rstd = [None] * NCHUNK
                t_bme = [None] * NCHUNK
                t_brme = [None] * NCHUNK
                t_y2 = [None] * (NCHUNK * KD)
                t_wr = [None] * NCHUNK

                for cc in range(NCHUNK):
                    t_y[cc] = pp.op("sync", lambda e, cc=cc: e.dma_start(
                        out=y1f[0][:].rearrange("p (k c) -> p k c", k=KD),
                        in_=dap(y1_scr, cc * 512,
                                [[2048, 128], [128 * 2048, KD], [1, 512]])),
                        deps=[t_u[cc * KD - 1] if cc >= 1 else None], dma_n=1)
                    for dt in range(KD):
                        idx = cc * KD + dt
                        t_u[idx] = pp.op(
                            "vector", lambda e, cc=cc, dt=dt: e.tensor_add(
                                ufull[:, dt * 512:(dt + 1) * 512],
                                ffnacc[:, dt * 2048 + cc * 512:
                                       dt * 2048 + (cc + 1) * 512],
                                y1f[0][:, dt * 512:(dt + 1) * 512]),
                            deps=[t_y[cc], prev_tok(t_y2, idx, KD)])

                        def stk_mm(t, dt=dt):
                            t.matmul(psMV[0:1, 0:512], onesK[:, 0:1],
                                     ubx5[dt % 2][:],
                                     start=(dt == 0), stop=(dt == KD - 1))
                            return t.matmul(psMV[0:1, 512:1024], onesK[:, 0:1],
                                            usx5[dt % 2][:],
                                            start=(dt == 0), stop=(dt == KD - 1))

                        stk_fns[idx] = stk_mm
                        if dt >= 2:
                            t_stk[idx - 2] = pp.op(
                                "tensor", stk_fns[idx - 2],
                                deps=[t_ub[idx - 2],
                                      t_lnv[cc - 1] if (dt == 2 and cc >= 1)
                                      else None])
                        t_ub[idx] = pp.op(
                            "scalar", lambda e, dt=dt: (
                                e.copy(ubx5[dt % 2][:],
                                       ufull[:, dt * 512:(dt + 1) * 512]),
                                e.activation(usx5[dt % 2][:],
                                             ufull[:, dt * 512:(dt + 1) * 512],
                                             AF.Square))[-1],
                            deps=[t_u[idx],
                                  t_stk[idx - 2] if idx >= 2 else None])
                    for dt in (KD - 2, KD - 1):
                        t_stk[cc * KD + dt] = pp.op(
                            "tensor", stk_fns[cc * KD + dt],
                            deps=[t_ub[cc * KD + dt]])
                    t_st[cc] = t_stk[cc * KD + KD - 1]

                    a1 = pp.op("vector", lambda e: e.tensor_scalar_mul(
                        srow[0:1, 0:512], psMV[0:1, 0:512], 1.0 / D),
                        deps=[t_st[cc]])
                    a2 = pp.op("vector", lambda e: e.tensor_scalar_mul(
                        srow[0:1, 512:1024], psMV[0:1, 512:1024], 1.0 / D),
                        deps=[t_st[cc]])
                    a3 = pp.op("vector", lambda e: e.tensor_mul(
                        srow[0:1, 1024:1536], srow[0:1, 0:512], srow[0:1, 0:512]),
                        deps=[a1], self_edge=True)
                    a4 = pp.op("vector", lambda e: e.tensor_sub(
                        srow[0:1, 1536:2048], srow[0:1, 512:1024],
                        srow[0:1, 1024:1536]), deps=[a2, a3], self_edge=True)
                    a5 = pp.op("vector", lambda e: e.tensor_copy(
                        mrow[0:1, :], srow[0:1, 0:512]), deps=[a1], self_edge=True)
                    t_lnv[cc] = a5
                    r1 = pp.op("scalar", lambda e: e.activation(
                        srow[0:1, 1024:1536], srow[0:1, 1536:2048], AF.Ln,
                        bias=cval[0:1, 1:2]), deps=[a4])
                    t_rstd[cc] = pp.op("scalar", lambda e: e.activation(
                        rrw[0:1, :], srow[0:1, 1024:1536],
                        AF.Exp, scale=cval[0:1, 3:4]),
                        deps=[r1], self_edge=True)
                    bm = pp.op("tensor", lambda e: e.matmul(
                        psBC[:], onesrow[0:1, :], mrow[0:1, :],
                        start=True, stop=True),
                        deps=[a5, t_brme[cc - 1] if cc >= 1 else None])
                    t_bme[cc] = pp.op("scalar", lambda e: e.copy(mbc[:], psBC[:]),
                                      deps=[bm])
                    brm = pp.op("tensor", lambda e: e.matmul(
                        psBC[:], onesrow[0:1, :], rrw[0:1, :],
                        start=True, stop=True),
                        deps=[t_bme[cc], t_rstd[cc]])
                    t_brme[cc] = pp.op("scalar", lambda e: e.copy(rbc5[:], psBC[:]),
                                       deps=[brm])

                    for dt in range(KD):
                        idx = cc * KD + dt
                        t_y2[idx] = pp.op(
                            "vector", lambda e, dt=dt: (
                                e.tensor_sub(ufull[:, dt * 512:(dt + 1) * 512],
                                             ufull[:, dt * 512:(dt + 1) * 512],
                                             mbc[:]),
                                e.tensor_mul(y2c[:, dt * 512:(dt + 1) * 512],
                                             ufull[:, dt * 512:(dt + 1) * 512],
                                             rbc5[:]))[-1],
                            deps=[t_bme[cc], t_brme[cc], t_ub[idx],
                                  t_wr[cc - 1] if cc >= 1 else None])
                    t_wr[cc] = pp.op("gpsimd", lambda e, cc=cc: e.dma_start(
                        out=dap(y2_out, cc * 512,
                                [[2048, 128], [128 * 2048, KD], [1, 512]]),
                        in_=dap_y2c(),
                    ), deps=[t_y2[cc * KD + k] for k in range(KD)], dma_n=1)

                pp.wait_only("gpsimd", [t_wr[NCHUNK - 1]])
                pp.wait_only("sync", [t_y[NCHUNK - 1]])
                pp.emit()

    return nc


_NC_CACHE = None


def kernel(L0_attn_x, L0_attn_complex_freq, L0_attn_W_Aq, L0_attn_W_Ak,
           L0_attn_W_Av, L0_attn_W_Bq, L0_attn_W_Bk, L0_attn_W_Bv, L0_attn_Wo,
           L0_ffn_gate, L0_ffn_expert_bias, L0_ffn_w1, L0_ffn_w2, L0_ffn_w3,
           L0_ffn_shared_w1, L0_ffn_shared_w2, L0_ffn_shared_w3):
    global _NC_CACHE
    x = np.asarray(L0_attn_x, dtype=np.float32)
    wproj = np.ascontiguousarray(np.concatenate(
        [np.asarray(L0_attn_W_Bq), np.asarray(L0_attn_W_Bk),
         np.asarray(L0_attn_W_Bv), np.asarray(L0_attn_W_Aq),
         np.asarray(L0_attn_W_Ak), np.asarray(L0_attn_W_Av)],
        axis=1).astype(np.float32)).reshape(-1)
    wo = np.ascontiguousarray(np.asarray(L0_attn_Wo, dtype=np.float32)).reshape(-1)
    fr = np.ascontiguousarray(np.asarray(
        L0_attn_complex_freq, dtype=np.float32)).reshape(-1)
    gT = np.ascontiguousarray(
        np.asarray(L0_ffn_gate, dtype=np.float32).T).reshape(-1)
    eb = np.ascontiguousarray(np.asarray(L0_ffn_expert_bias, dtype=np.float32))
    s1 = np.ascontiguousarray(
        np.asarray(L0_ffn_shared_w1, dtype=np.float32).T).reshape(-1)
    s3 = np.ascontiguousarray(
        np.asarray(L0_ffn_shared_w3, dtype=np.float32).T).reshape(-1)
    s2 = np.ascontiguousarray(
        np.asarray(L0_ffn_shared_w2, dtype=np.float32).T).reshape(-1)
    w1 = np.ascontiguousarray(
        np.asarray(L0_ffn_w1, dtype=np.float32).transpose(0, 2, 1)).reshape(-1)
    w3 = np.ascontiguousarray(
        np.asarray(L0_ffn_w3, dtype=np.float32).transpose(0, 2, 1)).reshape(-1)
    w2 = np.ascontiguousarray(
        np.asarray(L0_ffn_w2, dtype=np.float32).transpose(0, 2, 1)).reshape(-1)
    rsel = np.zeros((8, 8 * 128), dtype=np.float32)
    for e in range(8):
        rsel[e, e * 128:(e + 1) * 128] = 1.0
    rsel = rsel.reshape(-1)

    if _NC_CACHE is None:
        _NC_CACHE = build_program()
    nc = _NC_CACHE

    in_maps = []
    for c in range(8):
        b, h2 = c // 2, c % 2
        own = x[b, h2 * 2048:(h2 + 1) * 2048]
        other = x[b, (1 - h2) * 2048:(2 - h2) * 2048]
        xTc = np.ascontiguousarray(
            np.concatenate([own, other], axis=0).T).reshape(-1)
        in_maps.append({
            "xT": xTc, "wproj": wproj, "wo_in": wo, "freq": fr, "gateT": gT,
            "ebias": eb, "sw1T": s1, "sw3T": s3, "sw2T": s2,
            "w1T": w1, "w3T": w3, "w2T": w2, "rsel8_in": rsel,
        })

    res = run_bass_kernel_spmd(nc, in_maps, core_ids=list(range(8)))
    out = np.empty((B, S, D), dtype=np.float32)
    for c in range(8):
        b, h2 = c // 2, c % 2
        out[b, h2 * 2048:(h2 + 1) * 2048, :] = res.results[c]["y2"].T
    return out


# version 31 (final)
# speedup vs baseline: 1.0499x; 1.0310x over previous
import contextlib
import numpy as np
import concourse.bass as bass
import concourse.mybir as mybir
from concourse.bass_utils import run_bass_kernel_spmd

F32 = mybir.dt.float32
BF16 = mybir.dt.bfloat16
AF = mybir.ActivationFunctionType
ALU = mybir.AluOpType
AX = mybir.AxisListType

B, S, D, H, DH = 4, 4096, 2048, 16, 128
RQ, RK, RV = 6, 2, 2
HID = 1024
NE = 8
NT = S // 128
NO = 16
KD = D // 128
KH = HID // 128
NCHUNK = 4
Q_SCALE = 1.0 / (RQ * float(np.sqrt(DH)))

C_BQ, C_BK, C_BV, C_AQ, C_AK, C_AV = 0, 768, 1024, 1280, 1376, 1408
C_END = 1440

ENGINES = ("sync", "gpsimd", "vector", "scalar", "tensor")


class Pipe:

    DMA_SLOTS = {"sync": 4, "gpsimd": 4, "scalar": 2, "vector": 1, "tensor": 1}

    def __init__(self, nc, sems, cnt, dma_rot, tag, xsem=None, xbase=0):
        self.nc = nc
        self.tag = tag
        self.sem = sems
        self.cnt = cnt
        self.dma_rot = dma_rot
        self.xsem = xsem
        self.xbase = xbase
        self.start_cnt = dict(cnt)
        self.ops = {e: [] for e in ENGINES}
        self.hwm = {e: {k: cnt[k] for k in sems} for e in ENGINES}
        self.slot_cons = {}

    def op(self, eng, fn, deps=(), dma_n=0, self_edge=False):
        if dma_n:
            slot = self.dma_rot[eng] % self.DMA_SLOTS[eng]
            self.dma_rot[eng] += 1
            key = f"{eng}#d{slot}"
            cons = self.slot_cons.pop(key, {})
            deps = list(deps) + list(cons.values())
        else:
            key = eng + "#c"
        waits = []
        if dma_n and not cons and self.cnt[key] > self.start_cnt.get(key, 0):
            waits.append((key, self.cnt[key]))
            self.hwm[eng][key] = max(self.hwm[eng][key], self.cnt[key])
        if self_edge:
            for k in self.sem:
                if (k.split("#")[0] == eng and k != key
                        and self.cnt[k] > self.hwm[eng][k]):
                    waits.append((k, self.cnt[k]))
                    self.hwm[eng][k] = self.cnt[k]
            if self.cnt[key] > self.hwm[eng][key]:
                waits.append((key, self.cnt[key]))
                self.hwm[eng][key] = self.cnt[key]
        for d in deps:
            if d is None:
                continue
            pk, pc = d
            if pk == key or (pk.split("#")[0] == eng and pk.endswith("#c")
                             and not dma_n):
                continue
            if pc > self.hwm[eng][pk]:
                waits.append((pk, pc))
                self.hwm[eng][pk] = pc
        amt = 16 * dma_n if dma_n else 1
        self.ops[eng].append((fn, waits, amt, key))
        self.cnt[key] += amt
        tok = (key, self.cnt[key])
        for d in deps:
            if d is not None and "#d" in d[0]:
                self.slot_cons.setdefault(d[0], {})[eng] = tok
        return tok

    def wait_only(self, eng, deps):
        waits = []
        for d in deps:
            if d is None:
                continue
            pk, pc = d
            if pc > self.hwm[eng][pk]:
                waits.append((pk, pc))
                self.hwm[eng][pk] = pc
        self.ops[eng].append((None, waits, 0, eng + "#c"))

    def check(self):
        pos = {e: 0 for e in ENGINES}
        val = dict(self.start_cnt)
        progress = True
        while progress:
            progress = False
            for e in ENGINES:
                while pos[e] < len(self.ops[e]):
                    fn, waits, amt, key = self.ops[e][pos[e]]
                    if any(val[pk] < pc for pk, pc in waits):
                        break
                    val[key] += amt
                    pos[e] += 1
                    progress = True
        stuck = {e: (pos[e], len(self.ops[e])) for e in ENGINES
                 if pos[e] < len(self.ops[e])}
        if stuck:
            msg = []
            for e, (p, n) in stuck.items():
                msg.append(f"{e}@{p}/{n} waits={self.ops[e][p][1]}")
            raise RuntimeError(f"pipe {self.tag} deadlock: {msg}")

    def emit(self):
        self.check()
        if self.xsem is not None:
            finals = [(k, self.cnt[k]) for k in self.cnt
                      if self.cnt[k] > self.start_cnt[k] or k == "gpsimd#c"]
            self.ops["gpsimd"].append((None, finals, 0, "gpsimd#c"))
            self.ops["gpsimd"].append(("XINC", [], 0, "gpsimd#c"))
        with self.nc.Block(self.tag) as blk:
            for e in ENGINES:
                if not self.ops[e]:
                    continue

                def body(eng, _e=e):
                    if self.xsem is not None and self.xbase > 0:
                        eng.wait_ge(self.xsem, self.xbase)
                    for fn, waits, amt, key in self.ops[_e]:
                        if fn == "XINC":
                            eng.nop().then_inc(self.xsem, 1)
                            continue
                        for pk, pc in waits:
                            eng.wait_ge(self.sem[pk], pc)
                        if fn is None:
                            continue
                        instr = fn(eng)
                        if amt:
                            if isinstance(instr, (list, tuple)):
                                assert 16 * len(instr) == amt
                                for it in instr:
                                    it.then_inc(self.sem[key], 16)
                            elif instr is None:
                                eng.nop().then_inc(self.sem[key], amt)
                            else:
                                instr.then_inc(self.sem[key], amt)
                getattr(blk, e)(body)


def make_identity(g, identity):
    sq = identity.shape[1]
    g.memset(identity[:], 0.0)
    return g.affine_select(
        out=identity[:], in_=identity[:],
        compare_op=ALU.not_equal, fill=1.0, base=0,
        pattern=[[-1, sq]], channel_multiplier=1)


def prev_tok(lst, g, n=1):
    return lst[g - n] if g - n >= 0 else None


def build_program(debug=False):
    nc = bass.Bass("TRN2", target_bir_lowering=False)

    xT = nc.dram_tensor("xT", [D * S], F32, kind="ExternalInput")
    wproj = nc.dram_tensor("wproj", [D * C_END], F32, kind="ExternalInput")
    wo_in = nc.dram_tensor("wo_in", [D * D], F32, kind="ExternalInput")
    freq = nc.dram_tensor("freq", [2049 * 128], F32, kind="ExternalInput")
    gateT = nc.dram_tensor("gateT", [D * NE], F32, kind="ExternalInput")
    ebias = nc.dram_tensor("ebias", [NE], F32, kind="ExternalInput")
    sw1T = nc.dram_tensor("sw1T", [D * HID], F32, kind="ExternalInput")
    sw3T = nc.dram_tensor("sw3T", [D * HID], F32, kind="ExternalInput")
    sw2T = nc.dram_tensor("sw2T", [HID * D], F32, kind="ExternalInput")
    w1T = nc.dram_tensor("w1T", [NE * D * HID], F32, kind="ExternalInput")
    w3T = nc.dram_tensor("w3T", [NE * D * HID], F32, kind="ExternalInput")
    w2T = nc.dram_tensor("w2T", [NE * HID * D], F32, kind="ExternalInput")
    rsel8_in = nc.dram_tensor("rsel8_in", [8 * 8 * 128], F32, kind="ExternalInput")

    y2_out = nc.dram_tensor("y2", [D, 2048], F32, kind="ExternalOutput")
    if debug:
        dbg = {n: nc.dram_tensor(f"dbg_{n}", sz, BF16, kind="ExternalOutput")
               for n, sz in (("q", [H * 128 * 2048]), ("k", [H * 128 * S]),
                             ("v", [H * S * 128]), ("y1", [D * 2048]),
                             ("h", [9 * HID * 2048]), ("att", [128 * H * 2048]),
                             ("mf", [8 * 2048]), ("ffn", [128 * KD * 2048]))}

    q_scr = nc.dram_tensor("q_scr", [H * 128 * 2048], BF16, kind="Internal")
    k_scr = nc.dram_tensor("k_scr", [H * 128 * S], BF16, kind="Internal")
    v_scr = nc.dram_tensor("v_scr", [H * S * 128], BF16, kind="Internal")
    y1_scr = nc.dram_tensor("y1_scr", [D * 2048], BF16, kind="Internal")
    h_scr = nc.dram_tensor("h_scr", [9 * HID * 2048], BF16, kind="Internal")

    def dap(t, off, dims):
        return bass.AP(t, off, dims)

    with contextlib.ExitStack() as st0:
        Sb = lambda *a: st0.enter_context(nc.sbuf_tensor(*a))
        xsem = st0.enter_context(nc.semaphore("xphase"))
        xctr = [0]
        gsems = {}
        gcnt = {}
        grot = {e: 0 for e in ENGINES}
        for e in ENGINES:
            gsems[e + "#c"] = st0.enter_context(nc.semaphore(f"{e}_c"))
            gcnt[e + "#c"] = 0
            for sl in range(Pipe.DMA_SLOTS[e]):
                gsems[f"{e}#d{sl}"] = st0.enter_context(
                    nc.semaphore(f"{e}_d{sl}"))
                gcnt[f"{e}#d{sl}"] = 0

        def mkpipe(tag):
            p = Pipe(nc, gsems, gcnt, grot, tag, xsem=xsem, xbase=xctr[0])
            xctr[0] += 1
            return p

        onesK = Sb("onesK", [128, 16], BF16)
        onesrow = Sb("onesrow", [32, 128], BF16)
        identm = Sb("identm", [128, 128], BF16)
        rsel8 = Sb("rsel8", [32, 8 * 128], BF16)
        bias_bc = Sb("bias_bc", [128, NE], F32)
        gbf = Sb("gbf", [128, KD * NE], BF16)
        m_feat = Sb("m_feat", [32, 2048], BF16)
        negM16 = Sb("negM16", [32, 4], F32)
        cval = Sb("cval", [32, 4], F32)

        with contextlib.ExitStack() as stA:
            SA = lambda *a: stA.enter_context(nc.sbuf_tensor(*a))
            wbf = SA("wbf", [128, KD * C_END], BF16)

            with contextlib.ExitStack() as stp:
                SP = lambda *a: stp.enter_context(nc.sbuf_tensor(*a))
                wf32 = SP("wf32", [128, KD * C_END], F32)
                frB = SP("frB", [128, 64], F32)
                fiB = SP("fiB", [128, 64], F32)
                frW = SP("frW", [128, 512], F32)
                fiW = SP("fiW", [128, 512], F32)
                tA = SP("tA", [128, 512], F32)
                tB = SP("tB", [128, 512], F32)
                tC = SP("tC", [128, 512], F32)

                pp = mkpipe("p0")
                d_wk = [None] * KD
                for k in range(KD):
                    d_wk[k] = pp.op("sync", lambda e, k=k: e.dma_start(
                        out=wf32[:, k * C_END:(k + 1) * C_END],
                        in_=dap(wproj, k * 128 * C_END,
                                [[C_END, 128], [1, C_END]])), dma_n=1)
                d_fr = pp.op("gpsimd", lambda e: e.dma_start(
                    out=frB[:],
                    in_=dap(freq, 2048 * 128, [[0, 128], [2, 64]]),
                    allow_slow_non_contiguous=True), dma_n=1)
                d_fi = pp.op("gpsimd", lambda e: e.dma_start(
                    out=fiB[:],
                    in_=dap(freq, 2048 * 128 + 1, [[0, 128], [2, 64]]),
                    allow_slow_non_contiguous=True), dma_n=1)
                d_b = pp.op("gpsimd", lambda e: e.dma_start(
                    out=bias_bc[:], in_=dap(ebias, 0, [[0, 128], [1, NE]])), dma_n=1)
                d_g = pp.op("gpsimd", lambda e: e.dma_start(
                    out=gbf[:].rearrange("p (k c) -> p k c", k=KD),
                    in_=dap(gateT, 0, [[NE, 128], [128 * NE, KD], [1, NE]])), dma_n=1)
                t_ms = pp.op("gpsimd", lambda e: e.memset(rsel8[:], 0.0))
                d_r = pp.op("gpsimd", lambda e: e.dma_start(
                    out=rsel8[0:8, :],
                    in_=dap(rsel8_in, 0, [[1024, 8], [1, 1024]])),
                    deps=[t_ms], dma_n=1)
                pp.op("gpsimd", lambda e: e.memset(cval[:, 0:1], 1e-30))
                pp.op("gpsimd", lambda e: e.memset(cval[:, 1:2], 1e-5))
                pp.op("gpsimd", lambda e: e.memset(cval[:, 2:3], -1.0))
                pp.op("gpsimd", lambda e: e.memset(cval[:, 3:4], -0.5))
                pp.op("gpsimd", lambda e: e.memset(onesK[:], 1.0))
                pp.op("gpsimd", lambda e: e.memset(onesrow[:], 1.0))
                t_mi = pp.op("gpsimd", lambda e: e.memset(identm[:], 0.0))
                pp.op("gpsimd", lambda e: e.affine_select(
                    out=identm[:], in_=identm[:],
                    compare_op=ALU.not_equal, fill=1.0, base=0,
                    pattern=[[-1, 128]], channel_multiplier=1), self_edge=True)

                def rep(v):
                    last = None
                    for rr in range(8):
                        v.tensor_copy(frW[:, rr * 64:(rr + 1) * 64], frB[:])
                    for rr in range(8):
                        last = v.tensor_copy(fiW[:, rr * 64:(rr + 1) * 64],
                                             fiB[:])
                    return last

                t_rep = pp.op("vector", rep, deps=[d_fr, d_fi])

                def fold_k(v, k):
                    base = k * C_END
                    ev = wf32[:, base: base + 1024: 2]
                    od = wf32[:, base + 1: base + 1024: 2]
                    v.tensor_mul(tA[:], ev, frW[:])
                    v.tensor_mul(tB[:], od, fiW[:])
                    v.tensor_sub(tA[:], tA[:], tB[:])
                    v.tensor_mul(tB[:], ev, fiW[:])
                    v.tensor_mul(tC[:], od, frW[:])
                    v.tensor_add(tB[:], tB[:], tC[:])
                    v.tensor_copy(ev, tA[:])
                    return v.tensor_copy(od, tB[:])

                def cast_k(v, k):
                    b = k * C_END
                    v.tensor_copy(wbf[:, b: b + C_BV], wf32[:, b: b + C_BV])
                    v.tensor_scalar_mul(wbf[:, b + C_BV: b + C_AQ],
                                        wf32[:, b + C_BV: b + C_AQ], 1.0 / RV)
                    v.tensor_scalar_mul(wbf[:, b + C_AQ: b + C_AK],
                                        wf32[:, b + C_AQ: b + C_AK], Q_SCALE)
                    v.tensor_scalar_mul(wbf[:, b + C_AK: b + C_AV],
                                        wf32[:, b + C_AK: b + C_AV], 1.0 / RK)
                    return v.tensor_copy(wbf[:, b + C_AV: b + C_END],
                                         wf32[:, b + C_AV: b + C_END])

                t_fold = None
                for k in range(KD):
                    t_fk = pp.op("vector", lambda v, k=k: fold_k(v, k),
                                 deps=[d_wk[k], t_rep],
                                 self_edge=(k == 0))
                    t_fold = pp.op("vector", lambda v, k=k: cast_k(v, k),
                                   deps=[t_fk])
                pp.wait_only("gpsimd", [d_fr, d_fi, d_b, d_g, d_r])
                pp.wait_only("sync", [d_wk[KD - 1]])
                pp.emit()

            with contextlib.ExitStack() as stp:
                SP = lambda *a: stp.enter_context(nc.sbuf_tensor(*a))
                PPs = lambda *a: stp.enter_context(nc.psum_tensor(*a))
                xf32 = [SP(f"xf32_{p}", [128, D], F32) for p in range(3)]
                xbf = [SP(f"xbf_{p}", [128, D], BF16) for p in range(3)]
                bfq = [SP(f"bfq_{p}", [128, 768], BF16) for p in range(3)]
                faq = [SP(f"faq_{p}", [128, 96], F32) for p in range(3)]
                bkv = [SP(f"bkv_{p}", [128, 512], BF16) for p in range(3)]
                fakv = [SP(f"fakv_{p}", [128, 64], F32) for p in range(3)]
                qtok = [SP(f"qtok_{p}", [128, 2048], BF16) for p in range(3)]
                ktok = [SP(f"ktok_{p}", [128, 2048], BF16) for p in range(3)]
                vtok = [SP(f"vtok_{p}", [128, 2048], BF16) for p in range(3)]
                prod = [SP(f"prod_{r}", [128, 2048], BF16) for r in range(6)]
                prodK = [SP(f"prodK_{r}", [128, 2048], BF16) for r in range(2)]
                prodV = [SP(f"prodV_{r}", [128, 2048], BF16) for r in range(2)]
                sqn = [SP(f"sqn_{p}", [128, 2048], BF16) for p in range(3)]
                nrmt = [SP(f"nrmt_{p}", [128, 16], BF16) for p in range(3)]
                qnrm = SP("qnrm", [32, 2048], BF16)
                knrm = SP("knrm", [32, 4096], BF16)
                nrmall = SP("nrmall", [128, 48 * 16], BF16)
                nmax = SP("nmax", [32, 4], F32)
                psNT = [PPs(f"psNT_{p}", [32, 128], BF16) for p in range(2)]
                psA = [PPs(f"psA_{p}", [128, 512], F32) for p in range(2)]
                psB = [PPs(f"psB_{p}", [128, 416], F32) for p in range(2)]
                psC = [PPs(f"psC_{p}", [128, 512], F32) for p in range(2)]

                pp = mkpipe("p1")
                t_x = [None] * NT
                t_cast = [None] * NT
                t_proj = [None] * NT
                t_pev = [None] * NT
                t_form = [None] * NT
                t_wq = [None] * NT
                t_wk = [None] * NT
                t_wv = [None] * NT
                t_nsq = []
                t_nred = []
                t_ntp = []
                t_nev = []
                t_ntile = [None] * NT

                t_prodv = [None] * NT
                t_prods = [None] * NT

                def emit_front(i):
                    p = i % 3
                    q = i % 2
                    own = i < NO
                    t_x[i] = pp.op("sync", lambda e, i=i, p=p: e.dma_start(
                        out=xf32[p][:].rearrange("p (k c) -> p k c", k=KD),
                        in_=dap(xT, i * 128, [[S, 128], [128 * S, KD], [1, 128]])),
                        deps=[t_cast[i - 3] if i >= 3 else None], dma_n=1)
                    t_cast[i] = pp.op("vector", lambda e, p=p: e.tensor_copy(
                        xbf[p][:], xf32[p][:]),
                        deps=[t_x[i], t_proj[i - 3] if i >= 3 else None])

                    def proj(t, p=p, q=q, own=own):
                        last = None
                        for k in range(KD):
                            lhsT = xbf[p][:, k * 128:(k + 1) * 128]
                            wb = k * C_END
                            st_, sp_ = (k == 0), (k == KD - 1)
                            if own:
                                t.matmul(psA[q][:], lhsT,
                                         wbf[:, wb: wb + 512], start=st_, stop=sp_)
                                t.matmul(psB[q][:, 0:256], lhsT,
                                         wbf[:, wb + 512: wb + 768],
                                         start=st_, stop=sp_)
                                t.matmul(psB[q][:, 256:352], lhsT,
                                         wbf[:, wb + C_AQ: wb + C_AK],
                                         start=False, stop=sp_)
                            t.matmul(psC[q][:], lhsT,
                                     wbf[:, wb + C_BK: wb + C_AQ],
                                     start=st_, stop=sp_)
                            last = t.matmul(psB[q][:, 352:416], lhsT,
                                            wbf[:, wb + C_AK: wb + C_END],
                                            start=(st_ and not own), stop=sp_)
                        return last

                    t_proj[i] = pp.op("tensor", proj,
                                      deps=[t_cast[i],
                                            t_pev[i - 2] if i >= 2 else None])

                    def pev(sc, p=p, q=q, own=own):
                        if own:
                            sc.copy(bfq[p][:, 0:512], psA[q][:])
                            sc.copy(bfq[p][:, 512:768], psB[q][:, 0:256])
                            sc.copy(faq[p][:], psB[q][:, 256:352])
                        sc.copy(bkv[p][:], psC[q][:])
                        return sc.copy(fakv[p][:], psB[q][:, 352:416])

                    t_pev[i] = pp.op("scalar", pev,
                                     deps=[t_proj[i],
                                           t_prodv[i - 3] if i >= 3 else None,
                                           t_prods[i - 3] if i >= 3 else None])

                def emit_compute(i):
                    p = i % 3
                    own = i < NO

                    def q_prods(v, p=p):
                        last = None
                        for r in range(RQ):
                            for hh in range(H):
                                last = v.tensor_scalar(
                                    prod[r][:, hh * 128:(hh + 1) * 128],
                                    bfq[p][:, r * 128:(r + 1) * 128],
                                    faq[p][:, r * H + hh: r * H + hh + 1],
                                    None, op0=ALU.mult)
                        return last

                    def q_adds(v, p=p):
                        v.tensor_add(prod[0][:], prod[0][:], prod[1][:])
                        v.tensor_add(prod[2][:], prod[2][:], prod[3][:])
                        v.tensor_add(prod[4][:], prod[4][:], prod[5][:])
                        v.tensor_add(prod[0][:], prod[0][:], prod[2][:])
                        return v.tensor_add(qtok[p][:], prod[0][:], prod[4][:])

                    def k_prods_dve(v, p=p):
                        last = None
                        for r in range(RK):
                            for hh in range(H):
                                last = v.tensor_scalar(
                                    prodK[r][:, hh * 128:(hh + 1) * 128],
                                    bkv[p][:, r * 128:(r + 1) * 128],
                                    fakv[p][:, r * H + hh: r * H + hh + 1],
                                    None, op0=ALU.mult)
                        return last

                    def kv_prods_sc(sc, p=p, own=own):
                        last = None
                        if own:
                            for r in range(RK):
                                for hh in range(H):
                                    last = sc.activation(
                                        prodK[r][:, hh * 128:(hh + 1) * 128],
                                        bkv[p][:, r * 128:(r + 1) * 128],
                                        AF.Copy,
                                        scale=fakv[p][:, r * H + hh:
                                                      r * H + hh + 1])
                        for r in range(RV):
                            for hh in range(H):
                                last = sc.activation(
                                    prodV[r][:, hh * 128:(hh + 1) * 128],
                                    bkv[p][:, 256 + r * 128: 256 + (r + 1) * 128],
                                    AF.Copy,
                                    scale=fakv[p][:, 32 + r * H + hh:
                                                  33 + r * H + hh])
                        return last

                    def kv_adds(v, p=p):
                        v.tensor_add(ktok[p][:], prodK[0][:], prodK[1][:])
                        return v.tensor_add(vtok[p][:], prodV[0][:], prodV[1][:])

                    war = [t_wv[i - 3] if i >= 3 else None,
                           t_wk[i - 3] if i >= 3 else None,
                           t_wq[i - 3] if (i >= 3 and i - 3 < NO) else None,
                           t_ntile[i - 3] if i >= 3 else None]
                    if own:
                        tq1 = pp.op("vector", q_prods, deps=[t_pev[i]] + war)
                        tq2 = pp.op("vector", q_adds, deps=[tq1])
                        tsc = pp.op("scalar", kv_prods_sc,
                                    deps=[t_pev[i], t_form[i - 1] if i >= 1
                                          else None] + war)
                        t_form[i] = pp.op("vector", kv_adds, deps=[tq2, tsc])
                        t_prodv[i] = tq1
                    else:
                        tk1 = pp.op("vector", k_prods_dve,
                                    deps=[t_pev[i]] + war)
                        tsc = pp.op("scalar", kv_prods_sc,
                                    deps=[t_pev[i], t_form[i - 1] if i >= 1
                                          else None] + war)
                        t_form[i] = pp.op("vector", kv_adds, deps=[tk1, tsc])
                        t_prodv[i] = tk1
                    t_prods[i] = tsc

                    if own:
                        t_wq[i] = pp.op("gpsimd", lambda e, p=p, i=i: e.dma_start(
                            out=dap(q_scr, i * 128 * 128,
                                    [[128, 128], [2048 * 128, H], [1, 128]]),
                            in_=qtok[p][:].rearrange("p (h c) -> p h c", h=H)),
                            deps=[t_form[i]], dma_n=1)
                    t_wk[i] = pp.op("gpsimd", lambda e, p=p, i=i: e.dma_start(
                        out=dap(k_scr, i * 128 * 128,
                                [[128, 128], [S * 128, H], [1, 128]]),
                        in_=ktok[p][:].rearrange("p (h c) -> p h c", h=H)),
                        deps=[t_form[i]], dma_n=1)
                    t_wv[i] = pp.op("gpsimd", lambda e, p=p, i=i: e.dma_start(
                        out=dap(v_scr, i * 128 * 128,
                                [[128, 128], [S * 128, H], [1, 128]]),
                        in_=vtok[p][:].rearrange("p (h c) -> p h c", h=H)),
                        deps=[t_form[i]], dma_n=1)

                norm_dsts = []

                def emit_norm(ti, src_t, dst, col):
                    gn = len(t_nsq)
                    pn = gn % 3
                    norm_dsts.append((dst, col))
                    t_nsq.append(pp.op(
                        "scalar", lambda e, pn=pn, src_t=src_t: e.activation(
                            sqn[pn][:], src_t[:], AF.Square),
                        deps=[t_form[ti],
                              t_nred[gn - 3] if gn >= 3 else None]))

                    def nred(e, pn=pn, gn=gn):
                        with nc.allow_low_precision(reason="softmax bound only"):
                            return e.reduce_sum(
                                nrmall[:, gn * 16:(gn + 1) * 16].rearrange(
                                    "p (h o) -> p h o", h=H),
                                sqn[pn][:].rearrange("p (h c) -> p h c", h=H),
                                axis=AX.X)

                    t_nred.append(pp.op(
                        "vector", nred, deps=[t_nsq[gn]]))

                def emit_norms_for(j):
                    pj = j % 3
                    emit_norm(j, ktok[pj], knrm, j * 128)
                    if j < NO:
                        emit_norm(j, qtok[pj], qnrm, j * 128)
                    t_ntile[j] = t_nsq[-1]

                for i in range(NT + 2):
                    if i < NT:
                        emit_front(i)
                    if 1 <= i <= NT:
                        emit_compute(i - 1)
                    if 2 <= i:
                        emit_norms_for(i - 2)

                for gn, (dst, col) in enumerate(norm_dsts):
                    bn = gn % 2
                    t_ntp.append(pp.op(
                        "tensor", lambda e, gn=gn, bn=bn: e.transpose(
                            psNT[bn][0:16, :],
                            nrmall[:, gn * 16:(gn + 1) * 16], identm[:]),
                        deps=[t_nred[gn],
                              t_nev[gn - 2] if gn >= 2 else None]))
                    t_nev.append(pp.op(
                        "scalar", lambda e, dst=dst, col=col, bn=bn: e.copy(
                            dst[0:16, col:col + 128], psNT[bn][0:16, :]),
                        deps=[t_ntp[gn]]))
                b1 = pp.op("vector", lambda e: e.reduce_max(
                    nmax[0:16, 0:1], qnrm[0:16, :], axis=AX.X),
                    deps=[t_nev[-1]], self_edge=True)
                b2 = pp.op("vector", lambda e: e.reduce_max(
                    nmax[0:16, 1:2], knrm[0:16, :], axis=AX.X),
                    deps=[b1], self_edge=True)
                b3 = pp.op("vector", lambda e: e.tensor_mul(
                    nmax[0:16, 2:3], nmax[0:16, 0:1], nmax[0:16, 1:2]),
                    deps=[b2], self_edge=True)
                b4 = pp.op("scalar", lambda e: e.activation(
                    nmax[0:16, 3:4], nmax[0:16, 2:3], AF.Sqrt), deps=[b3])
                pp.op("vector", lambda e: e.tensor_scalar(
                    negM16[0:16, 0:1], nmax[0:16, 3:4], -1.0, -2.0,
                    op0=ALU.mult, op1=ALU.add), deps=[b4], self_edge=True)
                pp.wait_only("gpsimd", [t_wv[NT - 1], t_wk[NT - 1], t_wq[NO - 1]])
                pp.wait_only("sync", [t_x[NT - 1]])
                pp.emit()

        with contextlib.ExitStack() as stB:
            SB_ = lambda *a: stB.enter_context(nc.sbuf_tensor(*a))
            wobf = SB_("wobf", [128, KD * 2048], BF16)
            att = SB_("att", [128, H * 2048], BF16)

            with contextlib.ExitStack() as stp:
                SP = lambda *a: stp.enter_context(nc.sbuf_tensor(*a))
                PPs = lambda *a: stp.enter_context(nc.psum_tensor(*a))
                qsb = [SP(f"qsb_{p}", [128, 2048], BF16) for p in range(2)]
                ksb = [SP(f"ksb_{p}", [128, 4096], BF16) for p in range(2)]
                vsb = [SP(f"vsb_{p}", [128, 4096], BF16) for p in range(2)]
                ebuf = [SP(f"ebuf_{j}", [128, 1024], BF16) for j in range(2)]
                diag16 = SP("diag16", [32, 16], BF16)
                negMbA = SP("negMbA", [128, 16], F32)
                rrow = SP("rrow", [32, 512], BF16)
                rbc = SP("rbc", [128, 512], F32)
                psS = [PPs(f"psS_{j}", [128, 1024], F32) for j in range(2)]
                psAV = [PPs(f"psAV_{p}", [128, 512], F32) for p in range(2)]
                psD = [PPs(f"psD_{p}", [128, 512], F32) for p in range(2)]

                pp = mkpipe("p2")
                NCI = H * NCHUNK
                NG = NCI * NT
                NP = NG // 2
                t_ld = [None] * H
                t_sl = [None] * NG
                t_ep = [None] * NP
                t_avl = [None] * NG
                t_ln1 = [None] * NCI
                t_ln2 = [None] * NCI
                t_bcm = [None] * NCI
                t_rbce = [None] * NCI
                t_ev = [None] * NCI

                t_wo = pp.op("gpsimd", lambda e: e.dma_start(
                    out=wobf[:].rearrange("p (k c) -> p k c", k=KD),
                    in_=dap(wo_in, 0, [[D, 128], [128 * D, KD], [1, D]])), dma_n=1)

                def emit_load(hh):
                    p = hh % 2
                    deps = []
                    if hh >= 2:
                        deps = [t_sl[(hh - 1) * NCHUNK * NT - 1],
                                t_avl[(hh - 1) * NCHUNK * NT - 1]]

                    def f(e, hh=hh, p=p):
                        return [
                            e.dma_start(out=qsb[p][:],
                                        in_=dap(q_scr, hh * 2048 * 128,
                                                [[128, 2048], [1, 128]]),
                                        transpose=True),
                            e.dma_start(out=ksb[p][:, 0:2048],
                                        in_=dap(k_scr, hh * S * 128,
                                                [[128, 2048], [1, 128]]),
                                        transpose=True),
                            e.dma_start(out=ksb[p][:, 2048:4096],
                                        in_=dap(k_scr, hh * S * 128 + 2048 * 128,
                                                [[128, 2048], [1, 128]]),
                                        transpose=True),
                            e.dma_start(
                                out=vsb[p][:].rearrange("p (j c) -> p j c", j=NT),
                                in_=dap(v_scr, hh * S * 128,
                                        [[128, 128], [128 * 128, NT], [1, 128]]))]

                    t_ld[hh] = pp.op("sync", f, deps=deps, dma_n=4)

                def emit_bound():
                    d1 = pp.op("vector", lambda e: e.tensor_scalar(
                        diag16[0:16, :], identm[0:16, 0:16], negM16[0:16, 0:1],
                        None, op0=ALU.mult))
                    d2 = pp.op("tensor", lambda e: e.matmul(
                        psD[0][0:128, 0:16], onesrow[0:16, :], diag16[0:16, :],
                        start=True, stop=True), deps=[d1])
                    return pp.op("scalar", lambda e: e.copy(
                        negMbA[:], psD[0][0:128, 0:16]), deps=[d2])

                t_bnd = emit_bound()

                def emit_S(hh, cc, j):
                    p = hh % 2
                    g = (hh * NCHUNK + cc) * NT + j
                    pr = g // 2
                    t_sl[g] = pp.op(
                        "tensor", lambda e, p=p, cc=cc, j=j, g=g, pr=pr: e.matmul(
                            psS[pr % 2][:, (g % 2) * 512:(g % 2 + 1) * 512],
                            ksb[p][:, j * 128:(j + 1) * 128],
                            qsb[p][:, cc * 512:(cc + 1) * 512],
                            start=True, stop=True),
                        deps=[t_ld[hh], t_ep[pr - 2] if pr >= 2 else None])
                    if g % 2 == 1:
                        t_ep[pr] = pp.op(
                            "scalar", lambda e, hh=hh, pr=pr: e.activation(
                                ebuf[pr % 2][:], psS[pr % 2][:], AF.Exp,
                                bias=negMbA[:, hh:hh + 1], scale=1.0),
                            deps=[t_sl[g], t_bnd,
                                  t_avl[g - 3] if g >= 3 else None])

                def emit_AV(hh, cc, j):
                    p = hh % 2
                    ci = hh * NCHUNK + cc
                    g = ci * NT + j
                    pr = g // 2
                    deps_av = [t_ep[pr]]
                    if j == 0 and ci >= 2:
                        deps_av += [t_ev[ci - 2], t_rbce[ci - 2]]
                    eb = lambda: ebuf[pr % 2][:, (g % 2) * 512:(g % 2 + 1) * 512]
                    t_avl[g] = pp.op(
                        "tensor", lambda e, p=p, cc=cc, j=j, pr=pr, g=g: (
                            e.matmul(psAV[cc % 2][:],
                                     vsb[p][:, j * 128:(j + 1) * 128],
                                     ebuf[pr % 2][:, (g % 2) * 512:(g % 2 + 1) * 512],
                                     start=(j == 0), stop=(j == NT - 1)),
                            e.matmul(psD[cc % 2][0:1, :], onesK[:, 0:1],
                                     ebuf[pr % 2][:, (g % 2) * 512:(g % 2 + 1) * 512],
                                     start=(j == 0), stop=(j == NT - 1)))[-1],
                        deps=deps_av)

                def emit_tail(ci):
                    hh, cc = divmod(ci, NCHUNK)
                    t_ln1[ci] = pp.op(
                        "scalar", lambda e, cc=cc: e.activation(
                            rrow[0:1, :], psD[cc % 2][0:1, :], AF.Ln,
                            bias=cval[0:1, 0:1]),
                        deps=[t_avl[ci * NT + NT - 1]])
                    t_ln2[ci] = pp.op(
                        "scalar", lambda e: e.activation(
                            rrow[0:1, :], rrow[0:1, :], AF.Exp,
                            scale=cval[0:1, 2:3]),
                        deps=[t_ln1[ci]], self_edge=True)
                    t_bcm[ci] = pp.op(
                        "tensor", lambda e, cc=cc: e.matmul(
                            psD[cc % 2][:], onesrow[0:1, :], rrow[0:1, :],
                            start=True, stop=True), deps=[t_ln2[ci]])
                    t_rbce[ci] = pp.op(
                        "scalar", lambda e, cc=cc: e.copy(rbc[:], psD[cc % 2][:]),
                        deps=[t_bcm[ci], t_ev[ci - 1] if ci >= 1 else None])
                    t_ev[ci] = pp.op(
                        "vector", lambda e, hh=hh, cc=cc: e.tensor_mul(
                            att[:, hh * 2048 + cc * 512: hh * 2048 + (cc + 1) * 512],
                            psAV[cc % 2][:], rbc[:]),
                        deps=[t_rbce[ci], t_avl[ci * NT + NT - 1]])

                def emit_chunk(hh, cc):
                    ci = hh * NCHUNK + cc
                    emit_S(hh, cc, 0)
                    emit_S(hh, cc, 1)
                    for jp in range(1, NT // 2):
                        emit_S(hh, cc, 2 * jp)
                        emit_S(hh, cc, 2 * jp + 1)
                        emit_AV(hh, cc, 2 * jp - 2)
                        emit_AV(hh, cc, 2 * jp - 1)
                        if jp == 2 and ci >= 1 and t_ln1[ci - 1] is None:
                            emit_tail(ci - 1)
                    emit_AV(hh, cc, NT - 2)
                    emit_AV(hh, cc, NT - 1)

                emit_load(0)
                for hh in range(H):
                    if hh + 1 < H:
                        emit_load(hh + 1)
                    for cc in range(NCHUNK):
                        emit_chunk(hh, cc)
                emit_tail(H * NCHUNK - 1)
                pp.wait_only("sync", [t_ld[H - 1]])
                pp.wait_only("gpsimd", [t_wo])
                pp.emit()

            with contextlib.ExitStack() as stp:
                SP = lambda *a: stp.enter_context(nc.sbuf_tensor(*a))
                PPs = lambda *a: stp.enter_context(nc.psum_tensor(*a))
                xres = [SP(f"xres_{p}", [128, 512], F32) for p in range(4)]
                ufull = SP("ufull", [128, KD * 512], F32)
                ubx = [SP(f"ubx_{p}", [128, 512], BF16) for p in range(2)]
                usx = [SP(f"usx_{p}", [128, 512], BF16) for p in range(2)]
                srow = SP("srow", [32, 2048], F32)
                mrow = SP("mrow", [32, 512], BF16)
                rrw = SP("rrw", [32, 512], BF16)
                mbc = SP("mbc", [128, 512], BF16)
                rbc3 = SP("rbc3", [128, 512], BF16)
                y1c = SP("y1c", [128, KD * 512], BF16)
                scrs = SP("scrs", [128, 32], F32)
                ttb = [SP(f"ttb_{p}", [128, 8], F32) for p in range(2)]
                cge = [SP("cge_0", [128, 64], F32)] * 2
                cgt = [SP("cgt_0", [128, 64], F32)] * 2
                rka = [SP(f"rka_{p}", [128, 16], F32) for p in range(2)]
                mskb = [SP(f"mskb_{p}", [128, 8], BF16) for p in range(2)]
                psW = [PPs(f"psW_{p}", [128, 512], F32) for p in range(2)]
                psMV = PPs("psMV", [32, 1024], F32)
                psBC = PPs("psBC", [128, 512], F32)
                psG = [PPs(f"psG_{p}", [128, 8], F32) for p in range(2)]
                psMT = PPs("psMT", [32, 512], BF16)

                pp = mkpipe("p3")
                if debug:
                    pp.op("gpsimd", lambda e: e.dma_start(
                        out=dap(dbg["att"], 0, [[H * 2048, 128], [1, H * 2048]]),
                        in_=att[:]), dma_n=1)
                t_x3 = [None] * (NCHUNK * KD)
                t_wom = [None] * (NCHUNK * KD)
                t_u = [None] * (NCHUNK * KD)
                t_ub = [None] * (NCHUNK * KD)
                t_stk = [None] * (NCHUNK * KD)
                stk_fns = [None] * (NCHUNK * KD)
                t_st = [None] * NCHUNK
                t_lnv = [None] * NCHUNK
                t_rstd = [None] * NCHUNK
                t_bme = [None] * NCHUNK
                t_brme = [None] * NCHUNK
                t_y1 = [None] * (NCHUNK * KD)
                t_wr1 = [None] * NCHUNK
                t_g = [None] * 16
                t_sg = [None] * 16
                t_rk = [None] * 16
                t_ms1 = [None] * 16
                t_ms2 = [None] * 16
                t_ms3 = [None] * 16
                t_mt = [None] * 16
                t_mte = [None] * 16

                for cc in range(NCHUNK):
                    for dt in range(KD):
                        idx = cc * KD + dt
                        t_x3[idx] = pp.op(
                            "sync", lambda e, cc=cc, dt=dt, idx=idx: e.dma_start(
                                out=xres[idx % 4][:],
                                in_=dap(xT, dt * 128 * S + cc * 512,
                                        [[S, 128], [1, 512]])),
                            deps=[prev_tok(t_u, idx, 4)], dma_n=1)

                        def wom(t, cc=cc, dt=dt):
                            last = None
                            for k in range(KD):
                                last = t.matmul(
                                    psW[(cc * KD + dt) % 2][:],
                                    wobf[:, k * 2048 + dt * 128:
                                         k * 2048 + (dt + 1) * 128],
                                    att[:, k * 2048 + cc * 512:
                                        k * 2048 + (cc + 1) * 512],
                                    start=(k == 0), stop=(k == KD - 1))
                            return last

                        t_wom[idx] = pp.op("tensor", wom,
                                           deps=[prev_tok(t_u, idx, 2)])
                        t_u[idx] = pp.op(
                            "vector", lambda e, dt=dt, idx=idx: e.tensor_add(
                                ufull[:, dt * 512:(dt + 1) * 512],
                                psW[idx % 2][:], xres[idx % 4][:]),
                            deps=[t_wom[idx], t_x3[idx],
                                  prev_tok(t_y1, idx, KD)])

                        def stk_mm(t, dt=dt, cc=cc):
                            t.matmul(psMV[0:1, 0:512], onesK[:, 0:1],
                                     ubx[dt % 2][:],
                                     start=(dt == 0), stop=(dt == KD - 1))
                            return t.matmul(psMV[0:1, 512:1024], onesK[:, 0:1],
                                            usx[dt % 2][:],
                                            start=(dt == 0), stop=(dt == KD - 1))

                        stk_fns[idx] = stk_mm
                        if dt >= 2:
                            t_stk[idx - 2] = pp.op(
                                "tensor", stk_fns[idx - 2],
                                deps=[t_ub[idx - 2],
                                      t_lnv[cc - 1] if (dt == 2 and cc >= 1)
                                      else None])
                        t_ub[idx] = pp.op(
                            "scalar", lambda e, dt=dt: (
                                e.copy(ubx[dt % 2][:],
                                       ufull[:, dt * 512:(dt + 1) * 512]),
                                e.activation(usx[dt % 2][:],
                                             ufull[:, dt * 512:(dt + 1) * 512],
                                             AF.Square))[-1],
                            deps=[t_u[idx],
                                  t_stk[cc * KD + dt - 2]
                                  if cc * KD + dt >= 2 else None])
                    for dt in (KD - 2, KD - 1):
                        t_stk[cc * KD + dt] = pp.op(
                            "tensor", stk_fns[cc * KD + dt],
                            deps=[t_ub[cc * KD + dt]])
                    t_st[cc] = t_stk[cc * KD + KD - 1]

                    a1 = pp.op("vector", lambda e: e.tensor_scalar_mul(
                        srow[0:1, 0:512], psMV[0:1, 0:512], 1.0 / D),
                        deps=[t_st[cc]])
                    a2 = pp.op("vector", lambda e: e.tensor_scalar_mul(
                        srow[0:1, 512:1024], psMV[0:1, 512:1024], 1.0 / D),
                        deps=[t_st[cc]])
                    a3 = pp.op("vector", lambda e: e.tensor_mul(
                        srow[0:1, 1024:1536], srow[0:1, 0:512], srow[0:1, 0:512]),
                        deps=[a1], self_edge=True)
                    a4 = pp.op("vector", lambda e: e.tensor_sub(
                        srow[0:1, 1536:2048], srow[0:1, 512:1024],
                        srow[0:1, 1024:1536]), deps=[a2, a3], self_edge=True)
                    a5 = pp.op("vector", lambda e: e.tensor_copy(
                        mrow[0:1, :], srow[0:1, 0:512]), deps=[a1], self_edge=True)
                    t_lnv[cc] = a5
                    r1 = pp.op("scalar", lambda e: e.activation(
                        srow[0:1, 1024:1536], srow[0:1, 1536:2048], AF.Ln,
                        bias=cval[0:1, 1:2]), deps=[a4])
                    t_rstd[cc] = pp.op("scalar", lambda e: e.activation(
                        rrw[0:1, :], srow[0:1, 1024:1536], AF.Exp, scale=cval[0:1, 3:4]),
                        deps=[r1], self_edge=True)

                    bm = pp.op("tensor", lambda e: e.matmul(
                        psBC[:], onesrow[0:1, :], mrow[0:1, :],
                        start=True, stop=True),
                        deps=[a5, t_brme[cc - 1] if cc >= 1 else None])
                    t_bme[cc] = pp.op("scalar", lambda e: e.copy(mbc[:], psBC[:]),
                                      deps=[bm])
                    brm = pp.op("tensor", lambda e: e.matmul(
                        psBC[:], onesrow[0:1, :], rrw[0:1, :],
                        start=True, stop=True),
                        deps=[t_bme[cc], t_rstd[cc]])
                    t_brme[cc] = pp.op("scalar", lambda e: e.copy(rbc3[:], psBC[:]),
                                       deps=[brm])

                    for dt in range(KD):
                        idx = cc * KD + dt
                        t_y1[idx] = pp.op(
                            "vector", lambda e, dt=dt: (
                                e.tensor_sub(ufull[:, dt * 512:(dt + 1) * 512],
                                             ufull[:, dt * 512:(dt + 1) * 512],
                                             mbc[:]),
                                e.tensor_mul(y1c[:, dt * 512:(dt + 1) * 512],
                                             ufull[:, dt * 512:(dt + 1) * 512],
                                             rbc3[:]))[-1],
                            deps=[t_bme[cc], t_brme[cc], t_ub[idx],
                                  t_wr1[cc - 1] if cc >= 1 else None,
                                  t_g[(cc - 1) * 4 + 3] if cc >= 1 else None])

                    t_wr1[cc] = pp.op("gpsimd", lambda e, cc=cc: e.dma_start(
                        out=dap(y1_scr, cc * 512,
                                [[2048, 128], [128 * 2048, KD], [1, 512]]),
                        in_=y1c[:].rearrange("p (k c) -> p k c", k=KD)),
                        deps=[t_y1[cc * KD + k] for k in range(KD)], dma_n=1)

                    for st4 in range(4):
                        gi = cc * 4 + st4
                        p = gi % 2

                        def gmm(t, cc=cc, st4=st4, p=p):
                            last = None
                            for k in range(KD):
                                last = t.matmul(
                                    psG[p][:],
                                    y1c[:, k * 512 + st4 * 128:
                                        k * 512 + (st4 + 1) * 128],
                                    gbf[:, k * 8:(k + 1) * 8],
                                    start=(k == 0), stop=(k == KD - 1))
                            return last

                        t_g[gi] = pp.op(
                            "tensor", gmm,
                            deps=[t_y1[cc * KD + k] for k in range(KD)] +
                                 [prev_tok(t_sg, gi, 2)])
                        t_sg[gi] = pp.op(
                            "scalar", lambda e, st4=st4, p=p: e.activation(
                                scrs[:, st4 * 8:(st4 + 1) * 8], psG[p][:],
                                AF.Sigmoid),
                            deps=[t_g[gi], prev_tok(t_ms3, gi, 4)])

                        def rank(v, st4=st4, p=p):
                            sl = scrs[:, st4 * 8:(st4 + 1) * 8]
                            v.tensor_add(ttb[p][:], sl, bias_bc[:])
                            v.memset(rka[p][:], 0.0)
                            for e2 in range(NE):
                                v.tensor_scalar(cge[p][:, e2 * 8:(e2 + 1) * 8],
                                                ttb[p][:], ttb[p][:, e2:e2 + 1],
                                                None, op0=ALU.is_ge)
                                v.tensor_scalar(cgt[p][:, e2 * 8:(e2 + 1) * 8],
                                                ttb[p][:], ttb[p][:, e2:e2 + 1],
                                                None, op0=ALU.is_gt)
                            last = None
                            for e2 in range(NE):
                                if e2 > 0:
                                    last = v.reduce_sum(
                                        rka[p][:, e2:e2 + 1],
                                        cge[p][:, e2 * 8:e2 * 8 + e2], axis=AX.X)
                                if e2 < NE - 1:
                                    last = v.reduce_sum(
                                        rka[p][:, 8 + e2:9 + e2],
                                        cgt[p][:, e2 * 8 + e2 + 1:(e2 + 1) * 8],
                                        axis=AX.X)
                            return last

                        t_rk[gi] = pp.op("vector", rank,
                                         deps=[t_sg[gi], prev_tok(t_mt, gi, 2)],
                                         self_edge=True)
                        t_ms1[gi] = pp.op("vector", lambda v, p=p: v.tensor_add(
                            rka[p][:, 0:8], rka[p][:, 0:8], rka[p][:, 8:16]),
                            deps=[t_rk[gi]], self_edge=True)
                        t_ms2[gi] = pp.op("vector", lambda v, p=p: v.tensor_scalar(
                            ttb[p][:], rka[p][:, 0:8], 2.0, None, op0=ALU.is_lt),
                            deps=[t_ms1[gi]], self_edge=True)
                        t_ms3[gi] = pp.op("vector", lambda v, p=p, st4=st4:
                                          v.tensor_mul(
                            mskb[p][:], ttb[p][:], scrs[:, st4 * 8:(st4 + 1) * 8]),
                            deps=[t_ms2[gi]], self_edge=True)
                        t_mt[gi] = pp.op(
                            "tensor", lambda e, p=p, st4=st4: e.transpose(
                                psMT[0:8, st4 * 128:(st4 + 1) * 128],
                                mskb[p][:], identm[:]),
                            deps=[t_ms3[gi], prev_tok(t_mte, gi, 1)])
                        t_mte[gi] = pp.op(
                            "scalar", lambda e, cc=cc, st4=st4: e.copy(
                                m_feat[0:8, cc * 512 + st4 * 128:
                                       cc * 512 + (st4 + 1) * 128],
                                psMT[0:8, st4 * 128:(st4 + 1) * 128]),
                            deps=[t_mt[gi]])

                pp.wait_only("gpsimd", [t_wr1[NCHUNK - 1]])
                pp.wait_only("sync", [t_x3[NCHUNK * KD - 1]])
                pp.emit()

        with contextlib.ExitStack() as stp:
            SP = lambda *a: stp.enter_context(nc.sbuf_tensor(*a))
            PPs = lambda *a: stp.enter_context(nc.psum_tensor(*a))
            w1b = [SP(f"w1b_{p}", [128, KD * HID], BF16) for p in range(2)]
            w3b = [SP(f"w3b_{p}", [128, KD * HID], BF16) for p in range(2)]
            y1res = SP("y1res", [128, KD * 2048], BF16)
            abuf = [SP(f"abuf_{p}", [128, 512], BF16) for p in range(2)]
            bbuf = [SP("bbuf_0", [128, 512], BF16)] * 2
            sabuf = [SP("sabuf_0", [128, 512], BF16)] * 2
            hbuf = [SP(f"hbuf_{p}", [128, 512], BF16) for p in range(2)]
            mbc4 = [SP(f"mbc4_{p}", [128, 512], BF16) for p in range(2)]
            psU1 = [PPs(f"psU1_{p}", [128, 512], F32) for p in range(2)]
            psU3 = [PPs(f"psU3_{p}", [128, 512], F32) for p in range(2)]
            psMB = [PPs(f"psMB_{p}", [128, 512], F32) for p in range(2)]

            pp = mkpipe("p4")
            if debug:
                pp.op("gpsimd", lambda e: e.dma_start(
                    out=dap(dbg["mf"], 0, [[2048, 8], [1, 2048]]),
                    in_=m_feat[0:8, :]), dma_n=1)
            NHS = 9 * NCHUNK * KH
            t_w = [None] * 9
            t_yl = pp.op("sync", lambda e: e.dma_start(
                out=y1res[:].rearrange("p (k c) -> p k c", k=KD),
                in_=dap(y1_scr, 0,
                        [[2048, 128], [128 * 2048, KD], [1, 2048]])), dma_n=1)
            t_mbe = [None] * (9 * NCHUNK)
            t_u1 = [None] * NHS
            t_u3 = [None] * NHS
            t_a = [None] * NHS
            t_b = [None] * NHS
            t_sa = [None] * NHS
            t_h = [None] * NHS
            t_hw = [None] * NHS

            def emit_wload(s):
                p = s % 2
                deps = [t_u3[(s - 1) * NCHUNK * KH - 1]] if s >= 2 else []
                if s >= 2:
                    assert deps[0] is not None
                off = 0 if s == 0 else (s - 1) * D * HID

                def f(e, s=s, off=off, p=p):
                    return [
                        e.dma_start(
                            out=w1b[p][:].rearrange("p (k c) -> p k c", k=KD),
                            in_=dap(sw1T if s == 0 else w1T, off,
                                    [[HID, 128], [128 * HID, KD], [1, HID]])),
                        e.dma_start(
                            out=w3b[p][:].rearrange("p (k c) -> p k c", k=KD),
                            in_=dap(sw3T if s == 0 else w3T, off,
                                    [[HID, 128], [128 * HID, KD], [1, HID]]))]

                t_w[s] = pp.op("gpsimd", f, deps=deps, dma_n=2)

            def emit_mb(s_, cc_):
                sci_ = s_ * NCHUNK + cc_
                mp = sci_ % 2
                mb = pp.op("tensor", lambda e, s_=s_, cc_=cc_, mp=mp: e.matmul(
                    psMB[mp][:], rsel8[0:8, (s_ - 1) * 128: s_ * 128],
                    m_feat[0:8, cc_ * 512:(cc_ + 1) * 512],
                    start=True, stop=True),
                    deps=[prev_tok(t_mbe, sci_, 2)])
                t_mbe[sci_] = pp.op("scalar", lambda e, mp=mp: e.copy(
                    mbc4[mp][:], psMB[mp][:]),
                    deps=[mb, t_b[(sci_ - 1) * KH - 1] if sci_ >= 2 else None])

            emit_wload(0)
            emit_wload(1)
            for s in range(9):
                p = s % 2
                for cc in range(NCHUNK):
                    sci = s * NCHUNK + cc
                    if s > 0 and t_mbe[sci] is None:
                        emit_mb(s, cc)
                    for ht in range(KH):
                        hs = sci * KH + ht
                        hp = hs % 2

                        def u1(t, p=p, cc=cc, ht=ht, hp=hp):
                            last = None
                            for k in range(KD):
                                last = t.matmul(
                                    psU1[hp][:],
                                    w1b[p][:, k * HID + ht * 128:
                                           k * HID + (ht + 1) * 128],
                                    y1res[:, k * 2048 + cc * 512:
                                          k * 2048 + (cc + 1) * 512],
                                    start=(k == 0), stop=(k == KD - 1))
                            return last

                        def u3(t, p=p, cc=cc, ht=ht, hp=hp):
                            last = None
                            for k in range(KD):
                                last = t.matmul(
                                    psU3[hp][:],
                                    w3b[p][:, k * HID + ht * 128:
                                           k * HID + (ht + 1) * 128],
                                    y1res[:, k * 2048 + cc * 512:
                                          k * 2048 + (cc + 1) * 512],
                                    start=(k == 0), stop=(k == KD - 1))
                            return last

                        t_u1[hs] = pp.op("tensor", u1,
                                         deps=[t_w[s], t_yl,
                                               prev_tok(t_a, hs, 2)])
                        t_u3[hs] = pp.op("tensor", u3,
                                         deps=[prev_tok(t_b, hs, 2)])

                        if s == 0:
                            t_sa[hs] = pp.op(
                                "scalar", lambda e, hp=hp: e.activation(
                                    sabuf[hp][:], psU1[hp][:], AF.Silu),
                                deps=[t_u1[hs], prev_tok(t_h, hs, 1)])
                            t_a[hs] = t_sa[hs]
                            t_b[hs] = pp.op(
                                "vector", lambda e, hp=hp: e.tensor_copy(
                                    bbuf[hp][:], psU3[hp][:]),
                                deps=[t_u3[hs], prev_tok(t_h, hs, 2)])
                        else:
                            t_a[hs] = pp.op(
                                "vector", lambda e, hp=hp, mp=sci % 2:
                                e.tensor_mul(
                                    abuf[hp][:], psU1[hp][:], mbc4[mp][:]),
                                deps=[t_u1[hs], t_mbe[sci],
                                      prev_tok(t_sa, hs, 2)])
                            t_b[hs] = pp.op(
                                "vector", lambda e, hp=hp, mp=sci % 2:
                                e.tensor_mul(
                                    bbuf[hp][:], psU3[hp][:], mbc4[mp][:]),
                                deps=[t_u3[hs], t_mbe[sci],
                                      prev_tok(t_h, hs, 2)])
                            t_sa[hs] = pp.op(
                                "scalar", lambda e, hp=hp: e.activation(
                                    sabuf[hp][:], abuf[hp][:], AF.Silu),
                                deps=[t_a[hs], prev_tok(t_h, hs, 1)])
                        t_h[hs] = pp.op(
                            "vector", lambda e, hp=hp: e.tensor_mul(
                                hbuf[hp][:], sabuf[hp][:], bbuf[hp][:]),
                            deps=[t_sa[hs], t_b[hs], prev_tok(t_hw, hs, 2)])
                        t_hw[hs] = pp.op(
                            "scalar", lambda e, s=s, cc=cc, ht=ht, hp=hp:
                            e.dma_start(
                                out=dap(h_scr,
                                        s * HID * 2048 + ht * 128 * 2048 + cc * 512,
                                        [[2048, 128], [1, 512]]),
                                in_=hbuf[hp][:]),
                            deps=[t_h[hs]], dma_n=1)
                    nsci = sci + 1
                    ns_, ncc = divmod(nsci, NCHUNK)
                    if ns_ > 0 and nsci <= 8 * NCHUNK + 3 and nsci < 9 * NCHUNK:
                        emit_mb(ns_, ncc)
                if s + 2 <= 8:
                    emit_wload(s + 2)

            pp.wait_only("scalar", [t_hw[NHS - 1]])
            pp.wait_only("gpsimd", [t_w[8]])
            pp.wait_only("sync", [t_yl])
            pp.emit()

        with contextlib.ExitStack() as stB:
            SB5 = lambda *a: stB.enter_context(nc.sbuf_tensor(*a))
            ffnacc = SB5("ffnacc", [128, KD * 2048], BF16)

            with contextlib.ExitStack() as stp:
                SP = lambda *a: stp.enter_context(nc.sbuf_tensor(*a))
                PPs = lambda *a: stp.enter_context(nc.psum_tensor(*a))
                w2b = [SP(f"w2b_{p}", [128, KH * D], BF16) for p in range(2)]
                hcb = [SP(f"hcb_{p}", [128, KH * 512], BF16) for p in range(2)]
                psW2 = [PPs(f"psW2_{p}", [128, 512], F32) for p in range(2)]

                pp = mkpipe("p5a")
                NDS = 9 * NCHUNK * KD
                t_w2 = [None] * 9
                t_hc = [None] * (9 * NCHUNK)
                t_m2 = [None] * NDS
                t_ac = [None] * NDS

                def emit_w2(s):
                    p = s % 2
                    deps = [t_m2[(s - 1) * NCHUNK * KD - 1]] if s >= 2 else []
                    if s >= 2:
                        assert deps[0] is not None
                    off = 0 if s == 0 else (s - 1) * HID * D
                    t_w2[s] = pp.op("gpsimd", lambda e, s=s, off=off, p=p:
                                    e.dma_start(
                        out=w2b[p][:].rearrange("p (k c) -> p k c", k=KH),
                        in_=dap(sw2T if s == 0 else w2T, off,
                                [[D, 128], [128 * D, KH], [1, D]])),
                        deps=deps, dma_n=1)

                emit_w2(0)
                emit_w2(1)
                for s in range(9):
                    p = s % 2
                    for cc in range(NCHUNK):
                        sci = s * NCHUNK + cc
                        hp = sci % 2
                        t_hc[sci] = pp.op(
                            "sync", lambda e, s=s, cc=cc, hp=hp: e.dma_start(
                                out=hcb[hp][:].rearrange("p (k c) -> p k c", k=KH),
                                in_=dap(h_scr, s * HID * 2048 + cc * 512,
                                        [[2048, 128], [128 * 2048, KH], [1, 512]])),
                            deps=[t_m2[(sci - 1) * KD - 1] if sci >= 2 else None],
                            dma_n=1)
                        for dt in range(KD):
                            ds_ = sci * KD + dt
                            wp = ds_ % 2

                            def w2mm(t, cc=cc, dt=dt, hp=hp, wp=wp, p=p):
                                last = None
                                for k in range(KH):
                                    last = t.matmul(
                                        psW2[wp][:],
                                        w2b[p][:, k * D + dt * 128:
                                               k * D + (dt + 1) * 128],
                                        hcb[hp][:, k * 512:(k + 1) * 512],
                                        start=(k == 0), stop=(k == KH - 1))
                                return last

                            t_m2[ds_] = pp.op(
                                "tensor", w2mm,
                                deps=[t_w2[s], t_hc[sci], prev_tok(t_ac, ds_, 2)])

                            def acc(v, s=s, cc=cc, dt=dt, wp=wp):
                                sl = ffnacc[:, dt * 2048 + cc * 512:
                                            dt * 2048 + (cc + 1) * 512]
                                if s == 0:
                                    return v.tensor_copy(sl, psW2[wp][:])
                                return v.tensor_add(sl, psW2[wp][:], sl)

                            t_ac[ds_] = pp.op("vector", acc, deps=[t_m2[ds_]])
                    if s + 2 <= 8:
                        emit_w2(s + 2)

                pp.wait_only("sync", [t_hc[9 * NCHUNK - 1]])
                pp.wait_only("gpsimd", [t_w2[8]])
                pp.emit()

            with contextlib.ExitStack() as stp:
                SP = lambda *a: stp.enter_context(nc.sbuf_tensor(*a))
                PPs = lambda *a: stp.enter_context(nc.psum_tensor(*a))
                y1f = [SP("y1f_0", [128, KD * 512], BF16)]
                ufull = SP("ufull5", [128, KD * 512], F32)
                ubx5 = [SP(f"ubx5_{p}", [128, 512], BF16) for p in range(2)]
                usx5 = [SP(f"usx5_{p}", [128, 512], BF16) for p in range(2)]
                srow = SP("srow5", [32, 2048], F32)
                mrow = SP("mrow5", [32, 512], BF16)
                rrw = SP("rrw5", [32, 512], BF16)
                mbc = SP("mbc5", [128, 512], BF16)
                rbc5 = SP("rbc5", [128, 512], BF16)
                y2c = SP("y2c", [128, KD * 512], F32)
                psMV = PPs("psMV5", [32, 1024], F32)
                psBC = PPs("psBC5", [128, 512], F32)

                def dap_y2c():
                    return y2c[:].rearrange("p (k c) -> p k c", k=KD)

                pp = mkpipe("p5b")
                if debug:
                    pp.op("gpsimd", lambda e: e.dma_start(
                        out=dap(dbg["ffn"], 0, [[KD * 2048, 128], [1, KD * 2048]]),
                        in_=ffnacc[:]), dma_n=1)
                    for nm, scr, sz in (("q", q_scr, H * 128 * 2048),
                                        ("k", k_scr, H * 128 * S),
                                        ("v", v_scr, H * S * 128),
                                        ("y1", y1_scr, D * 2048),
                                        ("h", h_scr, 9 * HID * 2048)):
                        pp.op("gpsimd", lambda e, nm=nm, scr=scr, sz=sz:
                              e.dma_start(out=dap(dbg[nm], 0, [[1, sz]]),
                                          in_=dap(scr, 0, [[1, sz]])),
                              dma_n=1)
                t_y = [None] * NCHUNK
                t_u = [None] * (NCHUNK * KD)
                t_ub = [None] * (NCHUNK * KD)
                t_stk = [None] * (NCHUNK * KD)
                stk_fns = [None] * (NCHUNK * KD)
                t_st = [None] * NCHUNK
                t_lnv = [None] * NCHUNK
                t_rstd = [None] * NCHUNK
                t_bme = [None] * NCHUNK
                t_brme = [None] * NCHUNK
                t_y2 = [None] * (NCHUNK * KD)
                t_wr = [None] * NCHUNK

                for cc in range(NCHUNK):
                    t_y[cc] = pp.op("sync", lambda e, cc=cc: e.dma_start(
                        out=y1f[0][:].rearrange("p (k c) -> p k c", k=KD),
                        in_=dap(y1_scr, cc * 512,
                                [[2048, 128], [128 * 2048, KD], [1, 512]])),
                        deps=[t_u[cc * KD - 1] if cc >= 1 else None], dma_n=1)
                    for dt in range(KD):
                        idx = cc * KD + dt
                        t_u[idx] = pp.op(
                            "vector", lambda e, cc=cc, dt=dt: e.tensor_add(
                                ufull[:, dt * 512:(dt + 1) * 512],
                                ffnacc[:, dt * 2048 + cc * 512:
                                       dt * 2048 + (cc + 1) * 512],
                                y1f[0][:, dt * 512:(dt + 1) * 512]),
                            deps=[t_y[cc], prev_tok(t_y2, idx, KD)])

                        def stk_mm(t, dt=dt):
                            t.matmul(psMV[0:1, 0:512], onesK[:, 0:1],
                                     ubx5[dt % 2][:],
                                     start=(dt == 0), stop=(dt == KD - 1))
                            return t.matmul(psMV[0:1, 512:1024], onesK[:, 0:1],
                                            usx5[dt % 2][:],
                                            start=(dt == 0), stop=(dt == KD - 1))

                        stk_fns[idx] = stk_mm
                        if dt >= 2:
                            t_stk[idx - 2] = pp.op(
                                "tensor", stk_fns[idx - 2],
                                deps=[t_ub[idx - 2],
                                      t_lnv[cc - 1] if (dt == 2 and cc >= 1)
                                      else None])
                        t_ub[idx] = pp.op(
                            "scalar", lambda e, dt=dt: (
                                e.copy(ubx5[dt % 2][:],
                                       ufull[:, dt * 512:(dt + 1) * 512]),
                                e.activation(usx5[dt % 2][:],
                                             ufull[:, dt * 512:(dt + 1) * 512],
                                             AF.Square))[-1],
                            deps=[t_u[idx],
                                  t_stk[idx - 2] if idx >= 2 else None])
                    for dt in (KD - 2, KD - 1):
                        t_stk[cc * KD + dt] = pp.op(
                            "tensor", stk_fns[cc * KD + dt],
                            deps=[t_ub[cc * KD + dt]])
                    t_st[cc] = t_stk[cc * KD + KD - 1]

                    a1 = pp.op("vector", lambda e: e.tensor_scalar_mul(
                        srow[0:1, 0:512], psMV[0:1, 0:512], 1.0 / D),
                        deps=[t_st[cc]])
                    a2 = pp.op("vector", lambda e: e.tensor_scalar_mul(
                        srow[0:1, 512:1024], psMV[0:1, 512:1024], 1.0 / D),
                        deps=[t_st[cc]])
                    a3 = pp.op("vector", lambda e: e.tensor_mul(
                        srow[0:1, 1024:1536], srow[0:1, 0:512], srow[0:1, 0:512]),
                        deps=[a1], self_edge=True)
                    a4 = pp.op("vector", lambda e: e.tensor_sub(
                        srow[0:1, 1536:2048], srow[0:1, 512:1024],
                        srow[0:1, 1024:1536]), deps=[a2, a3], self_edge=True)
                    a5 = pp.op("vector", lambda e: e.tensor_copy(
                        mrow[0:1, :], srow[0:1, 0:512]), deps=[a1], self_edge=True)
                    t_lnv[cc] = a5
                    r1 = pp.op("scalar", lambda e: e.activation(
                        srow[0:1, 1024:1536], srow[0:1, 1536:2048], AF.Ln,
                        bias=cval[0:1, 1:2]), deps=[a4])
                    t_rstd[cc] = pp.op("scalar", lambda e: e.activation(
                        rrw[0:1, :], srow[0:1, 1024:1536],
                        AF.Exp, scale=cval[0:1, 3:4]),
                        deps=[r1], self_edge=True)
                    bm = pp.op("tensor", lambda e: e.matmul(
                        psBC[:], onesrow[0:1, :], mrow[0:1, :],
                        start=True, stop=True),
                        deps=[a5, t_brme[cc - 1] if cc >= 1 else None])
                    t_bme[cc] = pp.op("scalar", lambda e: e.copy(mbc[:], psBC[:]),
                                      deps=[bm])
                    brm = pp.op("tensor", lambda e: e.matmul(
                        psBC[:], onesrow[0:1, :], rrw[0:1, :],
                        start=True, stop=True),
                        deps=[t_bme[cc], t_rstd[cc]])
                    t_brme[cc] = pp.op("scalar", lambda e: e.copy(rbc5[:], psBC[:]),
                                       deps=[brm])

                    for dt in range(KD):
                        idx = cc * KD + dt
                        t_y2[idx] = pp.op(
                            "vector", lambda e, dt=dt: (
                                e.tensor_sub(ufull[:, dt * 512:(dt + 1) * 512],
                                             ufull[:, dt * 512:(dt + 1) * 512],
                                             mbc[:]),
                                e.tensor_mul(y2c[:, dt * 512:(dt + 1) * 512],
                                             ufull[:, dt * 512:(dt + 1) * 512],
                                             rbc5[:]))[-1],
                            deps=[t_bme[cc], t_brme[cc], t_ub[idx],
                                  t_wr[cc - 1] if cc >= 1 else None])
                    t_wr[cc] = pp.op("gpsimd", lambda e, cc=cc: e.dma_start(
                        out=dap(y2_out, cc * 512,
                                [[2048, 128], [128 * 2048, KD], [1, 512]]),
                        in_=dap_y2c(),
                    ), deps=[t_y2[cc * KD + k] for k in range(KD)], dma_n=1)

                pp.wait_only("gpsimd", [t_wr[NCHUNK - 1]])
                pp.wait_only("sync", [t_y[NCHUNK - 1]])
                pp.emit()

    return nc


_NC_CACHE = None


def kernel(L0_attn_x, L0_attn_complex_freq, L0_attn_W_Aq, L0_attn_W_Ak,
           L0_attn_W_Av, L0_attn_W_Bq, L0_attn_W_Bk, L0_attn_W_Bv, L0_attn_Wo,
           L0_ffn_gate, L0_ffn_expert_bias, L0_ffn_w1, L0_ffn_w2, L0_ffn_w3,
           L0_ffn_shared_w1, L0_ffn_shared_w2, L0_ffn_shared_w3):
    global _NC_CACHE
    x = np.asarray(L0_attn_x, dtype=np.float32)
    wproj = np.ascontiguousarray(np.concatenate(
        [np.asarray(L0_attn_W_Bq), np.asarray(L0_attn_W_Bk),
         np.asarray(L0_attn_W_Bv), np.asarray(L0_attn_W_Aq),
         np.asarray(L0_attn_W_Ak), np.asarray(L0_attn_W_Av)],
        axis=1).astype(np.float32)).reshape(-1)
    wo = np.ascontiguousarray(np.asarray(L0_attn_Wo, dtype=np.float32)).reshape(-1)
    fr = np.ascontiguousarray(np.asarray(
        L0_attn_complex_freq, dtype=np.float32)).reshape(-1)
    gT = np.ascontiguousarray(
        np.asarray(L0_ffn_gate, dtype=np.float32).T).reshape(-1)
    eb = np.ascontiguousarray(np.asarray(L0_ffn_expert_bias, dtype=np.float32))
    s1 = np.ascontiguousarray(
        np.asarray(L0_ffn_shared_w1, dtype=np.float32).T).reshape(-1)
    s3 = np.ascontiguousarray(
        np.asarray(L0_ffn_shared_w3, dtype=np.float32).T).reshape(-1)
    s2 = np.ascontiguousarray(
        np.asarray(L0_ffn_shared_w2, dtype=np.float32).T).reshape(-1)
    w1 = np.ascontiguousarray(
        np.asarray(L0_ffn_w1, dtype=np.float32).transpose(0, 2, 1)).reshape(-1)
    w3 = np.ascontiguousarray(
        np.asarray(L0_ffn_w3, dtype=np.float32).transpose(0, 2, 1)).reshape(-1)
    w2 = np.ascontiguousarray(
        np.asarray(L0_ffn_w2, dtype=np.float32).transpose(0, 2, 1)).reshape(-1)
    rsel = np.zeros((8, 8 * 128), dtype=np.float32)
    for e in range(8):
        rsel[e, e * 128:(e + 1) * 128] = 1.0
    rsel = rsel.reshape(-1)

    if _NC_CACHE is None:
        _NC_CACHE = build_program()
    nc = _NC_CACHE

    in_maps = []
    for c in range(8):
        b, h2 = c // 2, c % 2
        own = x[b, h2 * 2048:(h2 + 1) * 2048]
        other = x[b, (1 - h2) * 2048:(2 - h2) * 2048]
        xTc = np.ascontiguousarray(
            np.concatenate([own, other], axis=0).T).reshape(-1)
        in_maps.append({
            "xT": xTc, "wproj": wproj, "wo_in": wo, "freq": fr, "gateT": gT,
            "ebias": eb, "sw1T": s1, "sw3T": s3, "sw2T": s2,
            "w1T": w1, "w3T": w3, "w2T": w2, "rsel8_in": rsel,
        })

    res = run_bass_kernel_spmd(nc, in_maps, core_ids=list(range(8)))
    out = np.empty((B, S, D), dtype=np.float32)
    for c in range(8):
        b, h2 = c // 2, c % 2
        out[b, h2 * 2048:(h2 + 1) * 2048, :] = res.results[c]["y2"].T
    return out
